# version 14
# speedup vs baseline: 1.0202x; 1.0015x over previous
import numpy as np
import ml_dtypes
from contextlib import ExitStack, nullcontext
from concourse.aot_env import allow_ici_all_to_all
import concourse.bass as bass
import concourse.mybir as mybir
from concourse.bass_utils import run_bass_kernel_spmd

F32 = mybir.dt.float32
BF16 = mybir.dt.bfloat16
AF = mybir.ActivationFunctionType
ALU = mybir.AluOpType
BF = ml_dtypes.bfloat16

NC = 8
T = 8192
D = 2048
TC = T // NC
KT = D // 128
LNEPS = 1e-5


def _bf(x):
    return np.ascontiguousarray(np.asarray(x), dtype=BF)


def _f32(x):
    return np.ascontiguousarray(np.asarray(x), dtype=np.float32)


class Sem:

    def __init__(self, sem):
        self.sem = sem
        self.n = 0

    def inc(self, inst, k=1):
        self.n += k
        inst.then_inc(self.sem, k)
        return self.n


class NS:
    pass


def _phase_close(nc, sems):
    with nc.Block() as blk:
        @blk.gpsimd
        def _(gp):
            for s in sems:
                if s.n > 0:
                    gp.wait_ge(s.sem, s.n)
            for s in sems:
                gp.sem_clear(s.sem)
    for s in sems:
        nc.release_semaphore(s.sem)


def _mk_factory(nc, tag, sems):
    def mk(nm, dma=False):
        s = Sem(nc.alloc_semaphore(name=nm + tag))
        if not dma:
            sems.append(s)
        return s
    return mk


CONSTS = {
    "onesb": _bf(np.ones((128, 128))),
    "onesf": _f32(np.ones((128, 128))),
    "ident": _bf(np.eye(128)),
}


def _decl_consts(nc):
    c = NS()
    c.onesb = nc.dram_tensor("onesb", [128, 128], BF16, kind="ExternalInput").ap()
    c.onesf = nc.dram_tensor("onesf", [128, 128], F32, kind="ExternalInput").ap()
    c.ident = nc.dram_tensor("ident", [128, 128], BF16, kind="ExternalInput").ap()
    return c


def emit_ln(block, g, xs, yb, sqbuf):
    nt = TC

    scal_base = g.scal.n
    vec_base = g.vec.n

    @block.scalar
    def _(s):
        if g.xld is not None:
            s.wait_ge(g.xld.sem, 16 * KT)
        for i in range(KT):
            g.scal.inc(s.activation(yb[:, i * nt:(i + 1) * nt],
                                    xs[:, i * nt:(i + 1) * nt], AF.Copy))
    g.ln_cast = g.scal.n

    @block.vector
    def _(v):
        for i in range(KT):
            v.wait_ge(g.scal.sem, scal_base + i + 1)
            g.vec.inc(v.tensor_tensor(sqbuf[:, i * nt:(i + 1) * nt],
                                      yb[:, i * nt:(i + 1) * nt],
                                      yb[:, i * nt:(i + 1) * nt], ALU.mult))
    g.ln_sq = g.vec.n

    @block.tensor
    def _(t):
        t.wait_ge(g.cld.sem, g.cld_total)
        last = None
        for i in range(KT):
            t.wait_ge(g.scal.sem, scal_base + i + 1)
            for ch in range(2):
                t.matmul(g.psA[0:1, ch * 512:(ch + 1) * 512], g.onesb_s[:, 0:1],
                         yb[:, i * nt + ch * 512: i * nt + (ch + 1) * 512],
                         start=(i == 0), stop=(i == KT - 1))
            t.wait_ge(g.vec.sem, vec_base + i + 1)
            for ch in range(2):
                last = t.matmul(g.psB[0:1, ch * 512:(ch + 1) * 512], g.onesb_s[:, 0:1],
                                sqbuf[:, i * nt + ch * 512: i * nt + (ch + 1) * 512],
                                start=(i == 0), stop=(i == KT - 1))
        g.ln_stat_mm = g.tens.inc(last)

    @block.vector
    def _(v):
        v.wait_ge(g.tens.sem, g.ln_stat_mm)
        m = g.rows[0:1, 0:nt]
        ex2 = g.rows[0:1, nt:2 * nt]
        var = g.rows[0:1, 2 * nt:3 * nt]
        v.tensor_scalar(m, g.psA[0:1, 0:nt], 1.0 / D, None, ALU.mult)
        v.tensor_scalar(ex2, g.psB[0:1, 0:nt], 1.0 / D, None, ALU.mult)
        v.drain()
        v.tensor_tensor(var, m, m, ALU.mult)
        v.drain()
        v.tensor_tensor(var, ex2, var, ALU.subtract)
        v.drain()
        g.ln_rows = g.vec.inc(
            v.tensor_scalar(var, var, float(LNEPS), None, ALU.add))

    @block.scalar
    def _(s):
        s.wait_ge(g.vec.sem, g.ln_rows)
        var = g.rows[0:1, 2 * nt:3 * nt]
        lnv = g.rows[0:1, 3 * nt:4 * nt]
        s.activation(lnv, var, AF.Ln)
        s.drain()
        g.ln_rstd = g.scal.inc(
            s.activation(var, lnv, AF.Exp, scale=-0.5))

    @block.tensor
    def _(t):
        t.wait_ge(g.scal.sem, g.ln_rstd)
        last = None
        for ch in range(2):
            t.matmul(g.psC[:, ch * 512:(ch + 1) * 512], g.onesf_s[0:1, :],
                     g.rows[0:1, ch * 512:(ch + 1) * 512], start=True, stop=True)
            last = t.matmul(g.psA[:, ch * 512:(ch + 1) * 512], g.onesf_s[0:1, :],
                            g.rows[0:1, 2 * nt + ch * 512:2 * nt + (ch + 1) * 512],
                            start=True, stop=True)
        g.ln_bcast = g.tens.inc(last)

    @block.vector
    def _(v):
        v.wait_ge(g.tens.sem, g.ln_bcast)
        last = None
        for i in range(KT):
            v.tensor_tensor(g.tmp[:, 0:nt], yb[:, i * nt:(i + 1) * nt],
                            g.psC[:, 0:nt], ALU.subtract)
            v.drain()
            last = v.tensor_tensor(yb[:, i * nt:(i + 1) * nt], g.tmp[:, 0:nt],
                                   g.psA[:, 0:nt], ALU.mult)
            v.drain()
        g.ln_done = g.vec.inc(last)


def emit_moe(nc, P):
    xT, gTt = P.xT, P.gTt
    w1S, w3S, w2S = P.w1S, P.w3S, P.w2S
    ws1S, ws3S, ws2S = P.ws1S, P.ws3S, P.ws2S
    cst, crow_d, xoT = P.cst, P.crow_d, P.xoT
    xs = P.xs

    NEXP = 9
    with ExitStack() as ex:
        sbt = lambda nm, sh, dt: ex.enter_context(nc.sbuf_tensor(nm + P.tag, sh, dt)).ap()
        pst = lambda nm, sh, dt: ex.enter_context(nc.psum_tensor(nm + P.tag, sh, dt)).ap()
        g = NS()
        yb = sbt("yb", [128, KT * TC], BF16)
        scr4 = sbt("scr4", [128, KT * TC], BF16)
        gsl = sbt("gsl", [128, 8 * TC], BF16)
        bcast = sbt("bcast", [128, 2 * TC], BF16)
        wbuf = sbt("wbuf", [128, 4 * D], BF16)
        gbuf = sbt("gbuf", [128, 128], BF16)
        g.tmp = sbt("tmp", [128, TC], F32)
        g.rows = sbt("rows", [1, 4 * TC], F32)
        lg = sbt("lg", [128, 64], F32)
        sm1 = sbt("sm1", [128, 24], F32)
        pexp = sbt("pexp", [128, 64], F32)
        msk = sbt("msk", [128, 64], F32)
        comb_bf = sbt("comb_bf", [128, 64], BF16)
        crow_sb = sbt("crow_sb", [8, TC], BF16)
        crow_fl = sbt("crow_fl", [1, 2 * TC], BF16)
        g.onesb_s = sbt("onesb_s", [128, 128], BF16)
        g.onesf_s = sbt("onesf_s", [128, 128], F32)
        ident_s = sbt("ident_s", [128, 128], BF16)
        tmpbf = sbt("tmpbf", [128, 2 * 512], BF16)
        g0buf = sbt("g0buf", [128, 2 * 512], BF16)

        g.psA = pst("psA", [128, 1024], F32)
        g.psB = pst("psB", [128, 1024], F32)
        g.psC = pst("psC", [128, 1024], F32)
        psD = pst("psD", [128, 512], F32)
        psE = pst("psE", [128, 1024], BF16)

        W2BANK = [lambda a=a, c=c: [g.psA, g.psB, g.psC][a][:, c * 512:(c + 1) * 512]
                  for a in range(3) for c in range(2)]

        sems = []
        with nc.Block() as block:
            mk = _mk_factory(nc, P.tag, sems)
            if getattr(P, "xld_ext", None) is not None:
                g.xld = P.xld_ext
            else:
                g.xld = mk("xld") if P.load_x else None
            cld = mk("cld")
            g.cld = cld
            gld = mk("gld", dma=True)
            wldS = [mk("wld%d" % q, dma=True) for q in range(4)]
            w2ld = mk("w2ld", dma=True)
            g.scal = mk("scal")
            g.vec = mk("vec")
            g.tens = mk("tens")
            hmm = mk("hmm")
            w2mm = mk("w2mm")
            sil = mk("sil")
            hev = mk("hev")
            w2ev = mk("w2ev")
            bce = mk("bce")
            bcm = mk("bcm")
            bcr = mk("bcr")
            bcrS = [mk("bcr%d" % q) for q in range(2)]
            stf = mk("stf")

            @block.sync
            def _(sy):
                if P.load_x:
                    for i in range(KT):
                        g.xld.inc(sy.dma_start(out=xs[:, i * TC:(i + 1) * TC],
                                               in_=xT[i * 128:(i + 1) * 128, :]), 16)
                cld.inc(sy.dma_start(out=g.onesb_s, in_=cst.onesb), 16)
                cld.inc(sy.dma_start(out=g.onesf_s, in_=cst.onesf), 16)
                cld.inc(sy.dma_start(out=ident_s, in_=cst.ident), 16)
            g.cld_total = 48

            emit_ln(block, g, xs, yb, scr4)

            @block.gpsimd
            def _(gp):
                gld.inc(gp.dma_start(out=gbuf, in_=gTt), 16)
                first_w2 = True
                EO = [8] + list(range(8))
                for pos, e in enumerate(EO):
                    for i in range(8):
                        p = pos * 8 + i
                        if p >= 2:
                            gp.wait_ge(hmm.sem, 2 * p - 2)
                        for which in range(2):
                            q = 2 * p + which
                            if e < 8:
                                src = (w1S if which == 0 else w3S)
                                off = (e * 8 + i) * 128 * D
                            else:
                                src = (ws1S if which == 0 else ws3S)
                                off = i * 128 * D
                            ap = bass.AP(src.tensor, off, [[D, 128], [1, D]])
                            wldS[q % 4].inc(gp.dma_start(
                                out=wbuf[:, (q % 4) * D:(q % 4 + 1) * D], in_=ap), 16)
                        if i == 3:
                            if first_w2:
                                gp.wait_ge(g.tens.sem, g.ln_stat_mm)
                                first_w2 = False
                            if pos >= 1:
                                gp.wait_ge(w2mm.sem, 32 * pos)
                            for kt in range(8):
                                if e < 8:
                                    src, off = w2S, (e * 8 + kt) * 128 * D
                                else:
                                    src, off = ws2S, kt * 128 * D
                                ap = bass.AP(src.tensor, off, [[D, 128], [1, D]])
                                w2ld.inc(gp.dma_start(
                                    out=scr4[:, kt * D:(kt + 1) * D],
                                    in_=ap), 16)

            @block.tensor
            def _(t):
                t.wait_ge(g.vec.sem, g.ln_done)
                t.wait_ge(gld.sem, 16)
                last = None
                for tt in range(8):
                    for kt in range(KT):
                        last = t.matmul(g.psA[:, tt * 8:tt * 8 + 8],
                                        yb[:, kt * TC + tt * 128: kt * TC + tt * 128 + 128],
                                        gbuf[:, kt * 8:(kt + 1) * 8],
                                        start=(kt == 0), stop=(kt == KT - 1))
                g.rt_mm = g.tens.inc(last)

            @block.vector
            def _(v):
                v.wait_ge(g.tens.sem, g.rt_mm)
                v.tensor_copy(lg, g.psA[:, 0:64])
                v.drain()
                for tt in range(8):
                    v.tensor_reduce(sm1[:, tt:tt + 1], lg[:, tt * 8:(tt + 1) * 8],
                                    axis=mybir.AxisListType.X, op=ALU.max)
                v.drain()
                last = None
                for tt in range(8):
                    last = v.tensor_scalar(lg[:, tt * 8:(tt + 1) * 8],
                                           lg[:, tt * 8:(tt + 1) * 8],
                                           sm1[:, tt:tt + 1], None, ALU.subtract)
                g.rt_lp = g.vec.inc(last)

            @block.scalar
            def _(s):
                s.wait_ge(g.vec.sem, g.rt_lp)
                g.rt_exp = g.scal.inc(s.activation(pexp, lg, AF.Exp))

            @block.vector
            def _(v):
                v.wait_ge(g.scal.sem, g.rt_exp)
                for tt in range(8):
                    v.tensor_reduce(sm1[:, 8 + tt:9 + tt], pexp[:, tt * 8:(tt + 1) * 8],
                                    axis=mybir.AxisListType.X, op=ALU.add)
                v.drain()
                v.reciprocal(sm1[:, 8:16], sm1[:, 8:16])
                v.tensor_scalar(msk, lg, 0.0, None, ALU.is_ge)
                v.drain()
                v.scalar_tensor_tensor(msk, msk, -1e30, lg, ALU.mult, ALU.add)
                v.drain()
                for tt in range(8):
                    v.tensor_reduce(sm1[:, 16 + tt:17 + tt], msk[:, tt * 8:(tt + 1) * 8],
                                    axis=mybir.AxisListType.X, op=ALU.max)
                v.drain()
                for tt in range(8):
                    v.tensor_scalar(msk[:, tt * 8:(tt + 1) * 8], lg[:, tt * 8:(tt + 1) * 8],
                                    sm1[:, 16 + tt:17 + tt], None, ALU.is_ge)
                    v.tensor_scalar(pexp[:, tt * 8:(tt + 1) * 8], pexp[:, tt * 8:(tt + 1) * 8],
                                    sm1[:, 8 + tt:9 + tt], 0.5, ALU.mult, ALU.mult)
                v.drain()
                g.rt_comb = g.vec.inc(v.tensor_tensor(comb_bf, pexp, msk, ALU.mult))

            @block.tensor
            def _(t):
                t.wait_ge(g.vec.sem, g.rt_comb)
                t.wait_ge(cld.sem, 48)
                last = None
                for tt in range(8):
                    last = t.transpose(psE[0:8, tt * 128:(tt + 1) * 128],
                                       comb_bf[:, tt * 8:(tt + 1) * 8], ident_s)
                g.rt_tr = g.tens.inc(last)

            @block.vector
            def _(v):
                v.wait_ge(g.tens.sem, g.rt_tr)
                g.rt_crow = g.vec.inc(v.tensor_copy(crow_sb, psE[0:8, :]))

            @block.sync
            def _(sy):
                sy.wait_ge(g.vec.sem, g.rt_crow)
                bcr.inc(sy.dma_start(out=crow_d, in_=crow_sb), 16)
                sy.wait_ge(bcr.sem, 16)
                for e in range(8):
                    if e >= 2:
                        sy.wait_ge(bcm.sem, 2 * (e - 1))
                    bcrS[e % 2].inc(sy.dma_start(
                        out=crow_fl[0:1, (e % 2) * TC:(e % 2 + 1) * TC],
                        in_=crow_d[e:e + 1, :]), 16)

            @block.tensor
            def _(t):
                t.wait_ge(g.vec.sem, g.ln_done)
                hcc = 0
                w2c = 0
                EO = [8] + list(range(8))
                for pos, e in enumerate(EO):
                    if e < 8:
                        t.wait_ge(bcrS[e % 2].sem, 16 * (e // 2 + 1))
                        for ch in range(2):
                            idx = 2 * e + ch
                            if idx >= 1:
                                t.wait_ge(bce.sem, idx)
                            last = t.matmul(
                                psD[:, 0:512], g.onesb_s[0:1, :],
                                crow_fl[0:1, (e % 2) * TC + ch * 512:(e % 2) * TC + (ch + 1) * 512],
                                start=True, stop=True)
                            bcm.inc(last)
                    if pos >= 1:
                        t.wait_ge(w2ev.sem, 32 * pos)
                    for i in range(8):
                        p = pos * 8 + i
                        q0, q1 = 2 * p, 2 * p + 1
                        t.wait_ge(wldS[q0 % 4].sem, 16 * (q0 // 4 + 1))
                        t.wait_ge(wldS[q1 % 4].sem, 16 * (q1 // 4 + 1))
                        for ch in range(2):
                            slot = [g.psA, g.psB, g.psC][hcc % 3]
                            if hcc >= 3:
                                t.wait_ge(hev.sem, hcc - 2)
                            for kt in range(KT):
                                t.matmul(slot[:, 0:512],
                                         wbuf[:, (2 * p % 4) * D + kt * 128:(2 * p % 4) * D + (kt + 1) * 128],
                                         yb[:, kt * TC + ch * 512:kt * TC + (ch + 1) * 512],
                                         start=(kt == 0), stop=(kt == KT - 1))
                            last = None
                            for kt in range(KT):
                                last = t.matmul(
                                    slot[:, 512:1024],
                                    wbuf[:, ((2 * p + 1) % 4) * D + kt * 128:((2 * p + 1) % 4) * D + (kt + 1) * 128],
                                    yb[:, kt * TC + ch * 512:kt * TC + (ch + 1) * 512],
                                    start=(kt == 0), stop=(kt == KT - 1))
                            hmm.inc(last)
                            hcc += 1
                    t.wait_ge(w2ld.sem, 16 * 8 * (pos + 1))
                    t.wait_ge(hev.sem, 16 * (pos + 1))
                    for d in range(KT):
                        for ch in range(2):
                            pt = W2BANK[w2c % 6]()
                            if w2c >= 6:
                                t.wait_ge(w2ev.sem, w2c - 5)
                            last = None
                            for kt in range(8):
                                last = t.matmul(
                                    pt,
                                    scr4[:, kt * D + d * 128:kt * D + (d + 1) * 128],
                                    gsl[:, kt * TC + ch * 512:kt * TC + (ch + 1) * 512],
                                    start=(kt == 0), stop=(kt == 7))
                            w2mm.inc(last)
                            w2c += 1

            @block.scalar
            def _(s):
                hcc = 0
                EO = [8] + list(range(8))
                for pos, e in enumerate(EO):
                    if e < 8:
                        for ch in range(2):
                            s.wait_ge(bcm.sem, 2 * e + ch + 1)
                            if e >= 2:
                                s.wait_ge(hev.sem, 16 * e)
                            bce.inc(s.activation(
                                bcast[:, (e % 2) * TC + ch * 512:(e % 2) * TC + (ch + 1) * 512],
                                psD[:, 0:512], AF.Copy))
                    for i in range(8):
                        for ch in range(2):
                            s.wait_ge(hmm.sem, hcc + 1)
                            if hcc >= 2:
                                s.wait_ge(hev.sem, hcc - 1)
                            slot = [g.psA, g.psB, g.psC][hcc % 3]
                            sil.inc(s.activation(
                                tmpbf[:, (hcc % 2) * 512:(hcc % 2 + 1) * 512],
                                slot[:, 0:512], AF.Silu))
                            hcc += 1

            @block.vector
            def _(v):
                hcc = 0
                w2c = 0
                EO = [8] + list(range(8))
                for pos, e in enumerate(EO):
                    for i in range(8):
                        for ch in range(2):
                            v.wait_ge(sil.sem, hcc + 1)
                            slot = [g.psA, g.psB, g.psC][hcc % 3]
                            dst = gsl[:, i * TC + ch * 512:i * TC + (ch + 1) * 512]
                            if e < 8:
                                if i == 0 and ch == 0:
                                    v.wait_ge(bce.sem, 2 * (e + 1))
                                v.tensor_tensor(
                                    g0buf[:, (hcc % 2) * 512:(hcc % 2 + 1) * 512],
                                    tmpbf[:, (hcc % 2) * 512:(hcc % 2 + 1) * 512],
                                    slot[:, 512:1024], ALU.mult)
                                v.drain()
                                hev.inc(v.tensor_tensor(
                                    dst, g0buf[:, (hcc % 2) * 512:(hcc % 2 + 1) * 512],
                                    bcast[:, (e % 2) * TC + ch * 512:(e % 2) * TC + (ch + 1) * 512],
                                    ALU.mult))
                            else:
                                hev.inc(v.tensor_tensor(
                                    dst, tmpbf[:, (hcc % 2) * 512:(hcc % 2 + 1) * 512],
                                    slot[:, 512:1024], ALU.mult))
                            hcc += 1
                    for d in range(KT):
                        for ch in range(2):
                            v.wait_ge(w2mm.sem, w2c + 1)
                            pt = W2BANK[w2c % 6]()
                            dst = xs[:, d * TC + ch * 512:d * TC + (ch + 1) * 512]
                            if e < 8:
                                w2ev.inc(v.tensor_tensor(dst, dst, pt, ALU.add))
                            else:
                                w2ev.inc(v.scalar_tensor_tensor(dst, pt, 0.5, dst,
                                                                ALU.mult, ALU.add))
                            w2c += 1

            if P.store_x:
                @block.sync
                def _(sy):
                    sy.wait_ge(w2ev.sem, NEXP * 32)
                    for d in range(KT):
                        stf.inc(sy.dma_start(out=xoT[d * 128:(d + 1) * 128, :],
                                             in_=xs[:, d * TC:(d + 1) * TC]), 16)
                    sy.wait_ge(stf.sem, 16 * KT)

        _phase_close(nc, sems)


def _moe_drams(nc, P):
    P.gTt = nc.dram_tensor("gTt", [128, 128], F32, kind="ExternalInput").ap()
    P.w1S = nc.dram_tensor("w1S", [8 * 8 * 128, D], F32, kind="ExternalInput").ap()
    P.w3S = nc.dram_tensor("w3S", [8 * 8 * 128, D], F32, kind="ExternalInput").ap()
    P.w2S = nc.dram_tensor("w2S", [8 * 8 * 128, D], F32, kind="ExternalInput").ap()
    P.ws1S = nc.dram_tensor("ws1S", [8 * 128, D], F32, kind="ExternalInput").ap()
    P.ws3S = nc.dram_tensor("ws3S", [8 * 128, D], F32, kind="ExternalInput").ap()
    P.ws2S = nc.dram_tensor("ws2S", [8 * 128, D], F32, kind="ExternalInput").ap()
    P.crow_d = nc.dram_tensor("crow_d" + P.tag, [8, TC], BF16).ap()


def build_moe():
    nc = bass.Bass("TRN2", target_bir_lowering=False)
    P = NS()
    P.tag = ""
    P.load_x = True
    P.store_x = True
    P.xT = nc.dram_tensor("xT", [D, TC], F32, kind="ExternalInput").ap()
    _moe_drams(nc, P)
    P.cst = _decl_consts(nc)
    P.xoT = nc.dram_tensor("xoT", [D, TC], F32, kind="ExternalOutput").ap()
    with ExitStack() as gx:
        P.xs = gx.enter_context(nc.sbuf_tensor("xs_g", [128, KT * TC], F32)).ap()
        emit_moe(nc, P)
    return nc


def moe_prep(inp):
    w1 = _f32(inp["L0_ffn_w1"])
    w3 = _f32(inp["L0_ffn_w3"])
    w2 = _f32(inp["L0_ffn_w2"])
    gate = _f32(inp["L0_ffn_gate_w"])
    ws1 = _f32(inp["L0_ffn_ws1"])
    ws2 = _f32(inp["L0_ffn_ws2"])
    ws3 = _f32(inp["L0_ffn_ws3"])

    def slabify_in_out(wT):
        a = wT.reshape(KT, 128, 1024)
        out = np.empty((8, 128, KT * 128), dtype=np.float32)
        for i in range(8):
            out[i] = a[:, :, i * 128:(i + 1) * 128].transpose(1, 0, 2).reshape(128, KT * 128)
        return out

    w1S = np.stack([slabify_in_out(w1[e].T) for e in range(8)])
    w3S = np.stack([slabify_in_out(w3[e].T) for e in range(8)])
    w2S = np.stack([w2[e].T.reshape(8, 128, D) for e in range(8)])
    ws1Sl = slabify_in_out(ws1.T)
    ws3Sl = slabify_in_out(ws3.T)
    ws2Sl = ws2.T.reshape(8, 128, D)
    gTt = gate.T.reshape(KT, 128, 8).transpose(1, 0, 2).reshape(128, 128)
    return {
        "gTt": _f32(gTt),
        "w1S": _f32(w1S.reshape(8 * 8 * 128, D)),
        "w3S": _f32(w3S.reshape(8 * 8 * 128, D)),
        "w2S": _f32(w2S.reshape(8 * 8 * 128, D)),
        "ws1S": _f32(ws1Sl.reshape(8 * 128, D)),
        "ws3S": _f32(ws3Sl.reshape(8 * 128, D)),
        "ws2S": _f32(ws2Sl.reshape(8 * 128, D)),
    }


def run_moe(xT_full, pre):
    nc = build_moe()
    in_maps = []
    for c in range(NC):
        m = dict(pre)
        m["xT"] = _f32(xT_full[:, c * TC:(c + 1) * TC])
        m.update(CONSTS)
        in_maps.append(m)
    res = run_bass_kernel_spmd(nc, in_maps, core_ids=list(range(NC)))
    return np.concatenate([res.results[c]["xoT"] for c in range(NC)], axis=1)



def emit_attn0(nc, P):
    wqS, wkvS, bq_c, bkv_c, mask01 = P.wqS, P.wkvS, P.bq_c, P.bkv_c, P.mask01
    cst = P.cst
    y_src = P.y_src

    NCH = 16
    ORD = getattr(P, "ch_order", None) or list(range(NCH))
    y_gate = getattr(P, "y_gate", None)
    with ExitStack() as ex:
        sbt = lambda nm, sh, dt: ex.enter_context(nc.sbuf_tensor(nm + P.tag, sh, dt)).ap()
        pst = lambda nm, sh, dt: ex.enter_context(nc.psum_tensor(nm + P.tag, sh, dt)).ap()
        qT = [sbt("qT%d" % h, [128, T], BF16) for h in range(2)]
        kT = sbt("kT", [128, T], BF16)
        vb = sbt("vb", [128, 64 * 130], BF16)
        aoT = [sbt("aoTs%d" % h, [128, T], BF16) for h in range(2)]
        prb = sbt("prb", [128, 2 * 1024], BF16)
        ybuf = sbt("ybuf", [128, 4 * 512], BF16)
        wq_s = sbt("wq_s", [128, KT * 256], BF16)
        wkv_s = sbt("wkv_s", [128, KT * 256], BF16)
        vt_tmp = sbt("vt_tmp", [128, 2 * 512], BF16)
        sqb = sbt("sqb", [128, 2 * 512], BF16)
        bqs = sbt("bqs", [128, 4], F32)
        mrows = sbt("mrows", [1, 64], F32)
        mbias = sbt("mbias", [128, 32], F32)
        rr = sbt("rr", [128, 512], F32)
        rcb = sbt("rcb", [128, 512], BF16)
        mask_s = sbt("mask_s", [128, 896], BF16)
        ones_b = sbt("ones_b", [128, 128], BF16)
        ones_f = sbt("ones_f", [128, 128], F32)
        ident_s = sbt("ident_s", [128, 128], BF16)

        psS1 = pst("psS1", [128, 1024], F32)
        psS2 = pst("psS2", [128, 1024], F32)
        psO = pst("psO", [128, 1024], F32)
        psM = pst("psM", [128, 512], F32)
        psV = pst("psV", [128, 128], BF16)

        NG = [2 * (j + 1) for j in range(NCH)]

        sems = []
        with nc.Block() as block:
            mk = _mk_factory(nc, P.tag, sems)
            cld = mk("cld")
            wl = mk("wl", dma=True)
            yld = [mk("yld%d" % i) for i in range(4)]
            bp = mk("bp")
            qk = mk("qk")
            qev = mk("qev")
            vtr = mk("vtr")
            vbe = mk("vbe")
            sqs = mk("sqs")
            qnm = mk("qnm")
            qnr = mk("qnr")
            ngm = mk("ngm")
            nsc = mk("nsc")
            mbm = mk("mbm")
            mbe = mk("mbe")
            smm = mk("smm")
            xp = mk("xp")
            mkv = mk("mkv")
            omm = mk("omm")
            rcp = mk("rcp")
            rbm = mk("rbm")
            aoev = mk("aoev")
            stf = mk("stf")

            @block.sync
            def _(sy):
                cld.inc(sy.dma_start(out=ones_b, in_=cst.onesb), 16)
                cld.inc(sy.dma_start(out=ones_f, in_=cst.onesf), 16)
                cld.inc(sy.dma_start(out=ident_s, in_=cst.ident), 16)
                cld.inc(sy.dma_start(out=mask_s, in_=mask01), 16)
                cld.inc(sy.dma_start(out=bqs[:, 0:2], in_=bq_c), 16)
                cld.inc(sy.dma_start(out=bqs[:, 2:4], in_=bkv_c), 16)

            @block.gpsimd
            def _(gp):
                wl.inc(gp.dma_start(out=wq_s, in_=wqS), 16)
                wl.inc(gp.dma_start(out=wkv_s, in_=wkvS), 16)

            @block.sync
            def _(sy):
                for idx in range(NCH * KT):
                    si = idx // 16
                    if idx % 16 == 0 and y_gate is not None and si in y_gate:
                        sy.wait_ge(y_gate[si][0], y_gate[si][1])
                    if idx >= 4:
                        sy.wait_ge(qk.sem, idx - 3)
                    yld[idx % 4].inc(sy.dma_start(
                        out=ybuf[:, (idx % 4) * 512:(idx % 4 + 1) * 512],
                        in_=y_src(idx % 16, ORD[si])), 16)

            @block.vector
            def _(v):
                v.wait_ge(cld.sem, 96)
                v.tensor_scalar(bqs[:, 0:2], bqs[:, 0:2],
                                float(1.0 / np.sqrt(128.0)), None, ALU.mult)
                v.memset(bass.AP(vb.tensor, 64, [[64 * 130, 128], [130, 64], [1, 1]]), 1.0)
                bp.inc(v.memset(bass.AP(vb.tensor, 129,
                                        [[64 * 130, 128], [130, 64], [1, 1]]), 1.0))

            def emit_transposes(t, si, ch):
                t.wait_ge(qev.sem, 4 * si + 4)
                for k4 in range(4):
                    ti = si * 4 + k4
                    if ti >= 1:
                        t.wait_ge(vbe.sem, ti)
                    vtr.inc(t.transpose(
                        psV[:, :],
                        vt_tmp[:, (si % 2) * 512 + k4 * 128:(si % 2) * 512 + (k4 + 1) * 128],
                        ident_s))

            @block.tensor
            def _(t):
                t.wait_ge(wl.sem, 32)
                t.wait_ge(bp.sem, 1)
                for si in range(NCH):
                    if si >= 1:
                        t.wait_ge(qev.sem, 4 * si)
                    for kt in range(KT):
                        idx = si * 16 + kt
                        t.wait_ge(yld[idx % 4].sem, 16 * (idx // 4 + 1))
                        rhs = ybuf[:, (idx % 4) * 512:(idx % 4 + 1) * 512]
                        st = (kt == 0)
                        sp = (kt == KT - 1)
                        t.matmul(psS1[:, 0:512], wq_s[:, kt * 256:kt * 256 + 128],
                                 rhs, start=st, stop=sp)
                        t.matmul(psS1[:, 512:1024],
                                 wq_s[:, kt * 256 + 128:kt * 256 + 256],
                                 rhs, start=st, stop=sp)
                        t.matmul(psS2[:, 0:512], wkv_s[:, kt * 256:kt * 256 + 128],
                                 rhs, start=st, stop=sp)
                        qk.inc(t.matmul(psS2[:, 512:1024],
                                        wkv_s[:, kt * 256 + 128:kt * 256 + 256],
                                        rhs, start=st, stop=sp))
                    if si >= 1:
                        emit_transposes(t, si - 1, ORD[si - 1])
                emit_transposes(t, NCH - 1, ORD[NCH - 1])

            @block.scalar
            def _(s):
                s.wait_ge(bp.sem, 1)
                for si in range(NCH):
                    ch = ORD[si]
                    s.wait_ge(qk.sem, 16 * (si + 1))
                    qev.inc(s.activation(qT[0][:, ch * 512:(ch + 1) * 512],
                                         psS1[:, 0:512], AF.Identity,
                                         bias=bqs[:, 0:1],
                                         scale=float(1.0 / np.sqrt(128.0))))
                    qev.inc(s.activation(qT[1][:, ch * 512:(ch + 1) * 512],
                                         psS1[:, 512:1024], AF.Identity,
                                         bias=bqs[:, 1:2],
                                         scale=float(1.0 / np.sqrt(128.0))))
                    qev.inc(s.activation(kT[:, ch * 512:(ch + 1) * 512],
                                         psS2[:, 0:512], AF.Identity, bias=bqs[:, 2:3]))
                    if si >= 2:
                        s.wait_ge(vtr.sem, 4 * (si - 1))
                    qev.inc(s.activation(vt_tmp[:, (si % 2) * 512:(si % 2 + 1) * 512],
                                         psS2[:, 512:1024], AF.Identity, bias=bqs[:, 3:4]))

            @block.vector
            def _(v):
                for si in range(NCH):
                    for k4 in range(4):
                        v.wait_ge(vtr.sem, 4 * si + k4 + 1)
                        kt = ORD[si] * 4 + k4
                        v.tensor_copy(vb[:, kt * 130:kt * 130 + 64], psV[:, 0:64])
                        vbe.inc(v.tensor_copy(vb[:, kt * 130 + 65:kt * 130 + 129],
                                              psV[:, 64:128]))

            @block.scalar
            def _(s):
                for ci in range(48):
                    if ci >= 2:
                        s.wait_ge(qnm.sem, ci - 1)
                    if ci < 32:
                        src = qT[ci // 16][:, (ci % 16) * 512:((ci % 16) + 1) * 512]
                    else:
                        src = kT[:, (ci - 32) * 512:(ci - 31) * 512]
                    sqs.inc(s.activation(sqb[:, (ci % 2) * 512:(ci % 2 + 1) * 512],
                                         src, AF.Square))

            @block.tensor
            def _(t):
                for ci in range(48):
                    t.wait_ge(sqs.sem, ci + 1)
                    if ci >= 1:
                        t.wait_ge(qnr.sem, ci)
                    qnm.inc(t.matmul(psM[0:1, 0:512], ones_b[:, 0:1],
                                     sqb[:, (ci % 2) * 512:(ci % 2 + 1) * 512],
                                     start=True, stop=True))

            @block.vector
            def _(v):
                for ci in range(48):
                    v.wait_ge(qnm.sem, ci + 1)
                    qnr.inc(v.tensor_reduce(mrows[0:1, ci:ci + 1], psM[0:1, 0:512],
                                            axis=mybir.AxisListType.X, op=ALU.max))
                v.drain()
                v.tensor_reduce(mrows[0:1, 48:49], mrows[0:1, 32:48],
                                axis=mybir.AxisListType.X, op=ALU.max)
                v.drain()
                ngm.inc(v.tensor_scalar(mrows[0:1, 0:32], mrows[0:1, 0:32],
                                        mrows[0:1, 48:49], None, ALU.mult))

            @block.scalar
            def _(s):
                s.wait_ge(ngm.sem, 1)
                s.activation(mrows[0:1, 0:32], mrows[0:1, 0:32], AF.Ln)
                s.drain()
                nsc.inc(s.activation(mrows[0:1, 0:32], mrows[0:1, 0:32],
                                     AF.Exp, scale=0.5))

            @block.vector
            def _(v):
                v.wait_ge(nsc.sem, 1)
                ngm.inc(v.tensor_scalar(mrows[0:1, 0:32], mrows[0:1, 0:32],
                                        -1.0, None, ALU.mult))

            @block.tensor
            def _(t):
                t.wait_ge(ngm.sem, 2)
                mbm.inc(t.matmul(psM[:, 0:32], ones_f[0:1, :], mrows[0:1, 0:32],
                                 start=True, stop=True))

            @block.vector
            def _(v):
                v.wait_ge(mbm.sem, 1)
                mbe.inc(v.tensor_copy(mbias, psM[:, 0:32]))

            @block.tensor
            def _(t):
                t.wait_ge(mbe.sem, 1)
                t.wait_ge(vbe.sem, 4 * NCH)
                kgi = 0
                pend = [None]
                for ci in range(32):
                    h, ch = divmod(ci, NCH)

                    def emit_out0(kg, kgl, ng=NG[ch], ci=ci, h=h, ch=ch):
                        if kg == 0 and ci >= 1:
                            t.wait_ge(aoev.sem, 2 * ci)
                        t.wait_ge(xp.sem, kgl + 1)
                        if kg >= ng - 2:
                            t.wait_ge(mkv.sem, 4 * ci + 2 * (kg - (ng - 2) + 1))
                        st = (kg == 0)
                        sp = (kg == ng - 1)
                        for j in range(2):
                            kt = kg * 2 + j
                            t.matmul(psO[0:65, 0:512],
                                     vb[:, kt * 130:kt * 130 + 65],
                                     prb[:, (kgl % 2) * 1024 + j * 512:(kgl % 2) * 1024 + (j + 1) * 512],
                                     start=(st and j == 0), stop=(sp and j == 1))
                            mm = t.matmul(
                                psO[0:65, 512:1024],
                                vb[:, kt * 130 + 65:kt * 130 + 130],
                                prb[:, (kgl % 2) * 1024 + j * 512:(kgl % 2) * 1024 + (j + 1) * 512],
                                start=(st and j == 0), stop=(sp and j == 1))
                            if j == 1:
                                omm.inc(mm)

                    for kg in range(NG[ch]):
                        pgrp = [psS1, psS2][kgi % 2]
                        if kgi >= 2:
                            t.wait_ge(xp.sem, kgi - 1)
                        for j in range(2):
                            kt = kg * 2 + j
                            mm = t.matmul(pgrp[:, j * 512:(j + 1) * 512],
                                          kT[:, kt * 128:(kt + 1) * 128],
                                          qT[h][:, ch * 512:(ch + 1) * 512],
                                          start=True, stop=True)
                            if j == 1:
                                smm.inc(mm)
                        if kg == 0 and pend[0] is not None:
                            t.wait_ge(rcp.sem, pend[0] + 1)
                            rbm.inc(t.matmul(psM[0:64, 0:512], ones_f[64:65, 0:64],
                                             rr[64:65, 0:512], start=True, stop=True))
                            pend[0] = None
                        if kg >= 1:
                            emit_out0(kg - 1, kgi - 1)
                        kgi += 1
                    emit_out0(NG[ch] - 1, kgi - 1)
                    pend[0] = ci
                t.wait_ge(rcp.sem, pend[0] + 1)
                rbm.inc(t.matmul(psM[0:64, 0:512], ones_f[64:65, 0:64],
                                 rr[64:65, 0:512], start=True, stop=True))

            @block.scalar
            def _(s):
                s.wait_ge(mbe.sem, 1)
                kgi = 0
                for ci in range(32):
                    h, ch = divmod(ci, NCH)
                    for kg in range(NG[ch]):
                        s.wait_ge(smm.sem, kgi + 1)
                        if kgi >= 2:
                            s.wait_ge(omm.sem, kgi - 1)
                        pgrp = [psS1, psS2][kgi % 2]
                        xp.inc(s.activation(prb[:, (kgi % 2) * 1024:(kgi % 2 + 1) * 1024],
                                            pgrp[:, 0:1024], AF.Exp,
                                            bias=mbias[:, ci:ci + 1]))
                        kgi += 1

            @block.vector
            def _(v):
                kgi = 0
                for ci in range(32):
                    h, ch = divmod(ci, NCH)
                    for kg in range(NG[ch]):
                        if kg >= NG[ch] - 2:
                            v.wait_ge(xp.sem, kgi + 1)
                            for j in range(2):
                                kt = kg * 2 + j
                                dlt = kt * 128 - ch * 512
                                sl = 384 - dlt
                                mkv.inc(v.tensor_tensor(
                                    prb[:, (kgi % 2) * 1024 + j * 512:(kgi % 2) * 1024 + (j + 1) * 512],
                                    prb[:, (kgi % 2) * 1024 + j * 512:(kgi % 2) * 1024 + (j + 1) * 512],
                                    mask_s[:, sl:sl + 512], ALU.mult))
                        kgi += 1
                    v.wait_ge(omm.sem, kgi)
                    rcp.inc(v.reciprocal(rr[64:65, 0:512], psO[64:65, 0:512]))
                    v.wait_ge(rbm.sem, ci + 1)
                    v.tensor_copy(rcb[0:64, :], psM[0:64, 0:512])
                    v.drain()
                    v.tensor_tensor(aoT[h][0:64, ch * 512:(ch + 1) * 512],
                                    psO[0:64, 0:512], rcb[0:64, :], ALU.mult)
                    aoev.inc(v.tensor_tensor(aoT[h][64:128, ch * 512:(ch + 1) * 512],
                                             psO[0:64, 512:1024], rcb[0:64, :],
                                             ALU.mult), 2)

            @block.sync
            def _(sy):
                sy.wait_ge(aoev.sem, 64)
                if P.ao_ext is not None:
                    for h2 in range(2):
                        stf.inc(sy.dma_start(out=P.ao_ext[h2 * 128:(h2 + 1) * 128, :],
                                             in_=aoT[h2]), 16)
                    sy.wait_ge(stf.sem, 32)
                else:
                    n = 0
                    for r in range(NC):
                        for h2 in range(2):
                            stf.inc(sy.dma_start(
                                out=P.ao_in[256 * r + 128 * h2:256 * r + 128 * (h2 + 1), :],
                                in_=aoT[h2][:, r * TC:(r + 1) * TC]), 16)
                            n += 16
                    sy.wait_ge(stf.sem, n)

        _phase_close(nc, sems)


def build_attn0():
    nc = bass.Bass("TRN2", target_bir_lowering=False)
    P = NS()
    P.tag = ""
    yT = nc.dram_tensor("yT", [D, T], BF16, kind="ExternalInput").ap()
    P.y_src = lambda kt, ch: yT[kt * 128:(kt + 1) * 128, ch * 512:(ch + 1) * 512]
    P.wqS = nc.dram_tensor("wqS", [128, KT * 256], F32, kind="ExternalInput").ap()
    P.wkvS = nc.dram_tensor("wkvS", [128, KT * 256], F32, kind="ExternalInput").ap()
    P.bq_c = nc.dram_tensor("bq_c", [128, 2], F32, kind="ExternalInput").ap()
    P.bkv_c = nc.dram_tensor("bkv_c", [128, 2], F32, kind="ExternalInput").ap()
    P.mask01 = nc.dram_tensor("mask01", [128, 896], BF16, kind="ExternalInput").ap()
    P.cst = _decl_consts(nc)
    P.ao_ext = nc.dram_tensor("aoT", [256, T], BF16, kind="ExternalOutput").ap()
    P.ao_in = None
    emit_attn0(nc, P)
    return nc


def attn0_prep(inp):
    wqkv = _f32(inp["L0_attn_w_qkv"])
    bqkv = _f32(inp["L0_attn_b_qkv"])

    def tile_in_out(w):
        Fd = w.shape[1]
        return np.ascontiguousarray(
            w.reshape(KT, 128, Fd).transpose(1, 0, 2).reshape(128, KT * Fd))

    pre = {"wkvS": _f32(tile_in_out(wqkv[:, 2048:2304]))}
    bkv = np.stack([bqkv[2048:2176], bqkv[2176:2304]], axis=1)
    pre["bkv_c"] = _f32(bkv)
    pre["wqS"] = [None] * NC
    pre["bq_c"] = [None] * NC
    for c in range(NC):
        pre["wqS"][c] = _f32(tile_in_out(wqkv[:, 256 * c:256 * (c + 1)]))
        bq = bqkv[256 * c:256 * (c + 1)].reshape(2, 128).T
        pre["bq_c"][c] = _f32(bq)
    k = np.arange(128)[:, None]
    u = np.arange(896)[None, :]
    pre["mask01"] = _bf((k <= u - 384).astype(np.float32))
    return pre


def attn0_in_map(c, yT_bf, pre):
    return dict(yT=yT_bf, wqS=pre["wqS"][c], wkvS=pre["wkvS"],
                bq_c=pre["bq_c"][c], bkv_c=pre["bkv_c"], mask01=pre["mask01"],
                **CONSTS)



def emit_lnx(nc, P, rms=False):
    cst, yoT = P.cst, getattr(P, "yoT", None)
    if rms:
        wa_c, hoT = P.wa_c, getattr(P, "hoT", None)
    xs = P.xs
    xT = P.xT if P.load_x else None

    with ExitStack() as ex:
        sbt = lambda nm, sh, dt: ex.enter_context(nc.sbuf_tensor(nm + P.tag, sh, dt)).ap()
        pst = lambda nm, sh, dt: ex.enter_context(nc.psum_tensor(nm + P.tag, sh, dt)).ap()
        g = NS()
        yb = sbt("yb", [128, KT * TC], BF16)
        sqb = sbt("sqb", [128, KT * TC], BF16)
        hb = sbt("hb", [128, KT * TC], BF16) if rms else None
        g.tmp = sbt("tmp", [128, TC], F32)
        g.rows = sbt("rows", [1, 4 * TC], F32)
        g.onesb_s = sbt("onesb_s", [128, 128], BF16)
        g.onesf_s = sbt("onesf_s", [128, 128], F32)
        wac = sbt("wac", [128, KT], F32) if rms else None
        g.psA = pst("psA", [128, 1024], F32)
        g.psB = pst("psB", [128, 1024], F32)
        g.psC = pst("psC", [128, 1024], F32)

        sems = []
        with nc.Block() as block:
            mk = _mk_factory(nc, P.tag, sems)
            g.xld = mk("xld") if P.load_x else None
            cld = mk("cld")
            g.cld = cld
            g.scal = mk("scal")
            g.vec = mk("vec")
            g.tens = mk("tens")
            stf = mk("stf")

            @block.sync
            def _(sy):
                if P.load_x:
                    for i in range(KT):
                        g.xld.inc(sy.dma_start(out=xs[:, i * TC:(i + 1) * TC],
                                               in_=xT[i * 128:(i + 1) * 128, :]), 16)
                cld.inc(sy.dma_start(out=g.onesb_s, in_=cst.onesb), 16)
                cld.inc(sy.dma_start(out=g.onesf_s, in_=cst.onesf), 16)
                if rms:
                    cld.inc(sy.dma_start(out=wac, in_=wa_c), 16)
            g.cld_total = 48 if rms else 32

            emit_ln(block, g, xs, yb, sqb)

            if rms:
                @block.vector
                def _(v):
                    v.wait_ge(g.vec.sem, g.ln_done)
                    last = None
                    for i in range(KT):
                        last = v.tensor_tensor(sqb[:, i * TC:(i + 1) * TC],
                                               yb[:, i * TC:(i + 1) * TC],
                                               yb[:, i * TC:(i + 1) * TC], ALU.mult)
                    g.r_sq = g.vec.inc(last)

                @block.tensor
                def _(t):
                    t.wait_ge(g.vec.sem, g.r_sq)
                    last = None
                    for i in range(KT):
                        for ch in range(2):
                            last = t.matmul(g.psB[0:1, ch * 512:(ch + 1) * 512],
                                            g.onesb_s[:, 0:1],
                                            sqb[:, i * TC + ch * 512:i * TC + (ch + 1) * 512],
                                            start=(i == 0), stop=(i == KT - 1))
                    g.r_mm = g.tens.inc(last)

                @block.vector
                def _(v):
                    v.wait_ge(g.tens.sem, g.r_mm)
                    var = g.rows[0:1, 0:TC]
                    v.tensor_scalar(var, g.psB[0:1, 0:TC], 1.0 / D, None, ALU.mult)
                    v.drain()
                    g.r_var = g.vec.inc(
                        v.tensor_scalar(var, var, float(LNEPS), None, ALU.add))

                @block.scalar
                def _(s):
                    s.wait_ge(g.vec.sem, g.r_var)
                    var = g.rows[0:1, 0:TC]
                    lnv = g.rows[0:1, TC:2 * TC]
                    s.activation(lnv, var, AF.Ln)
                    s.drain()
                    g.r_rstd = g.scal.inc(s.activation(var, lnv, AF.Exp, scale=-0.5))

                @block.tensor
                def _(t):
                    t.wait_ge(g.scal.sem, g.r_rstd)
                    last = None
                    for ch in range(2):
                        last = t.matmul(g.psA[:, ch * 512:(ch + 1) * 512],
                                        g.onesf_s[0:1, :],
                                        g.rows[0:1, ch * 512:(ch + 1) * 512],
                                        start=True, stop=True)
                    g.r_bc = g.tens.inc(last)

                @block.vector
                def _(v):
                    v.wait_ge(g.tens.sem, g.r_bc)
                    last = None
                    for i in range(KT):
                        last = v.scalar_tensor_tensor(
                            hb[:, i * TC:(i + 1) * TC], yb[:, i * TC:(i + 1) * TC],
                            wac[:, i:i + 1], g.psA[:, 0:TC], ALU.mult, ALU.mult)
                    g.r_h = g.vec.inc(last)

            yoT_h = getattr(P, "yoT_halves", None)
            hoT_h = getattr(P, "hoT_halves", None)
            ag_emit = getattr(P, "ag_emit", None)
            agbuf, agdst = (hb, hoT_h) if rms and hoT_h is not None else (yb, yoT_h)

            stfH = [mk("stfA"), mk("stfB")] if agdst is not None else None

            @block.sync
            def _(sy):
                sy.wait_ge(g.vec.sem, g.r_h if rms else g.ln_done)
                n = 0
                if agdst is not None:
                    for hf in range(2):
                        for i in range(KT):
                            stfH[hf].inc(sy.dma_start(
                                out=agdst[hf][i * 128:(i + 1) * 128, :],
                                in_=agbuf[:, i * TC + hf * 512:i * TC + (hf + 1) * 512]), 16)
                for i in range(KT):
                    if yoT is not None:
                        stf.inc(sy.dma_start(out=yoT[i * 128:(i + 1) * 128, :],
                                             in_=yb[:, i * TC:(i + 1) * TC]), 16)
                        n += 16
                    if rms and hoT is not None:
                        stf.inc(sy.dma_start(out=hoT[i * 128:(i + 1) * 128, :],
                                             in_=hb[:, i * TC:(i + 1) * TC]), 16)
                        n += 16
                if stfH is not None:
                    sy.wait_ge(stfH[0].sem, 16 * KT)
                    sy.wait_ge(stfH[1].sem, 16 * KT)
                if n:
                    sy.wait_ge(stf.sem, n)

            if ag_emit is not None:
                @block.gpsimd
                def _(gp):
                    gp.wait_ge(stfH[0].sem, 16 * KT)
                    ag_emit(gp, 0)
                    gp.wait_ge(stfH[1].sem, 16 * KT)
                    ag_emit(gp, 1)

        _phase_close(nc, sems)


def build_lnx(rms=False):
    nc = bass.Bass("TRN2", target_bir_lowering=False)
    P = NS()
    P.tag = ""
    P.load_x = True
    P.xT = nc.dram_tensor("xT", [D, TC], F32, kind="ExternalInput").ap()
    P.cst = _decl_consts(nc)
    P.yoT = nc.dram_tensor("yoT", [D, TC], BF16, kind="ExternalOutput").ap()
    if rms:
        P.wa_c = nc.dram_tensor("wa_c", [128, KT], F32, kind="ExternalInput").ap()
        P.hoT = nc.dram_tensor("hoT", [D, TC], BF16, kind="ExternalOutput").ap()
    with ExitStack() as gx:
        P.xs = gx.enter_context(nc.sbuf_tensor("xs_g", [128, KT * TC], F32)).ap()
        emit_lnx(nc, P, rms=rms)
    return nc


def emit_proj(nc, P):
    aT, wf, bf_c = P.aT, P.wf, P.bf_c
    xs = P.xs
    xT = P.xT if P.load_x else None
    xoT = P.xoT if P.store_x else None

    with ExitStack() as ex:
        sbt = lambda nm, sh, dt: ex.enter_context(nc.sbuf_tensor(nm + P.tag, sh, dt)).ap()
        pst = lambda nm, sh, dt: ex.enter_context(nc.psum_tensor(nm + P.tag, sh, dt)).ap()
        asb = sbt("asb", [128, KT * TC], BF16)
        wfb = sbt("wfb", [128, KT * D], BF16)
        bfc = sbt("bfc", [128, KT], F32)
        psA = pst("psA", [128, 1024], F32)
        psB = pst("psB", [128, 1024], F32)
        psC = pst("psC", [128, 1024], F32)
        psD = pst("psD", [128, 1024], F32)
        PB = [lambda a=a, c=c: [psA, psB, psC, psD][a][:, c * 512:(c + 1) * 512]
              for a in range(4) for c in range(2)]

        sems = []
        with nc.Block() as block:
            mk = _mk_factory(nc, P.tag, sems)
            xld = mk("xld") if P.load_x else None
            ald = mk("ald")
            wld = mk("wld", dma=True)
            cld = mk("cld")
            pmm = mk("pmm")
            pev = mk("pev")
            stf = mk("stf")

            @block.sync
            def _(sy):
                if P.load_x:
                    for i in range(KT):
                        xld.inc(sy.dma_start(out=xs[:, i * TC:(i + 1) * TC],
                                             in_=xT[i * 128:(i + 1) * 128, :]), 16)
                for i in range(KT):
                    ald.inc(sy.dma_start(out=asb[:, i * TC:(i + 1) * TC],
                                         in_=aT[i * 128:(i + 1) * 128, :]), 16)
                cld.inc(sy.dma_start(out=bfc, in_=bf_c), 16)

            @block.gpsimd
            def _(gp):
                for kt in range(KT):
                    wld.inc(gp.dma_start(out=wfb[:, kt * D:(kt + 1) * D],
                                         in_=wf[kt * 128:(kt + 1) * 128, :]), 16)

            @block.tensor
            def _(t):
                t.wait_ge(ald.sem, 16 * KT)
                t.wait_ge(wld.sem, 16 * KT)
                c = 0
                for d in range(KT):
                    for ch in range(2):
                        if c >= 8:
                            t.wait_ge(pev.sem, c - 7)
                        last = None
                        for kt in range(KT):
                            last = t.matmul(PB[c % 8](),
                                            wfb[:, kt * D + d * 128:kt * D + (d + 1) * 128],
                                            asb[:, kt * TC + ch * 512:kt * TC + (ch + 1) * 512],
                                            start=(kt == 0), stop=(kt == KT - 1))
                        pmm.inc(last)
                        c += 1

            @block.vector
            def _(v):
                if P.load_x:
                    v.wait_ge(xld.sem, 16 * KT)
                v.wait_ge(cld.sem, 16)
                c = 0
                for d in range(KT):
                    for ch in range(2):
                        v.wait_ge(pmm.sem, c + 1)
                        dst = xs[:, d * TC + ch * 512:d * TC + (ch + 1) * 512]
                        pev.inc(v.scalar_tensor_tensor(dst, PB[c % 8](),
                                                       bfc[:, d:d + 1],
                                                       dst, ALU.add, ALU.add))
                        c += 1

            if P.store_x:
                @block.sync
                def _(sy):
                    sy.wait_ge(pev.sem, 32)
                    for d in range(KT):
                        stf.inc(sy.dma_start(out=xoT[d * 128:(d + 1) * 128, :],
                                             in_=xs[:, d * TC:(d + 1) * TC]), 16)
                    sy.wait_ge(stf.sem, 16 * KT)

        _phase_close(nc, sems)


def build_proj():
    nc = bass.Bass("TRN2", target_bir_lowering=False)
    P = NS()
    P.tag = ""
    P.load_x = True
    P.store_x = True
    P.xT = nc.dram_tensor("xT", [D, TC], F32, kind="ExternalInput").ap()
    P.aT = nc.dram_tensor("aT", [D, TC], BF16, kind="ExternalInput").ap()
    P.wf = nc.dram_tensor("wf", [D, D], F32, kind="ExternalInput").ap()
    P.bf_c = nc.dram_tensor("bf_c", [128, KT], F32, kind="ExternalInput").ap()
    P.xoT = nc.dram_tensor("xoT", [D, TC], F32, kind="ExternalOutput").ap()
    with ExitStack() as gx:
        P.xs = gx.enter_context(nc.sbuf_tensor("xs_g", [128, KT * TC], F32)).ap()
        emit_proj(nc, P)
    return nc


def emit_fold(nc, P):
    xT_pre = getattr(P, "xT_pre", None)
    wup, wgt, wdn = P.wup, P.wgt, P.wdn
    su_ug, su_dp = P.su_ug, P.su_dp
    wsc_up, wsc_gt, wsc_dn = P.wsc_up, P.wsc_gt, P.wsc_dn
    h128, cst = P.h128, P.cst
    a_up, a_gt, a_dn = P.a_up, P.a_gt, P.a_dn

    with ExitStack() as ex:
        sbt = lambda nm, sh, dt: ex.enter_context(nc.sbuf_tensor(nm + P.tag, sh, dt)).ap()
        pst = lambda nm, sh, dt: ex.enter_context(nc.psum_tensor(nm + P.tag, sh, dt)).ap()
        W = sbt("W", [128, 64 * 256], BF16)
        B1 = sbt("B1", [128, 64 * 256], BF16)
        B2 = sbt("B2", [128, 64 * 256], BF16)
        ob = sbt("ob", [128, 64 * 256], BF16)
        suc = sbt("suc", [128, 64 + 16], F32)
        wsc = sbt("wsc", [1, 1024], F32)
        h128s = sbt("h128s", [128, 128], BF16)
        onesf_s = sbt("onesf_s", [128, 128], F32)
        psA = pst("psA", [128, 1024], F32)
        psB = pst("psB", [128, 1024], F32)

        specs = [
            ("up", wup, 16, 1024, a_up, (0, 16), wsc_up, 1024, 4, float(2.0 ** -5.5)),
            ("gt", wgt, 16, 1024, a_gt, (0, 16), wsc_gt, 1024, 4, float(2.0 ** -5.5)),
            ("dn", wdn, 64, 256, a_dn, (16, 80), wsc_dn, 256, 6, float(2.0 ** -6.5)),
        ]

        sems = []
        with nc.Block() as block:
            mk = _mk_factory(nc, P.tag, sems)
            cld = mk("cld")
            wscld = mk("wscld")
            wl = [mk("wl%d" % i, dma=True) for i in range(3)]
            s1 = mk("s1")
            e1 = mk("e1")
            e1s = mk("e1s")
            bfl = mk("bfl")
            scd = mk("scd")
            bcm = mk("bcm")
            stf = mk("stf")

            @block.sync
            def _(sy):
                cld.inc(sy.dma_start(out=h128s, in_=h128), 16)
                cld.inc(sy.dma_start(out=onesf_s, in_=cst.onesf), 16)
                cld.inc(sy.dma_start(out=suc[:, 0:16], in_=su_ug), 16)
                cld.inc(sy.dma_start(out=suc[:, 16:80], in_=su_dp), 16)

            if xT_pre is not None:
                xld = mk("xld", dma=True)
                P.xld_out = xld

                @block.sync
                def _(sy):
                    for i in range(KT):
                        xld.inc(sy.dma_start(out=P.xs[:, i * TC:(i + 1) * TC],
                                             in_=xT_pre[i * 128:(i + 1) * 128, :]), 16)

            STF_TOT = [0, 16 * 8, 16 * 16, 16 * 20]

            @block.gpsimd
            def _(gp):
                srcs = [wup, wgt, wdn]
                for mi, (nm, w_ap, ntile, width, *_rest) in enumerate(specs):
                    if mi > 0:
                        gp.wait_ge(stf.sem, STF_TOT[mi])
                    for i in range(ntile):
                        wl[mi].inc(gp.dma_start(
                            out=W[:, i * width:(i + 1) * width],
                            in_=bass.AP(srcs[mi].tensor, i * 128 * width,
                                        [[width, 128], [1, width]])), 16)

            @block.vector
            def _(v):
                v.wait_ge(cld.sem, 64)
                v.tensor_scalar(suc[:, 0:16], suc[:, 0:16], float(2.0 ** -5.5),
                                None, ALU.mult)
                v.tensor_scalar(suc[:, 16:80], suc[:, 16:80], float(2.0 ** -6.5),
                                None, ALU.mult)
                v.drain()

            mtile_base = 0
            for mi, (nm, w_ap, ntile, width, out_ap, su_sl, wsc_ap, wsc_w,
                     nstage, smul) in enumerate(specs):

                @block.sync
                def _(sy, mi=mi, wsc_ap=wsc_ap, wsc_w=wsc_w):
                    if mi >= 1:
                        sy.wait_ge(bcm.sem, mi)
                    wscld.inc(sy.dma_start(out=wsc[0:1, 0:wsc_w], in_=wsc_ap), 16)

                @block.tensor
                def _(t, mi=mi, ntile=ntile, width=width):
                    t.wait_ge(cld.sem, 64)
                    t.wait_ge(wl[mi].sem, 16 * ntile)
                    if mi > 0:
                        t.wait_ge(scd.sem, sum(sp[2] for sp in specs[:mi]))
                    nch = width // 512 if width >= 512 else 1
                    eb_v = sum((sp[2] + 1) // 2 for sp in specs[:mi])
                    eb_s = sum(sp[2] // 2 for sp in specs[:mi])
                    for i in range(ntile):
                        if i >= 2:
                            j = i - 2
                            if j % 2 == 0:
                                t.wait_ge(e1.sem, eb_v + j // 2 + 1)
                            else:
                                t.wait_ge(e1s.sem, eb_s + j // 2 + 1)
                        last = None
                        if width >= 512:
                            for ch in range(width // 512):
                                last = t.matmul(
                                    [psA, psB][i % 2][:, ch * 512:(ch + 1) * 512],
                                    h128s,
                                    W[:, i * width + ch * 512:i * width + (ch + 1) * 512],
                                    start=True, stop=True)
                        else:
                            last = t.matmul([psA, psB][i % 2][:, 0:width], h128s,
                                            W[:, i * width:(i + 1) * width],
                                            start=True, stop=True)
                        s1.inc(last)

                @block.vector
                def _(v, mi=mi, ntile=ntile, width=width):
                    for i in range(0, ntile, 2):
                        v.wait_ge(s1.sem, mtile_base + i + 1)
                        e1.inc(v.tensor_copy(B1[:, i * width:(i + 1) * width],
                                             [psA, psB][i % 2][:, 0:width]))

                @block.scalar
                def _(sc, mi=mi, ntile=ntile, width=width):
                    for i in range(1, ntile, 2):
                        sc.wait_ge(s1.sem, mtile_base + i + 1)
                        e1s.inc(sc.activation(B1[:, i * width:(i + 1) * width],
                                              [psA, psB][i % 2][:, 0:width],
                                              AF.Copy))

                @block.vector
                def _(v, mi=mi, ntile=ntile, width=width, nstage=nstage):
                    v.wait_ge(e1.sem, sum((sp[2] + 1) // 2 for sp in specs[:mi + 1]))
                    v.wait_ge(e1s.sem, sum(sp[2] // 2 for sp in specs[:mi + 1]))
                    v.drain()
                    bufs = [B1, B2]
                    FW = 64 * 256
                    last = None
                    for st in range(nstage):
                        src, dst = bufs[st % 2], bufs[(st + 1) % 2]
                        hh = 1 << st
                        ng2 = ntile // (2 * hh)
                        blkw = hh * width
                        lo_o = bass.AP(dst.tensor, 0, [[FW, 128], [2 * blkw, ng2], [1, blkw]])
                        lo_a = bass.AP(src.tensor, 0, [[FW, 128], [2 * blkw, ng2], [1, blkw]])
                        lo_b = bass.AP(src.tensor, blkw, [[FW, 128], [2 * blkw, ng2], [1, blkw]])
                        hi_o = bass.AP(dst.tensor, blkw, [[FW, 128], [2 * blkw, ng2], [1, blkw]])
                        v.tensor_tensor(lo_o, lo_a, lo_b, ALU.add)
                        last = v.tensor_tensor(hi_o, lo_a, lo_b, ALU.subtract)
                        if st < nstage - 1:
                            v.drain()
                    bfl.inc(last)

                @block.tensor
                def _(t, mi=mi, wsc_w=wsc_w):
                    t.wait_ge(wscld.sem, 16 * (mi + 1))
                    t.wait_ge(bfl.sem, mi + 1)
                    last = None
                    for ch in range((wsc_w + 511) // 512):
                        wdt = min(512, wsc_w - ch * 512)
                        last = t.matmul(psA[:, ch * 512:ch * 512 + wdt],
                                        onesf_s[0:1, :],
                                        wsc[0:1, ch * 512:ch * 512 + wdt],
                                        start=True, stop=True)
                    bcm.inc(last)

                @block.vector
                def _(v, mi=mi, ntile=ntile, width=width, su_sl=su_sl, smul=smul,
                      nstage=nstage):
                    v.wait_ge(bcm.sem, mi + 1)
                    src = [B1, B2][nstage % 2]
                    last = None
                    for i in range(ntile):
                        if mi < 2:
                            dst = bass.AP(ob.tensor, i * 128,
                                          [[64 * 256, 128], [16 * 128, 8], [1, 128]])
                        else:
                            dst = ob[:, i * width:(i + 1) * width]
                        last = v.scalar_tensor_tensor(
                            dst,
                            src[:, i * width:(i + 1) * width],
                            suc[:, su_sl[0] + i:su_sl[0] + i + 1],
                            psA[:, 0:width], ALU.mult, ALU.mult)
                        scd.inc(last)

                @block.sync
                def _(sy, mi=mi, ntile=ntile, width=width, out_ap=out_ap):
                    sy.wait_ge(scd.sem, sum(sp[2] for sp in specs[:mi + 1]))
                    nslab = 8 if mi < 2 else 4
                    w2 = 2048 if mi < 2 else 4096
                    for o in range(nslab):
                        stf.inc(sy.dma_start(
                            out=bass.AP(out_ap.tensor, o * 128 * w2,
                                        [[w2, 128], [1, w2]]),
                            in_=ob[:, o * w2:(o + 1) * w2]), 16)

                mtile_base += ntile

            @block.sync
            def _(sy):
                sy.wait_ge(stf.sem, STF_TOT[3])

        _phase_close(nc, sems)


def _fold_drams(nc, P):
    P.wup = nc.dram_tensor("wup", [D, 1024], F32, kind="ExternalInput").ap()
    P.wgt = nc.dram_tensor("wgt", [D, 1024], F32, kind="ExternalInput").ap()
    P.wdn = nc.dram_tensor("wdn", [8192, 256], F32, kind="ExternalInput").ap()
    P.su_ug = nc.dram_tensor("su_ug", [128, 16], F32, kind="ExternalInput").ap()
    P.su_dp = nc.dram_tensor("su_dp", [128, 64], F32, kind="ExternalInput").ap()
    P.wsc_up = nc.dram_tensor("wsc_up", [1, 1024], F32, kind="ExternalInput").ap()
    P.wsc_gt = nc.dram_tensor("wsc_gt", [1, 1024], F32, kind="ExternalInput").ap()
    P.wsc_dn = nc.dram_tensor("wsc_dn", [1, 256], F32, kind="ExternalInput").ap()
    P.h128 = nc.dram_tensor("h128", [128, 128], BF16, kind="ExternalInput").ap()


def build_fold():
    nc = bass.Bass("TRN2", target_bir_lowering=False)
    P = NS()
    P.tag = ""
    _fold_drams(nc, P)
    P.cst = _decl_consts(nc)
    P.a_up = nc.dram_tensor("a_up", [8 * 128, D], BF16, kind="ExternalOutput").ap()
    P.a_gt = nc.dram_tensor("a_gt", [8 * 128, D], BF16, kind="ExternalOutput").ap()
    P.a_dn = nc.dram_tensor("a_dn", [4 * 128, 4096], BF16, kind="ExternalOutput").ap()
    emit_fold(nc, P)
    return nc


def emit_swiglu(nc, P):
    auS, agS, adS, cst = P.auS, P.agS, P.adS, P.cst
    xs = P.xs
    xT = P.xT if P.load_x else None
    xoT = P.xoT if P.store_x else None

    with ExitStack() as ex:
        sbt = lambda nm, sh, dt: ex.enter_context(nc.sbuf_tensor(nm + P.tag, sh, dt)).ap()
        pst = lambda nm, sh, dt: ex.enter_context(nc.psum_tensor(nm + P.tag, sh, dt)).ap()
        g = NS()
        yb = sbt("yb", [128, KT * TC], BF16)
        hb = sbt("hb", [128, 16 * TC], BF16)
        wub = sbt("wub", [128, 4 * D], BF16)
        adb = sbt("adb", [128, 2 * 16 * 256], BF16)
        sqb = sbt("sqb", [128, 2 * TC], BF16)
        tmpbf = sbt("tmpbf", [128, 2 * 512], BF16)
        g.tmp = sbt("tmp", [128, TC], F32)
        g.rows = sbt("rows", [1, 4 * TC], F32)
        g.onesb_s = sbt("onesb_s", [128, 128], BF16)
        g.onesf_s = sbt("onesf_s", [128, 128], F32)
        g.psA = pst("psA", [128, 1024], F32)
        g.psB = pst("psB", [128, 1024], F32)
        g.psC = pst("psC", [128, 1024], F32)
        psD = pst("psD", [128, 1024], F32)

        sems = []
        with nc.Block() as block:
            mk = _mk_factory(nc, P.tag, sems)
            g.xld = mk("xld") if P.load_x else None
            cld = mk("cld")
            g.cld = cld
            g.scal = mk("scal")
            g.vec = mk("vec")
            g.tens = mk("tens")
            wls = [mk("wls%d" % i) for i in range(2)]
            adl = [mk("adl%d" % i, dma=True) for i in range(2)]
            umm = mk("umm")
            sil = mk("sil")
            hev = mk("hev")
            dmm = mk("dmm")
            dev = mk("dev")
            stf = mk("stf")

            @block.sync
            def _(sy):
                if P.load_x:
                    for i in range(KT):
                        g.xld.inc(sy.dma_start(out=xs[:, i * TC:(i + 1) * TC],
                                               in_=xT[i * 128:(i + 1) * 128, :]), 16)
                cld.inc(sy.dma_start(out=g.onesb_s, in_=cst.onesb), 16)
                cld.inc(sy.dma_start(out=g.onesf_s, in_=cst.onesf), 16)
            g.cld_total = 32

            emit_ln(block, g, xs, yb, hb)

            @block.sync
            def _(sy):
                for o in range(64):
                    if o >= 2:
                        sy.wait_ge(umm.sem, 2 * (o - 1))
                    wls[o % 2].inc(sy.dma_start(
                        out=wub[:, (o % 2) * 2 * D:(o % 2) * 2 * D + D],
                        in_=bass.AP(auS.tensor, o * 128 * D, [[D, 128], [1, D]])), 16)
                    wls[o % 2].inc(sy.dma_start(
                        out=wub[:, (o % 2) * 2 * D + D:(o % 2) * 2 * D + 2 * D],
                        in_=bass.AP(agS.tensor, o * 128 * D, [[D, 128], [1, D]])), 16)

            @block.gpsimd
            def _(gp):
                for q in range(4):
                    for dg in range(8):
                        ci = q * 8 + dg
                        if ci >= 2:
                            gp.wait_ge(dmm.sem, 4 * (ci - 1))
                        adl[ci % 2].inc(gp.dma_start(
                            out=adb[:, (ci % 2) * 16 * 256:(ci % 2 + 1) * 16 * 256],
                            in_=bass.AP(adS.tensor, (dg * 512 + q * 128) * 4096,
                                        [[4096, 128], [1, 4096]])), 16)

            @block.tensor
            def _(t):
                t.wait_ge(g.vec.sem, g.ln_done)
                ucc = 0
                dcc = 0
                for q in range(4):
                    if q >= 1:
                        t.wait_ge(dev.sem, 32 * q)
                    for oi in range(16):
                        o = q * 16 + oi
                        t.wait_ge(wls[o % 2].sem, 32 * (o // 2 + 1))
                        for ch in range(2):
                            slot = [g.psA, g.psB, g.psC][ucc % 3]
                            if ucc >= 3:
                                t.wait_ge(hev.sem, ucc - 2)
                            for kt in range(KT):
                                t.matmul(slot[:, 0:512],
                                         wub[:, (o % 2) * 2 * D + kt * 128:(o % 2) * 2 * D + (kt + 1) * 128],
                                         yb[:, kt * TC + ch * 512:kt * TC + (ch + 1) * 512],
                                         start=(kt == 0), stop=(kt == KT - 1))
                            last = None
                            for kt in range(KT):
                                last = t.matmul(
                                    slot[:, 512:1024],
                                    wub[:, (o % 2) * 2 * D + D + kt * 128:(o % 2) * 2 * D + D + (kt + 1) * 128],
                                    yb[:, kt * TC + ch * 512:kt * TC + (ch + 1) * 512],
                                    start=(kt == 0), stop=(kt == KT - 1))
                            umm.inc(last)
                            ucc += 1
                    t.wait_ge(hev.sem, 32 * (q + 1))
                    for dg in range(8):
                        ci = q * 8 + dg
                        t.wait_ge(adl[ci % 2].sem, 16 * (ci // 2 + 1))
                        for dd in range(2):
                            d = dg * 2 + dd
                            for ch in range(2):
                                pt = [g.psA[:, 0:512], g.psA[:, 512:1024],
                                      g.psB[:, 0:512], g.psB[:, 512:1024],
                                      g.psC[:, 0:512], g.psC[:, 512:1024],
                                      psD[:, 0:512], psD[:, 512:1024]][dcc % 8]
                                if dcc >= 8:
                                    t.wait_ge(dev.sem, dcc - 7)
                                last = None
                                for kt in range(16):
                                    last = t.matmul(
                                        pt,
                                        adb[:, (ci % 2) * 16 * 256 + kt * 256 + dd * 128:(ci % 2) * 16 * 256 + kt * 256 + dd * 128 + 128],
                                        hb[:, kt * TC + ch * 512:kt * TC + (ch + 1) * 512],
                                        start=(kt == 0), stop=(kt == 15))
                                dmm.inc(last)
                                dcc += 1

            @block.scalar
            def _(s):
                ucc = 0
                for q in range(4):
                    for oi in range(16):
                        for ch in range(2):
                            s.wait_ge(umm.sem, ucc + 1)
                            if ucc >= 2:
                                s.wait_ge(hev.sem, ucc - 1)
                            slot = [g.psA, g.psB, g.psC][ucc % 3]
                            sil.inc(s.activation(
                                tmpbf[:, (ucc % 2) * 512:(ucc % 2 + 1) * 512],
                                slot[:, 512:1024], AF.Silu))
                            ucc += 1

            @block.vector
            def _(v):
                ucc = 0
                dcc = 0
                for q in range(4):
                    for oi in range(16):
                        for ch in range(2):
                            v.wait_ge(sil.sem, ucc + 1)
                            slot = [g.psA, g.psB, g.psC][ucc % 3]
                            hev.inc(v.tensor_tensor(
                                hb[:, oi * TC + ch * 512:oi * TC + (ch + 1) * 512],
                                tmpbf[:, (ucc % 2) * 512:(ucc % 2 + 1) * 512],
                                slot[:, 0:512], ALU.mult))
                            ucc += 1
                    for dg in range(8):
                        for dd in range(2):
                            d = dg * 2 + dd
                            for ch in range(2):
                                v.wait_ge(dmm.sem, dcc + 1)
                                pt = [g.psA[:, 0:512], g.psA[:, 512:1024],
                                      g.psB[:, 0:512], g.psB[:, 512:1024],
                                      g.psC[:, 0:512], g.psC[:, 512:1024],
                                      psD[:, 0:512], psD[:, 512:1024]][dcc % 8]
                                dst = xs[:, d * TC + ch * 512:d * TC + (ch + 1) * 512]
                                dev.inc(v.scalar_tensor_tensor(
                                    dst, pt, 0.5, dst, ALU.mult, ALU.add))
                                dcc += 1

            if P.store_x:
                @block.sync
                def _(sy):
                    sy.wait_ge(dev.sem, 128)
                    for d in range(KT):
                        stf.inc(sy.dma_start(out=xoT[d * 128:(d + 1) * 128, :],
                                             in_=xs[:, d * TC:(d + 1) * TC]), 16)
                    sy.wait_ge(stf.sem, 16 * KT)

        _phase_close(nc, sems)


def build_swiglu():
    nc = bass.Bass("TRN2", target_bir_lowering=False)
    P = NS()
    P.tag = ""
    P.load_x = True
    P.store_x = True
    P.xT = nc.dram_tensor("xT", [D, TC], F32, kind="ExternalInput").ap()
    P.auS = nc.dram_tensor("auS", [64 * 128, D], BF16, kind="ExternalInput").ap()
    P.agS = nc.dram_tensor("agS", [64 * 128, D], BF16, kind="ExternalInput").ap()
    P.adS = nc.dram_tensor("adS", [65536, 256], BF16, kind="ExternalInput").ap()
    P.cst = _decl_consts(nc)
    P.xoT = nc.dram_tensor("xoT", [D, TC], F32, kind="ExternalOutput").ap()
    with ExitStack() as gx:
        P.xs = gx.enter_context(nc.sbuf_tensor("xs_g", [128, KT * TC], F32)).ap()
        emit_swiglu(nc, P)
    return nc



def emit_attn1(nc, P):
    wS, cosT_d, sinT_d, pt_d, mask01 = P.wS, P.cosT_d, P.sinT_d, P.ptm_d, P.mask01
    cst = P.cst
    y_src = P.y_src

    NCH = 16
    ORD = getattr(P, "ch_order", None) or list(range(NCH))
    y_gate = getattr(P, "y_gate", None)
    with ExitStack() as ex:
        sbt = lambda nm, sh, dt: ex.enter_context(nc.sbuf_tensor(nm + P.tag, sh, dt)).ap()
        pst = lambda nm, sh, dt: ex.enter_context(nc.psum_tensor(nm + P.tag, sh, dt)).ap()
        qT = [sbt("qT%d" % i, [128, T], BF16) for i in range(2)]
        kT = [sbt("kTt%d" % i, [128, T], BF16) for i in range(2)]
        vb = sbt("vb", [128, 2 * 64 * 130], BF16)
        cbuf = sbt("cbuf", [128, 2 * 1024], BF16)
        prb = sbt("prb", [128, 2 * 1024], BF16)
        ybuf = sbt("ybuf", [128, 4 * 512], BF16)
        w_s = sbt("w_s", [128, KT * 768], BF16)
        vt_tmp = sbt("vt_tmp", [128, 2 * 1024], BF16)
        rp2 = sbt("rp2", [128, 512], F32)
        mrows = sbt("mrows", [1, 160], F32)
        mbias = sbt("mbias", [128, 64], F32)
        rr = sbt("rr", [128, 512], F32)
        rp1 = rr
        sqb = prb
        rcb = cbuf[:, 0:512]
        mask_s = cbuf[:, 1024:1920]
        ptm_s = sbt("ptm_s", [128, 128], BF16)
        ones_b = sbt("ones_b", [128, 128], BF16)
        ones_f = sbt("ones_f", [128, 128], F32)
        ident_s = sbt("ident_s", [128, 128], BF16)

        psA = pst("psA", [128, 1024], F32)
        psB = pst("psB", [128, 1024], F32)
        psC = pst("psC", [128, 1024], F32)
        psD = pst("psD", [128, 512], F32)
        psV = pst("psV", [128, 512], BF16)

        sems = []
        with nc.Block() as block:
            mk = _mk_factory(nc, P.tag, sems)
            cld = mk("cld")
            wl = mk("wl", dma=True)
            yld = [mk("yld%d" % i) for i in range(4)]
            qk = mk("qk")
            qev = mk("qev")
            rpm = mk("rpm")
            rpe = mk("rpe")
            vtr = mk("vtr")
            vbe = mk("vbe")
            sqs = mk("sqs")
            qnm = mk("qnm")
            qnr = mk("qnr")
            ngm = mk("ngm")
            nsc = mk("nsc")
            mbm = mk("mbm")
            mbe = mk("mbe")
            smm = mk("smm")
            xp = mk("xp")
            mkv = mk("mkv")
            omm = mk("omm")
            rcp = mk("rcp")
            rbm = mk("rbm")
            aoev = mk("aoev")
            stf = mk("stf")

            cls = [mk("cls%d" % i, dma=True) for i in range(2)]
            mld = mk("mld", dma=True)

            @block.sync
            def _(sy):
                cld.inc(sy.dma_start(out=ones_b, in_=cst.onesb), 16)
                cld.inc(sy.dma_start(out=ones_f, in_=cst.onesf), 16)
                cld.inc(sy.dma_start(out=ident_s, in_=cst.ident), 16)
                cld.inc(sy.dma_start(out=ptm_s, in_=pt_d), 16)

            @block.gpsimd
            def _(gp):
                wl.inc(gp.dma_start(out=w_s, in_=wS), 16)
                for pc in range(NCH):
                    ch = ORD[pc]
                    if pc >= 2:
                        gp.wait_ge(rpe.sem, 4 * (pc - 1))
                    cls[pc % 2].inc(gp.dma_start(
                        out=cbuf[:, (pc % 2) * 1024:(pc % 2) * 1024 + 512],
                        in_=cosT_d[:, ch * 512:(ch + 1) * 512]), 16)
                    cls[pc % 2].inc(gp.dma_start(
                        out=cbuf[:, (pc % 2) * 1024 + 512:(pc % 2 + 1) * 1024],
                        in_=sinT_d[:, ch * 512:(ch + 1) * 512]), 16)
                gp.wait_ge(rpe.sem, 64)
                mld.inc(gp.dma_start(out=mask_s, in_=mask01), 16)

            @block.sync
            def _(sy):
                for idx in range(NCH * KT):
                    si = idx // 16
                    if idx % 16 == 0 and y_gate is not None and si in y_gate:
                        sy.wait_ge(y_gate[si][0], y_gate[si][1])
                    if idx >= 4:
                        sy.wait_ge(qk.sem, idx - 3)
                    yld[idx % 4].inc(sy.dma_start(
                        out=ybuf[:, (idx % 4) * 512:(idx % 4 + 1) * 512],
                        in_=y_src(idx % 16, ORD[si])), 16)

            RTILES = [qT[0], qT[1], kT[0], kT[1]]

            def emit_rope_mm(t, si, ch):
                for tile in range(4):
                    ri = si * 4 + tile
                    if ri >= 1:
                        t.wait_ge(rpe.sem, ri)
                    rpm.inc(t.matmul(psD[:, 0:512], ptm_s,
                                     RTILES[tile][:, ch * 512:(ch + 1) * 512],
                                     start=True, stop=True))

            def emit_vtr(t, si):
                for vt2 in range(2):
                    for k4 in range(4):
                        ti = si * 8 + vt2 * 4 + k4
                        if ti >= 1:
                            t.wait_ge(vbe.sem, ti)
                        vtr.inc(t.transpose(
                            psV[:, 0:128],
                            vt_tmp[:, (si % 2) * 1024 + vt2 * 512 + k4 * 128:(si % 2) * 1024 + vt2 * 512 + (k4 + 1) * 128],
                            ident_s))

            @block.tensor
            def _(t):
                t.wait_ge(wl.sem, 16)
                t.wait_ge(cld.sem, 64)
                for si in range(NCH):
                    if si >= 1:
                        t.wait_ge(qev.sem, 6 * si)
                    for kt in range(KT):
                        idx = si * 16 + kt
                        t.wait_ge(yld[idx % 4].sem, 16 * (idx // 4 + 1))
                        rhs = ybuf[:, (idx % 4) * 512:(idx % 4 + 1) * 512]
                        st = (kt == 0)
                        sp = (kt == KT - 1)
                        dsts = [psA[:, 0:512], psA[:, 512:1024],
                                psB[:, 0:512], psB[:, 512:1024],
                                psC[:, 0:512], psC[:, 512:1024]]
                        for f in range(6):
                            mm = t.matmul(dsts[f],
                                          w_s[:, kt * 768 + f * 128:kt * 768 + (f + 1) * 128],
                                          rhs, start=st, stop=sp)
                        qk.inc(mm)
                    if si >= 1:
                        emit_rope_mm(t, si - 1, ORD[si - 1])
                        emit_vtr(t, si - 1)
                t.wait_ge(qev.sem, 6 * NCH)
                emit_rope_mm(t, NCH - 1, ORD[NCH - 1])
                emit_vtr(t, NCH - 1)

            @block.scalar
            def _(s):
                for si in range(NCH):
                    ch = ORD[si]
                    s.wait_ge(qk.sem, 16 * (si + 1))
                    srcs = [psA[:, 0:512], psA[:, 512:1024],
                            psB[:, 0:512], psB[:, 512:1024],
                            psC[:, 0:512], psC[:, 512:1024]]
                    qev.inc(s.activation(qT[0][:, ch * 512:(ch + 1) * 512], srcs[0],
                                         AF.Copy, scale=0.125))
                    qev.inc(s.activation(qT[1][:, ch * 512:(ch + 1) * 512], srcs[1],
                                         AF.Copy, scale=0.125))
                    qev.inc(s.activation(kT[0][:, ch * 512:(ch + 1) * 512], srcs[2],
                                         AF.Copy))
                    qev.inc(s.activation(kT[1][:, ch * 512:(ch + 1) * 512], srcs[3],
                                         AF.Copy))
                    if si >= 2:
                        s.wait_ge(vtr.sem, 8 * (si - 1))
                    qev.inc(s.activation(vt_tmp[:, (si % 2) * 1024:(si % 2) * 1024 + 512],
                                         srcs[4], AF.Copy))
                    qev.inc(s.activation(
                        vt_tmp[:, (si % 2) * 1024 + 512:(si % 2 + 1) * 1024],
                        srcs[5], AF.Copy))

            @block.vector
            def _(v):
                v.wait_ge(cld.sem, 64)
                for pc in range(NCH):
                    ch = ORD[pc]
                    v.wait_ge(cls[pc % 2].sem, 32 * (pc // 2 + 1))
                    for tile in range(4):
                        ri = pc * 4 + tile
                        v.wait_ge(rpm.sem, ri + 1)
                        src = RTILES[tile][:, ch * 512:(ch + 1) * 512]
                        v.tensor_tensor(rp1, src, cbuf[:, (pc % 2) * 1024:(pc % 2) * 1024 + 512],
                                        ALU.mult)
                        v.tensor_tensor(rp2, psD[:, 0:512],
                                        cbuf[:, (pc % 2) * 1024 + 512:(pc % 2 + 1) * 1024], ALU.mult)
                        v.drain()
                        rpe.inc(v.tensor_tensor(src, rp1, rp2, ALU.add))
                        v.drain()
                    for vt2 in range(2):
                        for k4 in range(4):
                            ti = pc * 8 + vt2 * 4 + k4
                            v.wait_ge(vtr.sem, ti + 1)
                            ktg = ch * 4 + k4
                            base = (vt2 * 64 + ktg) * 130
                            v.tensor_copy(vb[:, base:base + 64], psV[:, 0:64])
                            vbe.inc(v.tensor_copy(vb[:, base + 65:base + 129],
                                                  psV[:, 64:128]))

            @block.vector
            def _(v):
                v.wait_ge(cld.sem, 64)
                v.memset(bass.AP(vb.tensor, 64, [[2 * 64 * 130, 128], [130, 128], [1, 1]]), 1.0)
                mbe.inc(v.memset(bass.AP(vb.tensor, 129,
                                         [[2 * 64 * 130, 128], [130, 128], [1, 1]]), 1.0))

            @block.scalar
            def _(s):
                for ci in range(128):
                    h, ch = divmod(ci % 64, NCH)
                    isq = ci < 64
                    if ci == 0:
                        s.wait_ge(rpe.sem, 64)
                    if ci >= 2:
                        s.wait_ge(qnm.sem, ci - 1)
                    src = (qT if isq else kT)[h // 2]
                    base = 64 * (h % 2)
                    sqs.inc(s.activation(
                        sqb[(0 if base == 0 else 64):(64 if base == 0 else 128),
                            (ci % 2) * 512:(ci % 2 + 1) * 512],
                        src[base:base + 64, ch * 512:(ch + 1) * 512], AF.Square))

            @block.tensor
            def _(t):
                t.wait_ge(rpe.sem, 64)
                for ci in range(128):
                    h = (ci % 64) // NCH
                    base = 64 * (h % 2)
                    t.wait_ge(sqs.sem, ci + 1)
                    if ci >= 1:
                        t.wait_ge(qnr.sem, ci)
                    qnm.inc(t.matmul(psD[0:1, 0:512], ones_b[base:base + 64, 0:1],
                                     sqb[(0 if base == 0 else 64):(64 if base == 0 else 128),
                                         (ci % 2) * 512:(ci % 2 + 1) * 512],
                                     start=True, stop=True))

            @block.vector
            def _(v):
                for ci in range(128):
                    v.wait_ge(qnm.sem, ci + 1)
                    qnr.inc(v.tensor_reduce(mrows[0:1, ci:ci + 1], psD[0:1, 0:512],
                                            axis=mybir.AxisListType.X, op=ALU.max))
                v.drain()
                for h in range(4):
                    v.tensor_reduce(mrows[0:1, 128 + h:129 + h],
                                    mrows[0:1, 64 + 16 * h:64 + 16 * (h + 1)],
                                    axis=mybir.AxisListType.X, op=ALU.max)
                v.drain()
                last = None
                for h in range(4):
                    last = v.tensor_scalar(mrows[0:1, 16 * h:16 * (h + 1)],
                                           mrows[0:1, 16 * h:16 * (h + 1)],
                                           mrows[0:1, 128 + h:129 + h], None,
                                           ALU.mult)
                ngm.inc(last)

            @block.scalar
            def _(s):
                s.wait_ge(ngm.sem, 1)
                s.activation(mrows[0:1, 0:64], mrows[0:1, 0:64], AF.Ln)
                s.drain()
                nsc.inc(s.activation(mrows[0:1, 0:64], mrows[0:1, 0:64],
                                     AF.Exp, scale=0.5))

            @block.vector
            def _(v):
                v.wait_ge(nsc.sem, 1)
                ngm.inc(v.tensor_scalar(mrows[0:1, 0:64], mrows[0:1, 0:64],
                                        -1.0, None, ALU.mult))

            @block.tensor
            def _(t):
                t.wait_ge(ngm.sem, 2)
                mbm.inc(t.matmul(psD[:, 0:64], ones_f[0:1, :], mrows[0:1, 0:64],
                                 start=True, stop=True))

            @block.vector
            def _(v):
                v.wait_ge(mbm.sem, 1)
                mbe.inc(v.tensor_copy(mbias, psD[:, 0:64]))

            @block.tensor
            def _(t):
                t.wait_ge(mbe.sem, 2)
                t.wait_ge(vbe.sem, 128)
                kgi = 0
                cc = 0
                pend = [None]
                for h in range(4):
                    vt2 = h // 2
                    base = 64 * (h % 2)
                    for blk in range(4):
                        for cl in range(4):
                            ch = blk * 4 + cl
                            ng = 2 * (cl + 1)

                            def emit_out1(kg, kgl, ng=ng, cc2=cc, blk=blk,
                                          vt2=vt2, h=h):
                                if kg == 0 and cc2 >= 2:
                                    t.wait_ge(aoev.sem, cc2 - 1)
                                t.wait_ge(xp.sem, kgl + 1)
                                if kg >= ng - 2:
                                    t.wait_ge(mkv.sem, 4 * cc2 + 2 * (kg - (ng - 2) + 1))
                                st = (kg == 0)
                                sp = (kg == ng - 1)
                                for j in range(2):
                                    ktl = kg * 2 + j
                                    ktg = blk * 16 + ktl
                                    vbase = (vt2 * 64 + ktg) * 130 + (h % 2) * 65
                                    mm = t.matmul(
                                        psC[0:65, (cc2 % 2) * 512:(cc2 % 2) * 512 + 512],
                                        vb[:, vbase:vbase + 65],
                                        prb[:, (kgl % 2) * 1024 + j * 512:(kgl % 2) * 1024 + (j + 1) * 512],
                                        start=(st and j == 0), stop=(sp and j == 1))
                                omm.inc(mm)

                            for kg in range(ng):
                                pgrp = [psA, psB][kgi % 2]
                                if kgi >= 2:
                                    t.wait_ge(xp.sem, kgi - 1)
                                for j in range(2):
                                    ktl = kg * 2 + j
                                    ktg = blk * 16 + ktl
                                    mm = t.matmul(
                                        pgrp[:, j * 512:(j + 1) * 512],
                                        kT[vt2][base:base + 64, ktg * 128:(ktg + 1) * 128],
                                        qT[vt2][base:base + 64, ch * 512:(ch + 1) * 512],
                                        start=True, stop=True)
                                smm.inc(mm)
                                if kg == 0 and pend[0] is not None:
                                    t.wait_ge(rcp.sem, pend[0] + 1)
                                    rbm.inc(t.matmul(psD[0:64, 0:512],
                                                     ones_f[64:65, 0:64],
                                                     rr[64:65, 0:512],
                                                     start=True, stop=True))
                                    pend[0] = None
                                if kg >= 1:
                                    emit_out1(kg - 1, kgi - 1)
                                kgi += 1
                            emit_out1(ng - 1, kgi - 1)
                            pend[0] = cc
                            cc += 1
                t.wait_ge(rcp.sem, pend[0] + 1)
                rbm.inc(t.matmul(psD[0:64, 0:512], ones_f[64:65, 0:64],
                                 rr[64:65, 0:512], start=True, stop=True))

            @block.scalar
            def _(s):
                s.wait_ge(mbe.sem, 2)
                kgi = 0
                for h in range(4):
                    for blk in range(4):
                        for cl in range(4):
                            ci = h * 16 + blk * 4 + cl
                            ng = 2 * (cl + 1)
                            for kg in range(ng):
                                s.wait_ge(smm.sem, kgi + 1)
                                if kgi >= 2:
                                    s.wait_ge(omm.sem, kgi - 1)
                                pgrp = [psA, psB][kgi % 2]
                                xp.inc(s.activation(
                                    prb[:, (kgi % 2) * 1024:(kgi % 2 + 1) * 1024],
                                    pgrp[:, 0:1024], AF.Exp,
                                    bias=mbias[:, ci:ci + 1]))
                                kgi += 1

            @block.vector
            def _(v):
                kgi = 0
                cc = 0
                for h in range(4):
                    vt2 = h // 2
                    base = 64 * (h % 2)
                    for blk in range(4):
                        for cl in range(4):
                            ch = blk * 4 + cl
                            ng = 2 * (cl + 1)
                            for kg in range(ng):
                                if kg >= ng - 2:
                                    v.wait_ge(mld.sem, 16)
                                    v.wait_ge(xp.sem, kgi + 1)
                                    for j in range(2):
                                        ktl = kg * 2 + j
                                        dlt = ktl * 128 - cl * 512
                                        sl = 384 - dlt
                                        mkv.inc(v.tensor_tensor(
                                            prb[:, (kgi % 2) * 1024 + j * 512:(kgi % 2) * 1024 + (j + 1) * 512],
                                            prb[:, (kgi % 2) * 1024 + j * 512:(kgi % 2) * 1024 + (j + 1) * 512],
                                            mask_s[:, sl:sl + 512], ALU.mult))
                                kgi += 1
                            v.wait_ge(omm.sem, kgi)
                            rcp.inc(v.reciprocal(rr[64:65, 0:512],
                                                 psC[64:65, (cc % 2) * 512:(cc % 2) * 512 + 512]))
                            v.wait_ge(rbm.sem, cc + 1)
                            v.tensor_copy(rcb[0:64, :], psD[0:64, 0:512])
                            v.drain()
                            aoev.inc(v.tensor_tensor(
                                qT[vt2][base:base + 64, ch * 512:(ch + 1) * 512],
                                psC[0:64, (cc % 2) * 512:(cc % 2) * 512 + 512],
                                rcb[0:64, :], ALU.mult))
                            cc += 1

            @block.sync
            def _(sy):
                sy.wait_ge(aoev.sem, 64)
                if P.ao_ext is not None:
                    for vt2 in range(2):
                        stf.inc(sy.dma_start(out=P.ao_ext[vt2 * 128:(vt2 + 1) * 128, :],
                                             in_=qT[vt2]), 16)
                    sy.wait_ge(stf.sem, 32)
                else:
                    n = 0
                    for r in range(NC):
                        for vt2 in range(2):
                            stf.inc(sy.dma_start(
                                out=P.ao_in[256 * r + 128 * vt2:256 * r + 128 * (vt2 + 1), :],
                                in_=qT[vt2][:, r * TC:(r + 1) * TC]), 16)
                            n += 16
                    sy.wait_ge(stf.sem, n)

        _phase_close(nc, sems)


def build_attn1():
    nc = bass.Bass("TRN2", target_bir_lowering=False)
    P = NS()
    P.tag = ""
    hT = nc.dram_tensor("hT", [D, T], BF16, kind="ExternalInput").ap()
    P.y_src = lambda kt, ch: hT[kt * 128:(kt + 1) * 128, ch * 512:(ch + 1) * 512]
    P.wS = nc.dram_tensor("wS", [128, KT * 768], F32, kind="ExternalInput").ap()
    P.cosT_d = nc.dram_tensor("cosT", [128, T], BF16, kind="ExternalInput").ap()
    P.sinT_d = nc.dram_tensor("sinT", [128, T], BF16, kind="ExternalInput").ap()
    P.ptm_d = nc.dram_tensor("ptm", [128, 128], BF16, kind="ExternalInput").ap()
    P.mask01 = nc.dram_tensor("mask01", [128, 896], BF16, kind="ExternalInput").ap()
    P.cst = _decl_consts(nc)
    P.ao_ext = nc.dram_tensor("aoT", [256, T], BF16, kind="ExternalOutput").ap()
    P.ao_in = None
    emit_attn1(nc, P)
    return nc


def attn1_prep(inp):
    wqkv = _f32(inp["L1_attn_w_qkv"])
    wT = np.ascontiguousarray(wqkv.T)

    def tile_in_out(w):
        Fd = w.shape[1]
        return np.ascontiguousarray(
            w.reshape(KT, 128, Fd).transpose(1, 0, 2).reshape(128, KT * Fd))

    pre = {"wS": [None] * NC}
    for c in range(NC):
        cols = np.concatenate([wT[:, 256 * c:256 * (c + 1)],
                               wT[:, 2048 + 256 * c:2048 + 256 * (c + 1)],
                               wT[:, 4096 + 256 * c:4096 + 256 * (c + 1)]], axis=1)
        pre["wS"][c] = _f32(tile_in_out(cols))
    dd = np.arange(64)
    inv = 1.0 / (10000.0 ** (np.concatenate([dd[:32], dd[:32]]) * 2.0 / 64.0))
    tpos = np.arange(T)
    ang = tpos[None, :] * inv[:, None]
    cos2 = np.concatenate([np.cos(ang), np.cos(ang)], axis=0)
    sin_half = np.sin(ang)
    sin2 = np.concatenate([sin_half, sin_half], axis=0)
    pre["cosT"] = _bf(cos2)
    pre["sinT"] = _bf(sin2)
    P = np.zeros((128, 128), dtype=np.float32)
    for hh in range(2):
        b = hh * 64
        for i in range(32):
            P[b + i, b + 32 + i] = -1.0
            P[b + 32 + i, b + i] = 1.0
    pre["ptm"] = _bf(P.T)
    k = np.arange(128)[:, None]
    u = np.arange(896)[None, :]
    pre["mask01"] = _bf((k <= u - 384).astype(np.float32))
    return pre



def emit_post1(nc, P):
    yT_c, aT, woT, wn_c, wupS, wdnT = P.yT_c, P.aT, P.woT, P.wn_c, P.wupS, P.wdnT
    cst = P.cst
    xs = P.xs
    xT = P.xT if P.load_x else None
    xoT = P.xoT if P.store_x else None

    with ExitStack() as ex:
        sbt = lambda nm, sh, dt: ex.enter_context(nc.sbuf_tensor(nm + P.tag, sh, dt)).ap()
        pst = lambda nm, sh, dt: ex.enter_context(nc.psum_tensor(nm + P.tag, sh, dt)).ap()
        tb = sbt("tb", [128, KT * TC], BF16)
        ab = sbt("ab", [128, KT * TC], BF16)
        hb = sbt("hb", [128, KT * TC], BF16)
        wob = sbt("wob", [128, 4 * 512], BF16)
        wub = sbt("wub", [128, 4 * D], BF16)
        adb = sbt("adb", [128, 2 * 16 * 256], BF16)
        tmpbf = sbt("tmpbf", [128, 2 * 512], BF16)
        wnc = sbt("wnc", [128, KT], F32)
        rows = sbt("rows", [1, 2 * TC], F32)
        ones_b = sbt("ones_b", [128, 128], BF16)
        ones_f = sbt("ones_f", [128, 128], F32)
        psA = pst("psA", [128, 1024], F32)
        psB = pst("psB", [128, 1024], F32)
        psC = pst("psC", [128, 1024], F32)
        psD = pst("psD", [128, 1024], F32)
        PB8 = [lambda a=a, c=c: [psA, psB, psC, psD][a][:, c * 512:(c + 1) * 512]
               for a in range(4) for c in range(2)]

        sems = []
        with nc.Block() as block:
            mk = _mk_factory(nc, P.tag, sems)
            xld = mk("xld") if P.load_x else None
            yld = mk("yld")
            ald = mk("ald")
            cld = mk("cld")
            wol = [mk("wol%d" % i, dma=True) for i in range(4)]
            qmm = mk("qmm")
            pev = mk("pev")
            rsq = mk("rsq")
            rmm = mk("rmm")
            rrw = mk("rrw")
            rsc = mk("rsc")
            rbc = mk("rbc")
            rhv = mk("rhv")
            wul = [mk("wul%d" % i, dma=True) for i in range(2)]
            umm = mk("umm")
            sil = mk("sil")
            hev = mk("hev")
            adl = [mk("adl%d" % i, dma=True) for i in range(2)]
            dmm = mk("dmm")
            dev = mk("dev")
            stf = mk("stf")

            @block.sync
            def _(sy):
                if P.load_x:
                    for i in range(KT):
                        xld.inc(sy.dma_start(out=xs[:, i * TC:(i + 1) * TC],
                                             in_=xT[i * 128:(i + 1) * 128, :]), 16)
                for i in range(KT):
                    yld.inc(sy.dma_start(out=tb[:, i * TC:(i + 1) * TC],
                                         in_=yT_c[i * 128:(i + 1) * 128, :]), 16)
                for i in range(KT):
                    ald.inc(sy.dma_start(out=ab[:, i * TC:(i + 1) * TC],
                                         in_=aT[i * 128:(i + 1) * 128, :]), 16)
                cld.inc(sy.dma_start(out=ones_b, in_=cst.onesb), 16)
                cld.inc(sy.dma_start(out=ones_f, in_=cst.onesf), 16)
                cld.inc(sy.dma_start(out=wnc, in_=wn_c), 16)

            @block.gpsimd
            def _(gp):
                for idx in range(64):
                    dg, kt = divmod(idx, 16)
                    if idx >= 4:
                        gp.wait_ge(qmm.sem, idx - 3)
                    wol[idx % 4].inc(gp.dma_start(
                        out=wob[:, (idx % 4) * 512:(idx % 4 + 1) * 512],
                        in_=bass.AP(woT.tensor, kt * 128 * D + dg * 512,
                                    [[D, 128], [1, 512]])), 16)

            @block.tensor
            def _(t):
                t.wait_ge(ald.sem, 16 * KT)
                for dg in range(4):
                    if dg >= 1:
                        t.wait_ge(pev.sem, 8 * dg)
                    for kt in range(16):
                        idx = dg * 16 + kt
                        t.wait_ge(wol[idx % 4].sem, 16 * (idx // 4 + 1))
                        mm = None
                        for dd in range(4):
                            for ch in range(2):
                                mm = t.matmul(
                                    PB8[dd * 2 + ch](),
                                    wob[:, (idx % 4) * 512 + dd * 128:(idx % 4) * 512 + (dd + 1) * 128],
                                    ab[:, kt * TC + ch * 512:kt * TC + (ch + 1) * 512],
                                    start=(kt == 0), stop=(kt == 15))
                        qmm.inc(mm)

            @block.vector
            def _(v):
                if P.load_x:
                    v.wait_ge(xld.sem, 16 * KT)
                v.wait_ge(yld.sem, 16 * KT)
                for dg in range(4):
                    v.wait_ge(qmm.sem, 16 * (dg + 1))
                    for dd in range(4):
                        for ch in range(2):
                            d = dg * 4 + dd
                            tt = tb[:, d * TC + ch * 512:d * TC + (ch + 1) * 512]
                            v.tensor_tensor(tt, PB8[dd * 2 + ch](), tt, ALU.add)
                            v.drain()
                            dst = xs[:, d * TC + ch * 512:d * TC + (ch + 1) * 512]
                            pev.inc(v.tensor_tensor(dst, dst, tt, ALU.add))

            @block.vector
            def _(v):
                v.wait_ge(pev.sem, 32)
                last = None
                for i in range(KT):
                    last = v.tensor_tensor(hb[:, i * TC:(i + 1) * TC],
                                           tb[:, i * TC:(i + 1) * TC],
                                           tb[:, i * TC:(i + 1) * TC], ALU.mult)
                rsq.inc(last)

            @block.tensor
            def _(t):
                t.wait_ge(cld.sem, 48)
                t.wait_ge(rsq.sem, 1)
                last = None
                for i in range(KT):
                    for ch in range(2):
                        last = t.matmul(psA[0:1, ch * 512:(ch + 1) * 512],
                                        ones_b[:, 0:1],
                                        hb[:, i * TC + ch * 512:i * TC + (ch + 1) * 512],
                                        start=(i == 0), stop=(i == KT - 1))
                rmm.inc(last)

            @block.vector
            def _(v):
                v.wait_ge(rmm.sem, 1)
                var = rows[0:1, 0:TC]
                v.tensor_scalar(var, psA[0:1, 0:TC], 1.0 / D, None, ALU.mult)
                v.drain()
                rrw.inc(v.tensor_scalar(var, var, float(LNEPS), None, ALU.add))

            @block.scalar
            def _(s):
                s.wait_ge(rrw.sem, 1)
                var = rows[0:1, 0:TC]
                lnv = rows[0:1, TC:2 * TC]
                s.activation(lnv, var, AF.Ln)
                s.drain()
                rsc.inc(s.activation(var, lnv, AF.Exp, scale=-0.5))

            @block.tensor
            def _(t):
                t.wait_ge(rsc.sem, 1)
                last = None
                for ch in range(2):
                    last = t.matmul(psB[:, ch * 512:(ch + 1) * 512], ones_f[0:1, :],
                                    rows[0:1, ch * 512:(ch + 1) * 512],
                                    start=True, stop=True)
                rbc.inc(last)

            @block.vector
            def _(v):
                v.wait_ge(rbc.sem, 1)
                last = None
                for i in range(KT):
                    last = v.scalar_tensor_tensor(ab[:, i * TC:(i + 1) * TC],
                                                  tb[:, i * TC:(i + 1) * TC],
                                                  wnc[:, i:i + 1], psB[:, 0:TC],
                                                  ALU.mult, ALU.mult)
                rhv.inc(last)

            @block.gpsimd
            def _(gp):
                for hi in range(2):
                    for oi in range(16):
                        gi = hi * 16 + oi
                        if gi >= 2:
                            gp.wait_ge(umm.sem, 2 * (gi - 1))
                        og = hi * 16 + oi
                        ov = 32 + hi * 16 + oi
                        wul[gi % 2].inc(gp.dma_start(
                            out=wub[:, (gi % 2) * 2 * D:(gi % 2) * 2 * D + D],
                            in_=bass.AP(wupS.tensor, og * 128 * D, [[D, 128], [1, D]])), 16)
                        wul[gi % 2].inc(gp.dma_start(
                            out=wub[:, (gi % 2) * 2 * D + D:(gi % 2) * 2 * D + 2 * D],
                            in_=bass.AP(wupS.tensor, ov * 128 * D, [[D, 128], [1, D]])), 16)
                    for ci in range(8):
                        gi2 = hi * 8 + ci
                        if gi2 >= 2:
                            gp.wait_ge(dmm.sem, 4 * (gi2 - 1))
                        adl[gi2 % 2].inc(gp.dma_start(
                            out=adb[:, (gi2 % 2) * 16 * 256:(gi2 % 2 + 1) * 16 * 256],
                            in_=bass.AP(wdnT.tensor, gi2 * 128 * 4096,
                                        [[4096, 128], [1, 4096]])), 16)

            @block.tensor
            def _(t):
                t.wait_ge(rhv.sem, 1)
                ucc = 0
                dcc = 0
                for hi in range(2):
                    if hi >= 1:
                        t.wait_ge(dev.sem, 32 * hi)
                    for oi in range(16):
                        gi = hi * 16 + oi
                        t.wait_ge(wul[gi % 2].sem, 32 * (gi // 2 + 1))
                        for ch in range(2):
                            slot = [psA, psB, psC][ucc % 3]
                            if ucc >= 3:
                                t.wait_ge(hev.sem, ucc - 2)
                            for kt in range(KT):
                                t.matmul(slot[:, 0:512],
                                         wub[:, (gi % 2) * 2 * D + kt * 128:(gi % 2) * 2 * D + (kt + 1) * 128],
                                         ab[:, kt * TC + ch * 512:kt * TC + (ch + 1) * 512],
                                         start=(kt == 0), stop=(kt == KT - 1))
                            last = None
                            for kt in range(KT):
                                last = t.matmul(
                                    slot[:, 512:1024],
                                    wub[:, (gi % 2) * 2 * D + D + kt * 128:(gi % 2) * 2 * D + D + (kt + 1) * 128],
                                    ab[:, kt * TC + ch * 512:kt * TC + (ch + 1) * 512],
                                    start=(kt == 0), stop=(kt == KT - 1))
                            umm.inc(last)
                            ucc += 1
                    t.wait_ge(hev.sem, 32 * (hi + 1))
                    for ci in range(8):
                        gi2 = hi * 8 + ci
                        t.wait_ge(adl[gi2 % 2].sem, 16 * (gi2 // 2 + 1))
                        for dd in range(2):
                            d = ci * 2 + dd
                            for ch in range(2):
                                pt = PB8[dcc % 8]()
                                if dcc >= 8:
                                    t.wait_ge(dev.sem, dcc - 7)
                                last = None
                                for kt in range(16):
                                    last = t.matmul(
                                        pt,
                                        adb[:, (gi2 % 2) * 16 * 256 + kt * 256 + dd * 128:(gi2 % 2) * 16 * 256 + kt * 256 + (dd + 1) * 128],
                                        hb[:, kt * TC + ch * 512:kt * TC + (ch + 1) * 512],
                                        start=(kt == 0), stop=(kt == 15))
                                dmm.inc(last)
                                dcc += 1

            @block.scalar
            def _(s):
                ucc = 0
                for hi in range(2):
                    for oi in range(16):
                        for ch in range(2):
                            s.wait_ge(umm.sem, ucc + 1)
                            if ucc >= 2:
                                s.wait_ge(hev.sem, ucc - 1)
                            slot = [psA, psB, psC][ucc % 3]
                            sil.inc(s.activation(
                                tmpbf[:, (ucc % 2) * 512:(ucc % 2 + 1) * 512],
                                slot[:, 0:512], AF.Silu))
                            ucc += 1

            @block.vector
            def _(v):
                ucc = 0
                dcc = 0
                for hi in range(2):
                    for oi in range(16):
                        for ch in range(2):
                            v.wait_ge(sil.sem, ucc + 1)
                            slot = [psA, psB, psC][ucc % 3]
                            hev.inc(v.tensor_tensor(
                                hb[:, oi * TC + ch * 512:oi * TC + (ch + 1) * 512],
                                tmpbf[:, (ucc % 2) * 512:(ucc % 2 + 1) * 512],
                                slot[:, 512:1024], ALU.mult))
                            ucc += 1
                    for ci in range(8):
                        for dd in range(2):
                            d = ci * 2 + dd
                            for ch in range(2):
                                v.wait_ge(dmm.sem, dcc + 1)
                                pt = PB8[dcc % 8]()
                                dst = xs[:, d * TC + ch * 512:d * TC + (ch + 1) * 512]
                                dev.inc(v.tensor_tensor(dst, dst, pt, ALU.add))
                                dcc += 1

            if P.store_x:
                @block.sync
                def _(sy):
                    sy.wait_ge(dev.sem, 64)
                    for d in range(KT):
                        stf.inc(sy.dma_start(out=xoT[d * 128:(d + 1) * 128, :],
                                             in_=xs[:, d * TC:(d + 1) * TC]), 16)
                    sy.wait_ge(stf.sem, 16 * KT)

        _phase_close(nc, sems)


def build_post1():
    nc = bass.Bass("TRN2", target_bir_lowering=False)
    P = NS()
    P.tag = ""
    P.load_x = True
    P.store_x = True
    P.xT = nc.dram_tensor("xT", [D, TC], F32, kind="ExternalInput").ap()
    P.yT_c = nc.dram_tensor("yT_c", [D, TC], BF16, kind="ExternalInput").ap()
    P.aT = nc.dram_tensor("aT", [D, TC], BF16, kind="ExternalInput").ap()
    P.woT = nc.dram_tensor("woT", [D, D], F32, kind="ExternalInput").ap()
    P.wn_c = nc.dram_tensor("wn_c", [128, KT], F32, kind="ExternalInput").ap()
    P.wupS = nc.dram_tensor("wupS", [64 * 128, D], F32, kind="ExternalInput").ap()
    P.wdnT = nc.dram_tensor("wdnT", [2048, 4096], F32, kind="ExternalInput").ap()
    P.cst = _decl_consts(nc)
    P.xoT = nc.dram_tensor("xoT", [D, TC], F32, kind="ExternalOutput").ap()
    with ExitStack() as gx:
        P.xs = gx.enter_context(nc.sbuf_tensor("xs_g", [128, KT * TC], F32)).ap()
        emit_post1(nc, P)
    return nc


_BUILD_CACHE = {}


def _get(name, fn):
    if name not in _BUILD_CACHE:
        _BUILD_CACHE[name] = fn()
    return _BUILD_CACHE[name]


def _run(name, fn, in_maps):
    nc = _get(name, fn)
    return run_bass_kernel_spmd(nc, in_maps, core_ids=list(range(NC)))


def _hadamard(n):
    H = np.array([[1.0]], dtype=np.float32)
    while H.shape[0] < n:
        H = np.block([[H, H], [H, -H]])
    return H.astype(np.float32)


def _cols(v):
    return _f32(np.asarray(v, dtype=np.float32).reshape(KT, 128).T)


def _slabify_up(A):
    out = np.empty((64, 128, D), dtype=A.dtype)
    a = A.reshape(KT, 128, 8192)
    for o in range(64):
        out[o] = a[:, :, o * 128:(o + 1) * 128].transpose(1, 0, 2).reshape(128, D)
    return np.ascontiguousarray(out.reshape(64 * 128, D))



def build_mega(upto=99):
    nc = bass.Bass("TRN2", target_bir_lowering=False)
    cst = _decl_consts(nc)
    xT0 = nc.dram_tensor("xT", [D, TC], F32, kind="ExternalInput").ap()
    xoT = nc.dram_tensor("xoT", [D, TC], F32, kind="ExternalOutput").ap()

    W = NS()
    W.tag = "_w"
    _moe_drams(nc, W)
    _fold_drams(nc, W)
    W.wqS = nc.dram_tensor("wqS", [128, KT * 256], F32, kind="ExternalInput").ap()
    W.wkvS = nc.dram_tensor("wkvS", [128, KT * 256], F32, kind="ExternalInput").ap()
    W.bq_c = nc.dram_tensor("bq_c", [128, 2], F32, kind="ExternalInput").ap()
    W.bkv_c = nc.dram_tensor("bkv_c", [128, 2], F32, kind="ExternalInput").ap()
    W.mask01 = nc.dram_tensor("mask01", [128, 896], BF16, kind="ExternalInput").ap()
    W.wf = nc.dram_tensor("wf", [D, D], F32, kind="ExternalInput").ap()
    W.bf_c = nc.dram_tensor("bf_c", [128, KT], F32, kind="ExternalInput").ap()
    W.wa_c = nc.dram_tensor("wa_c", [128, KT], F32, kind="ExternalInput").ap()
    W.wS = nc.dram_tensor("wS", [128, KT * 768], F32, kind="ExternalInput").ap()
    W.cosT_d = nc.dram_tensor("cosT", [128, T], BF16, kind="ExternalInput").ap()
    W.sinT_d = nc.dram_tensor("sinT", [128, T], BF16, kind="ExternalInput").ap()
    W.ptm_d = nc.dram_tensor("ptm", [128, 128], BF16, kind="ExternalInput").ap()
    W.woT = nc.dram_tensor("woT", [D, D], F32, kind="ExternalInput").ap()
    W.wn_c = nc.dram_tensor("wn_c", [128, KT], F32, kind="ExternalInput").ap()
    W.wupS = nc.dram_tensor("wupS", [64 * 128, D], F32, kind="ExternalInput").ap()
    W.wdnT = nc.dram_tensor("wdnT", [2048, 4096], F32, kind="ExternalInput").ap()

    a_up_sl = nc.dram_tensor("a_up_sl", [8 * 128, D], BF16).ap()
    a_gt_sl = nc.dram_tensor("a_gt_sl", [8 * 128, D], BF16).ap()
    a_dn_sl = nc.dram_tensor("a_dn_sl", [4 * 128, 4096], BF16).ap()
    a_up_f = nc.dram_tensor("a_up_f", [64 * 128, D], BF16, addr_space="Shared").ap()
    a_gt_f = nc.dram_tensor("a_gt_f", [64 * 128, D], BF16, addr_space="Shared").ap()
    a_dn_f = nc.dram_tensor("a_dn_f", [32 * 128, 4096], BF16, addr_space="Shared").ap()
    y1slA = nc.dram_tensor("y1slA", [D, 512], BF16).ap()
    y1slB = nc.dram_tensor("y1slB", [D, 512], BF16).ap()
    y1fA = nc.dram_tensor("y1fA", [NC * D, 512], BF16, addr_space="Shared").ap()
    y1fB = nc.dram_tensor("y1fB", [NC * D, 512], BF16, addr_space="Shared").ap()
    h4slA = nc.dram_tensor("h4slA", [D, 512], BF16).ap()
    h4slB = nc.dram_tensor("h4slB", [D, 512], BF16).ap()
    h4fA = nc.dram_tensor("h4fA", [NC * D, 512], BF16, addr_space="Shared").ap()
    h4fB = nc.dram_tensor("h4fB", [NC * D, 512], BF16, addr_space="Shared").ap()
    y4sl = nc.dram_tensor("y4sl", [D, TC], BF16).ap()
    ao0in = nc.dram_tensor("ao0in", [D, TC], BF16).ap()
    ao0out = nc.dram_tensor("ao0out", [D, TC], BF16).ap()
    ao1in = nc.dram_tensor("ao1in", [D, TC], BF16).ap()
    ao1out = nc.dram_tensor("ao1out", [D, TC], BF16).ap()

    with ExitStack() as gx:
        _xs = [None]
        cc = Sem(nc.alloc_semaphore(name="cc"))

        def coll(kind, in_ap, out_ap, a2a=False, wait=True):
            with nc.Block() as blk:
                @blk.gpsimd
                def _(gp):
                    ctx = allow_ici_all_to_all() if a2a else nullcontext()
                    with ctx:
                        op = gp.collective_compute(
                            kind, ALU.bypass,
                            replica_groups=[list(range(NC))],
                            ins=[in_ap], outs=[out_ap])
                    cc.inc(op, 1)
                    if wait:
                        gp.wait_ge(cc.sem, cc.n)

        def coll_wait():
            with nc.Block() as blk:
                @blk.gpsimd
                def _(gp):
                    gp.wait_ge(cc.sem, cc.n)

        def mkP(tag, **kw):
            P = NS()
            P.__dict__.update(W.__dict__)
            P.tag = tag
            P.cst = cst
            P.xs = _xs[0]
            P.xT = None
            P.xoT = None
            P.load_x = False
            P.store_x = False
            P.__dict__.update(kw)
            return P

        _xs[0] = gx.enter_context(nc.sbuf_tensor("xs_g", [128, KT * TC], F32)).ap()
        xldf = [None]
        if upto >= 1:
            with nc.named_scope("ph_fold"):
                Pf = mkP("_f", a_up=a_up_sl, a_gt=a_gt_sl, a_dn=a_dn_sl,
                         xT_pre=xT0)
                emit_fold(nc, Pf)
                xldf[0] = Pf.xld_out
        import os as _os
        if upto >= 2:
            _nag = int(_os.environ.get("MEGA_NAG", "3"))
            _defer = _os.environ.get("MEGA_DEFER_AG", "1") == "1"
            with nc.named_scope("ph_agA"):
                if _nag >= 1:
                    coll("AllGather", a_up_sl, a_up_f, wait=not _defer)
                if _nag >= 2:
                    coll("AllGather", a_gt_sl, a_gt_f, wait=not _defer)
                if _nag >= 3:
                    coll("AllGather", a_dn_sl, a_dn_f, wait=not _defer)

        if upto >= 3:
            with nc.named_scope("ph_moe1"):
                if xldf[0] is not None:
                    P1 = mkP("_m1", xld_ext=xldf[0], crow_d=W.crow_d)
                else:
                    P1 = mkP("_m1", load_x=True, xT=xT0, crow_d=W.crow_d)
                emit_moe(nc, P1)
        gy1 = [0, 0]

        def ag_y1(gp, hf):
            op = gp.collective_compute(
                "AllGather", ALU.bypass, replica_groups=[list(range(NC))],
                ins=[[y1slA, y1slB][hf]], outs=[[y1fA, y1fB][hf]])
            cc.inc(op, 1)
            gy1[hf] = cc.n
        if upto >= 4:
            with nc.named_scope("ph_ln1"):
                Pl1 = mkP("_l1", yoT=None, yoT_halves=(y1slA, y1slB),
                          ag_emit=ag_y1 if upto >= 5 else None)
                emit_lnx(nc, Pl1, rms=False)
        CH_ORD = list(range(0, 16, 2)) + list(range(1, 16, 2))
        y1fh = [y1fA, y1fB]
        Pa0 = mkP("_a0", ao_ext=None, ao_in=ao0in, ch_order=CH_ORD,
                  y_gate={0: (cc.sem, gy1[0]), 8: (cc.sem, gy1[1])} if upto >= 5 else None,
                  y_src=lambda kt, ch: y1fh[ch % 2][
                      (ch // 2) * D + kt * 128:(ch // 2) * D + (kt + 1) * 128, :])
        if upto >= 6:
            with nc.named_scope("ph_attn0"):
                emit_attn0(nc, Pa0)
        if upto >= 7:
            with nc.named_scope("ph_a2a0"):
                coll("AllToAll", ao0in, ao0out, a2a=True)
        if upto >= 8:
            with nc.named_scope("ph_proj"):
                Ppj = mkP("_pj", aT=ao0out)
                emit_proj(nc, Ppj)
        if upto >= 9:
            with nc.named_scope("ph_moe2"):
                P2 = mkP("_m2", crow_d=W.crow_d)
                emit_moe(nc, P2)

        if upto >= 10:
            with nc.named_scope("ph_agAw"):
                coll_wait()
            with nc.named_scope("ph_swi1"):
                Ps1 = mkP("_s1", auS=a_up_f, agS=a_gt_f, adS=a_dn_f)
                emit_swiglu(nc, Ps1)
        gh4 = [0, 0]

        def ag_h4(gp, hf):
            op = gp.collective_compute(
                "AllGather", ALU.bypass, replica_groups=[list(range(NC))],
                ins=[[h4slA, h4slB][hf]], outs=[[h4fA, h4fB][hf]])
            cc.inc(op, 1)
            gh4[hf] = cc.n
        if upto >= 11:
            with nc.named_scope("ph_ln2"):
                Pl2 = mkP("_l2", yoT=y4sl, hoT=None, hoT_halves=(h4slA, h4slB),
                          ag_emit=ag_h4 if upto >= 12 else None)
                emit_lnx(nc, Pl2, rms=True)
        h4fh = [h4fA, h4fB]
        Pa1 = mkP("_a1", ao_ext=None, ao_in=ao1in, ch_order=CH_ORD,
                  y_gate={0: (cc.sem, gh4[0]), 8: (cc.sem, gh4[1])} if upto >= 12 else None,
                  y_src=lambda kt, ch: h4fh[ch % 2][
                      (ch // 2) * D + kt * 128:(ch // 2) * D + (kt + 1) * 128, :])
        if upto >= 13:
            with nc.named_scope("ph_attn1"):
                emit_attn1(nc, Pa1)
        if upto >= 14:
            with nc.named_scope("ph_a2a1"):
                coll("AllToAll", ao1in, ao1out, a2a=True)
        if upto >= 15:
            with nc.named_scope("ph_post1"):
                Pp1 = mkP("_p1", yT_c=y4sl, aT=ao1out)
                emit_post1(nc, Pp1)
        if upto >= 16:
            with nc.named_scope("ph_swi2"):
                Ps2 = mkP("_s2", auS=a_up_f, agS=a_gt_f, adS=a_dn_f,
                          store_x=True, xoT=xoT)
                emit_swiglu(nc, Ps2)
        else:
            with ExitStack() as ex2:
                tz = ex2.enter_context(nc.sbuf_tensor("tz_fin", [128, TC], F32)).ap()
                szf = Sem(nc.alloc_semaphore(name="szf"))
                with nc.Block() as blk:
                    @blk.gpsimd
                    def _(gp):
                        gp.memset(tz, 0.0).then_inc(szf.sem, 1)
                    @blk.sync
                    def _(sy):
                        sy.wait_ge(szf.sem, 1)
                        n = 0
                        for dd2 in range(KT):
                            sy.dma_start(out=xoT[dd2 * 128:(dd2 + 1) * 128, :],
                                         in_=tz).then_inc(szf.sem, 16)
                            n += 16
                        sy.wait_ge(szf.sem, 1 + n)

    return nc


def mega_in_maps(inputs):
    pre_moe = moe_prep(inputs)
    pre_a0 = attn0_prep(inputs)
    pre_a1 = attn1_prep(inputs)
    x0 = _f32(np.asarray(inputs["L0_attn_x"], dtype=np.float32)[0].T)

    W_up = _f32(inputs["L1_ffn_W_up"])
    W_gt = _f32(inputs["L1_ffn_W_gate"])
    W_dn = _f32(inputs["L1_ffn_W_down"])
    SU_ug = _f32(inputs["L1_ffn_SU_ug"])
    SU_dp = _f32(inputs["L1_ffn_SU_dp"])
    Wsc_ug = _f32(inputs["L1_ffn_Wscale_ug"])
    Wsc_dp = _f32(inputs["L1_ffn_Wscale_dp"])
    h128 = _bf(_hadamard(128))

    wfc = _f32(inputs["L0_attn_w_fc"])
    bfc = _cols(inputs["L0_attn_b_fc"])
    woT = _f32(np.asarray(inputs["L1_attn_w_o"], dtype=np.float32).T)
    wupS = _slabify_up(_f32(np.asarray(inputs["L1_attn_w_up"], dtype=np.float32).T))
    _wdn_raw = _f32(np.asarray(inputs["L1_attn_w_down"], dtype=np.float32).T)
    _wdnS = np.empty((16, 128, 16, 256), dtype=np.float32)
    for hi in range(2):
        for ci in range(8):
            blk = _wdn_raw[hi * 2048:(hi + 1) * 2048, ci * 256:(ci + 1) * 256]
            _wdnS[hi * 8 + ci] = blk.reshape(16, 128, 256).transpose(1, 0, 2)
    wdnT = _f32(_wdnS.reshape(2048, 4096))
    wn_c = _cols(inputs["L1_attn_ffn_norm_w"])
    wa_c = _cols(inputs["L1_attn_attn_norm_w"])

    maps = []
    for c in range(NC):
        m = dict(pre_moe)
        m.update(CONSTS)
        m["xT"] = _f32(x0[:, c * TC:(c + 1) * TC])
        m["wup"] = _f32(W_up[1024 * c:1024 * (c + 1), :].T)
        m["wgt"] = _f32(W_gt[1024 * c:1024 * (c + 1), :].T)
        m["wdn"] = _f32(W_dn[256 * c:256 * (c + 1), :].T)
        m["su_ug"] = _f32(SU_ug.reshape(16, 128).T)
        m["su_dp"] = _f32(SU_dp.reshape(64, 128).T)
        m["wsc_up"] = _f32(Wsc_ug[None, 1024 * c:1024 * (c + 1)])
        m["wsc_gt"] = _f32(Wsc_ug[None, 8192 + 1024 * c:8192 + 1024 * (c + 1)])
        m["wsc_dn"] = _f32(Wsc_dp[None, 256 * c:256 * (c + 1)])
        m["h128"] = h128
        m["wqS"] = pre_a0["wqS"][c]
        m["wkvS"] = pre_a0["wkvS"]
        m["bq_c"] = pre_a0["bq_c"][c]
        m["bkv_c"] = pre_a0["bkv_c"]
        m["mask01"] = pre_a0["mask01"]
        m["wf"] = wfc
        m["bf_c"] = bfc
        m["wa_c"] = wa_c
        m["wS"] = pre_a1["wS"][c]
        m["cosT"] = pre_a1["cosT"]
        m["sinT"] = pre_a1["sinT"]
        m["ptm"] = pre_a1["ptm"]
        m["woT"] = woT
        m["wn_c"] = wn_c
        m["wupS"] = wupS
        m["wdnT"] = wdnT
        maps.append(m)
    return maps


def kernel(**inputs):
    maps = mega_in_maps(inputs)
    nc = _get("mega", build_mega)
    res = run_bass_kernel_spmd(nc, maps, core_ids=list(range(NC)))
    x6 = np.concatenate([np.asarray(res.results[c]["xoT"]) for c in range(NC)], axis=1)
    return np.ascontiguousarray(x6.T, dtype=np.float32).reshape(1, T, D)


def kernel_unfused(**inputs):
    x0 = _f32(np.asarray(inputs["L0_attn_x"], dtype=np.float32)[0].T)

    pre_moe = moe_prep(inputs)
    pre_a0 = attn0_prep(inputs)
    pre_a1 = attn1_prep(inputs)

    W_up = _f32(inputs["L1_ffn_W_up"])
    W_gt = _f32(inputs["L1_ffn_W_gate"])
    W_dn = _f32(inputs["L1_ffn_W_down"])
    SU_ug = _f32(inputs["L1_ffn_SU_ug"])
    SU_dp = _f32(inputs["L1_ffn_SU_dp"])
    Wsc_ug = _f32(inputs["L1_ffn_Wscale_ug"])
    Wsc_dp = _f32(inputs["L1_ffn_Wscale_dp"])
    h128 = _bf(_hadamard(128))
    fold_maps = []
    for c in range(NC):
        m = {
            "wup": _f32(W_up[1024 * c:1024 * (c + 1), :].T),
            "wgt": _f32(W_gt[1024 * c:1024 * (c + 1), :].T),
            "wdn": _f32(W_dn[256 * c:256 * (c + 1), :].T),
            "su_ug": _f32(SU_ug.reshape(16, 128).T),
            "su_dp": _f32(SU_dp.reshape(64, 128).T),
            "wsc_up": _f32(Wsc_ug[None, 1024 * c:1024 * (c + 1)]),
            "wsc_gt": _f32(Wsc_ug[None, 8192 + 1024 * c:8192 + 1024 * (c + 1)]),
            "wsc_dn": _f32(Wsc_dp[None, 256 * c:256 * (c + 1)]),
            "h128": h128,
        }
        m.update(CONSTS)
        fold_maps.append(m)
    res = _run("fold", build_fold, fold_maps)
    pre_sw = {
        "auS": np.concatenate([np.asarray(res.results[c]["a_up"]) for c in range(NC)], axis=0),
        "agS": np.concatenate([np.asarray(res.results[c]["a_gt"]) for c in range(NC)], axis=0),
        "adS": np.concatenate([np.asarray(res.results[c]["a_dn"]) for c in range(NC)], axis=0),
    }

    def run_moe_l(xT):
        maps = []
        for c in range(NC):
            m = dict(pre_moe)
            m["xT"] = _f32(xT[:, c * TC:(c + 1) * TC])
            m.update(CONSTS)
            maps.append(m)
        r = _run("moe", build_moe, maps)
        return np.concatenate([r.results[c]["xoT"] for c in range(NC)], axis=1)

    def run_lnx_l(xT, rms=False, wa=None):
        maps = []
        for c in range(NC):
            m = {"xT": _f32(xT[:, c * TC:(c + 1) * TC])}
            if rms:
                m["wa_c"] = _cols(wa)
            m.update(CONSTS)
            maps.append(m)
        r = _run("lnx_rms" if rms else "lnx",
                 (lambda: build_lnx(True)) if rms else (lambda: build_lnx(False)),
                 maps)
        y = np.concatenate([r.results[c]["yoT"] for c in range(NC)], axis=1)
        if rms:
            h = np.concatenate([r.results[c]["hoT"] for c in range(NC)], axis=1)
            return y, h
        return y

    def run_swiglu_l(xT):
        maps = []
        for c in range(NC):
            m = {"xT": _f32(xT[:, c * TC:(c + 1) * TC])}
            m.update(pre_sw)
            m.update(CONSTS)
            maps.append(m)
        r = _run("swiglu", build_swiglu, maps)
        return np.concatenate([r.results[c]["xoT"] for c in range(NC)], axis=1)

    x1 = run_moe_l(x0)
    y1 = run_lnx_l(x1)
    r = _run("attn0", build_attn0,
             [attn0_in_map(c, y1, pre_a0) for c in range(NC)])
    ao0 = np.concatenate([r.results[c]["aoT"] for c in range(NC)], axis=0)

    wfc = _f32(inputs["L0_attn_w_fc"])
    bfc = _cols(inputs["L0_attn_b_fc"])
    maps = []
    for c in range(NC):
        maps.append({"xT": _f32(x1[:, c * TC:(c + 1) * TC]),
                     "aT": np.ascontiguousarray(ao0[:, c * TC:(c + 1) * TC]),
                     "wf": wfc, "bf_c": bfc})
    r = _run("proj", build_proj, maps)
    x2 = np.concatenate([r.results[c]["xoT"] for c in range(NC)], axis=1)

    x3 = run_moe_l(x2)

    x4 = run_swiglu_l(x3)
    y4, h4 = run_lnx_l(x4, rms=True, wa=inputs["L1_attn_attn_norm_w"])
    maps = []
    for c in range(NC):
        m = dict(wS=pre_a1["wS"][c], cosT=pre_a1["cosT"], sinT=pre_a1["sinT"],
                 ptm=pre_a1["ptm"], mask01=pre_a1["mask01"], hT=h4)
        m.update(CONSTS)
        maps.append(m)
    r = _run("attn1", build_attn1, maps)
    ao1 = np.concatenate([r.results[c]["aoT"] for c in range(NC)], axis=0)

    woT = _f32(np.asarray(inputs["L1_attn_w_o"], dtype=np.float32).T)
    wupS = _slabify_up(_f32(np.asarray(inputs["L1_attn_w_up"], dtype=np.float32).T))
    _wdn_raw = _f32(np.asarray(inputs["L1_attn_w_down"], dtype=np.float32).T)
    _wdnS = np.empty((16, 128, 16, 256), dtype=np.float32)
    for hi in range(2):
        for ci in range(8):
            blk = _wdn_raw[hi * 2048:(hi + 1) * 2048, ci * 256:(ci + 1) * 256]
            _wdnS[hi * 8 + ci] = blk.reshape(16, 128, 256).transpose(1, 0, 2)
    wdnT = _f32(_wdnS.reshape(2048, 4096))
    wn_c = _cols(inputs["L1_attn_ffn_norm_w"])
    maps = []
    for c in range(NC):
        m = {"xT": _f32(x4[:, c * TC:(c + 1) * TC]),
             "yT_c": np.ascontiguousarray(y4[:, c * TC:(c + 1) * TC]),
             "aT": np.ascontiguousarray(ao1[:, c * TC:(c + 1) * TC]),
             "woT": woT, "wn_c": wn_c, "wupS": wupS, "wdnT": wdnT}
        m.update(CONSTS)
        maps.append(m)
    r = _run("post1", build_post1, maps)
    x5 = np.concatenate([r.results[c]["xoT"] for c in range(NC)], axis=1)

    x6 = run_swiglu_l(x5)
    return np.ascontiguousarray(x6.T, dtype=np.float32).reshape(1, T, D)


# version 17
# speedup vs baseline: 1.0211x; 1.0010x over previous
import numpy as np
import ml_dtypes
from contextlib import ExitStack, nullcontext
from concourse.aot_env import allow_ici_all_to_all
import concourse.bass as bass
import concourse.mybir as mybir
from concourse.bass_utils import run_bass_kernel_spmd

F32 = mybir.dt.float32
BF16 = mybir.dt.bfloat16
AF = mybir.ActivationFunctionType
ALU = mybir.AluOpType
BF = ml_dtypes.bfloat16

NC = 8
T = 8192
D = 2048
TC = T // NC
KT = D // 128
LNEPS = 1e-5


def _bf(x):
    return np.ascontiguousarray(np.asarray(x), dtype=BF)


def _f32(x):
    return np.ascontiguousarray(np.asarray(x), dtype=np.float32)


class Sem:

    def __init__(self, sem):
        self.sem = sem
        self.n = 0

    def inc(self, inst, k=1):
        self.n += k
        inst.then_inc(self.sem, k)
        return self.n


class NS:
    pass


def _phase_close(nc, sems):
    with nc.Block() as blk:
        @blk.gpsimd
        def _(gp):
            for s in sems:
                if s.n > 0:
                    gp.wait_ge(s.sem, s.n)
            for s in sems:
                gp.sem_clear(s.sem)
    for s in sems:
        nc.release_semaphore(s.sem)


def _mk_factory(nc, tag, sems):
    def mk(nm, dma=False):
        s = Sem(nc.alloc_semaphore(name=nm + tag))
        if not dma:
            sems.append(s)
        return s
    return mk


CONSTS = {
    "onesb": _bf(np.ones((128, 128))),
    "onesf": _f32(np.ones((128, 128))),
    "ident": _bf(np.eye(128)),
}


def _decl_consts(nc):
    c = NS()
    c.onesb = nc.dram_tensor("onesb", [128, 128], BF16, kind="ExternalInput").ap()
    c.onesf = nc.dram_tensor("onesf", [128, 128], F32, kind="ExternalInput").ap()
    c.ident = nc.dram_tensor("ident", [128, 128], BF16, kind="ExternalInput").ap()
    return c


def emit_ln(block, g, xs, yb, sqbuf):
    nt = TC

    scal_base = g.scal.n
    vec_base = g.vec.n

    @block.scalar
    def _(s):
        if g.xld is not None:
            s.wait_ge(g.xld.sem, 16 * KT)
        for i in range(KT):
            g.scal.inc(s.activation(yb[:, i * nt:(i + 1) * nt],
                                    xs[:, i * nt:(i + 1) * nt], AF.Copy))
    g.ln_cast = g.scal.n

    @block.vector
    def _(v):
        for i in range(KT):
            v.wait_ge(g.scal.sem, scal_base + i + 1)
            g.vec.inc(v.tensor_tensor(sqbuf[:, i * nt:(i + 1) * nt],
                                      yb[:, i * nt:(i + 1) * nt],
                                      yb[:, i * nt:(i + 1) * nt], ALU.mult))
    g.ln_sq = g.vec.n

    @block.tensor
    def _(t):
        t.wait_ge(g.cld.sem, g.cld_total)
        last = None
        for i in range(KT):
            t.wait_ge(g.scal.sem, scal_base + i + 1)
            for ch in range(2):
                t.matmul(g.psA[0:1, ch * 512:(ch + 1) * 512], g.onesb_s[:, 0:1],
                         yb[:, i * nt + ch * 512: i * nt + (ch + 1) * 512],
                         start=(i == 0), stop=(i == KT - 1))
            t.wait_ge(g.vec.sem, vec_base + i + 1)
            for ch in range(2):
                last = t.matmul(g.psB[0:1, ch * 512:(ch + 1) * 512], g.onesb_s[:, 0:1],
                                sqbuf[:, i * nt + ch * 512: i * nt + (ch + 1) * 512],
                                start=(i == 0), stop=(i == KT - 1))
        g.ln_stat_mm = g.tens.inc(last)

    @block.vector
    def _(v):
        v.wait_ge(g.tens.sem, g.ln_stat_mm)
        m = g.rows[0:1, 0:nt]
        ex2 = g.rows[0:1, nt:2 * nt]
        var = g.rows[0:1, 2 * nt:3 * nt]
        v.tensor_scalar(m, g.psA[0:1, 0:nt], 1.0 / D, None, ALU.mult)
        v.tensor_scalar(ex2, g.psB[0:1, 0:nt], 1.0 / D, None, ALU.mult)
        v.drain()
        v.tensor_tensor(var, m, m, ALU.mult)
        v.drain()
        v.tensor_tensor(var, ex2, var, ALU.subtract)
        v.drain()
        g.ln_rows = g.vec.inc(
            v.tensor_scalar(var, var, float(LNEPS), None, ALU.add))

    @block.scalar
    def _(s):
        s.wait_ge(g.vec.sem, g.ln_rows)
        var = g.rows[0:1, 2 * nt:3 * nt]
        lnv = g.rows[0:1, 3 * nt:4 * nt]
        s.activation(lnv, var, AF.Ln)
        s.drain()
        g.ln_rstd = g.scal.inc(
            s.activation(var, lnv, AF.Exp, scale=-0.5))

    @block.tensor
    def _(t):
        t.wait_ge(g.scal.sem, g.ln_rstd)
        last = None
        for ch in range(2):
            t.matmul(g.psC[:, ch * 512:(ch + 1) * 512], g.onesf_s[0:1, :],
                     g.rows[0:1, ch * 512:(ch + 1) * 512], start=True, stop=True)
            last = t.matmul(g.psA[:, ch * 512:(ch + 1) * 512], g.onesf_s[0:1, :],
                            g.rows[0:1, 2 * nt + ch * 512:2 * nt + (ch + 1) * 512],
                            start=True, stop=True)
        g.ln_bcast = g.tens.inc(last)

    g.ln_aply0 = g.vec.n

    @block.vector
    def _(v):
        v.wait_ge(g.tens.sem, g.ln_bcast)
        v.tensor_copy(g.lnm, g.psC[:, 0:nt])
        v.tensor_copy(g.lnr, g.psA[:, 0:nt])
        v.drain()
        for i in range(KT):
            v.tensor_tensor(g.tmp[:, 0:nt], yb[:, i * nt:(i + 1) * nt],
                            g.lnm, ALU.subtract)
            v.drain()
            g.vec.inc(v.tensor_tensor(yb[:, i * nt:(i + 1) * nt], g.tmp[:, 0:nt],
                                      g.lnr, ALU.mult))
            v.drain()
    g.ln_done = g.vec.n


def emit_moe(nc, P):
    xT, gTt = P.xT, P.gTt
    w1S, w3S, w2S = P.w1S, P.w3S, P.w2S
    ws1S, ws3S, ws2S = P.ws1S, P.ws3S, P.ws2S
    cst, crow_d, xoT = P.cst, P.crow_d, P.xoT
    xs = P.xs

    NEXP = 9
    with ExitStack() as ex:
        sbt = lambda nm, sh, dt: ex.enter_context(nc.sbuf_tensor(nm + P.tag, sh, dt)).ap()
        pst = lambda nm, sh, dt: ex.enter_context(nc.psum_tensor(nm + P.tag, sh, dt)).ap()
        g = NS()
        yb = sbt("yb", [128, KT * TC], BF16)
        scr4 = sbt("scr4", [128, KT * TC], BF16)
        gsl = sbt("gsl", [128, 8 * TC], BF16)
        bcast = sbt("bcast", [128, 2 * TC], BF16)
        wbuf = sbt("wbuf", [128, 4 * D], BF16)
        gbuf = sbt("gbuf", [128, 128], BF16)
        g.tmp = sbt("tmp", [128, TC], F32)
        g.lnm = sbt("lnm", [128, TC], F32)
        g.lnr = sbt("lnr", [128, TC], F32)
        g.rows = sbt("rows", [1, 4 * TC], F32)
        lg = sbt("lg", [128, 64], F32)
        sm1 = sbt("sm1", [128, 24], F32)
        pexp = sbt("pexp", [128, 64], F32)
        msk = sbt("msk", [128, 64], F32)
        comb_bf = sbt("comb_bf", [128, 64], BF16)
        crow_sb = sbt("crow_sb", [8, TC], BF16)
        crow_fl = sbt("crow_fl", [1, 2 * TC], BF16)
        g.onesb_s = sbt("onesb_s", [128, 128], BF16)
        g.onesf_s = sbt("onesf_s", [128, 128], F32)
        ident_s = sbt("ident_s", [128, 128], BF16)
        tmpbf = sbt("tmpbf", [128, 2 * 512], BF16)
        g0buf = sbt("g0buf", [128, 2 * 512], BF16)

        g.psA = pst("psA", [128, 1024], F32)
        g.psB = pst("psB", [128, 1024], F32)
        g.psC = pst("psC", [128, 1024], F32)
        psD = pst("psD", [128, 512], F32)
        psE = pst("psE", [128, 1024], BF16)

        W2BANK = [lambda a=a, c=c: [g.psA, g.psB, g.psC][a][:, c * 512:(c + 1) * 512]
                  for a in range(3) for c in range(2)]

        sems = []
        with nc.Block() as block:
            mk = _mk_factory(nc, P.tag, sems)
            if getattr(P, "xld_ext", None) is not None:
                g.xld = P.xld_ext
            else:
                g.xld = mk("xld") if P.load_x else None
            cld = mk("cld")
            g.cld = cld
            gld = mk("gld", dma=True)
            wldS = [mk("wld%d" % q, dma=True) for q in range(4)]
            w2ld = mk("w2ld", dma=True)
            g.scal = mk("scal")
            g.vec = mk("vec")
            g.tens = mk("tens")
            hmm = mk("hmm")
            w2mm = mk("w2mm")
            sil = mk("sil")
            hev = mk("hev")
            w2ev = mk("w2ev")
            bce = mk("bce")
            bcm = mk("bcm")
            bcr = mk("bcr")
            bcrS = [mk("bcr%d" % q) for q in range(2)]
            stf = mk("stf")

            @block.sync
            def _(sy):
                if P.load_x:
                    for i in range(KT):
                        g.xld.inc(sy.dma_start(out=xs[:, i * TC:(i + 1) * TC],
                                               in_=xT[i * 128:(i + 1) * 128, :]), 16)
                cld.inc(sy.dma_start(out=g.onesb_s, in_=cst.onesb), 16)
                cld.inc(sy.dma_start(out=g.onesf_s, in_=cst.onesf), 16)
                cld.inc(sy.dma_start(out=ident_s, in_=cst.ident), 16)
            g.cld_total = 48

            emit_ln(block, g, xs, yb, scr4)

            @block.gpsimd
            def _(gp):
                gld.inc(gp.dma_start(out=gbuf, in_=gTt), 16)
                first_w2 = True
                EO = [8] + list(range(8))
                for pos, e in enumerate(EO):
                    for i in range(8):
                        p = pos * 8 + i
                        if p >= 2:
                            gp.wait_ge(hmm.sem, 2 * p - 2)
                        for which in range(2):
                            q = 2 * p + which
                            if e < 8:
                                src = (w1S if which == 0 else w3S)
                                off = (e * 8 + i) * 128 * D
                            else:
                                src = (ws1S if which == 0 else ws3S)
                                off = i * 128 * D
                            ap = bass.AP(src.tensor, off, [[D, 128], [1, D]])
                            wldS[q % 4].inc(gp.dma_start(
                                out=wbuf[:, (q % 4) * D:(q % 4 + 1) * D], in_=ap), 16)
                        if i == 3:
                            if first_w2:
                                gp.wait_ge(g.tens.sem, g.ln_stat_mm)
                                first_w2 = False
                            if pos >= 1:
                                gp.wait_ge(w2mm.sem, 32 * pos)
                            for kt in range(8):
                                if e < 8:
                                    src, off = w2S, (e * 8 + kt) * 128 * D
                                else:
                                    src, off = ws2S, kt * 128 * D
                                ap = bass.AP(src.tensor, off, [[D, 128], [1, D]])
                                w2ld.inc(gp.dma_start(
                                    out=scr4[:, kt * D:(kt + 1) * D],
                                    in_=ap), 16)

            @block.tensor
            def _(t):
                t.wait_ge(gld.sem, 16)
                last = None
                for tt in range(8):
                    for kt in range(KT):
                        if tt == 0:
                            t.wait_ge(g.vec.sem, g.ln_aply0 + kt + 1)
                        last = t.matmul(g.psA[:, tt * 8:tt * 8 + 8],
                                        yb[:, kt * TC + tt * 128: kt * TC + tt * 128 + 128],
                                        gbuf[:, kt * 8:(kt + 1) * 8],
                                        start=(kt == 0), stop=(kt == KT - 1))
                g.rt_mm = g.tens.inc(last)

            @block.vector
            def _(v):
                v.wait_ge(g.tens.sem, g.rt_mm)
                v.tensor_copy(lg, g.psA[:, 0:64])
                v.drain()
                for tt in range(8):
                    v.tensor_reduce(sm1[:, tt:tt + 1], lg[:, tt * 8:(tt + 1) * 8],
                                    axis=mybir.AxisListType.X, op=ALU.max)
                v.drain()
                last = None
                for tt in range(8):
                    last = v.tensor_scalar(lg[:, tt * 8:(tt + 1) * 8],
                                           lg[:, tt * 8:(tt + 1) * 8],
                                           sm1[:, tt:tt + 1], None, ALU.subtract)
                g.rt_lp = g.vec.inc(last)

            @block.scalar
            def _(s):
                s.wait_ge(g.vec.sem, g.rt_lp)
                g.rt_exp = g.scal.inc(s.activation(pexp, lg, AF.Exp))

            @block.vector
            def _(v):
                v.wait_ge(g.scal.sem, g.rt_exp)
                for tt in range(8):
                    v.tensor_reduce(sm1[:, 8 + tt:9 + tt], pexp[:, tt * 8:(tt + 1) * 8],
                                    axis=mybir.AxisListType.X, op=ALU.add)
                v.drain()
                v.reciprocal(sm1[:, 8:16], sm1[:, 8:16])
                v.tensor_scalar(msk, lg, 0.0, None, ALU.is_ge)
                v.drain()
                v.scalar_tensor_tensor(msk, msk, -1e30, lg, ALU.mult, ALU.add)
                v.drain()
                for tt in range(8):
                    v.tensor_reduce(sm1[:, 16 + tt:17 + tt], msk[:, tt * 8:(tt + 1) * 8],
                                    axis=mybir.AxisListType.X, op=ALU.max)
                v.drain()
                for tt in range(8):
                    v.tensor_scalar(msk[:, tt * 8:(tt + 1) * 8], lg[:, tt * 8:(tt + 1) * 8],
                                    sm1[:, 16 + tt:17 + tt], None, ALU.is_ge)
                    v.tensor_scalar(pexp[:, tt * 8:(tt + 1) * 8], pexp[:, tt * 8:(tt + 1) * 8],
                                    sm1[:, 8 + tt:9 + tt], 0.5, ALU.mult, ALU.mult)
                v.drain()
                g.rt_comb = g.vec.inc(v.tensor_tensor(comb_bf, pexp, msk, ALU.mult))

            @block.tensor
            def _(t):
                t.wait_ge(g.vec.sem, g.rt_comb)
                t.wait_ge(cld.sem, 48)
                last = None
                for tt in range(8):
                    last = t.transpose(psE[0:8, tt * 128:(tt + 1) * 128],
                                       comb_bf[:, tt * 8:(tt + 1) * 8], ident_s)
                g.rt_tr = g.tens.inc(last)

            @block.vector
            def _(v):
                v.wait_ge(g.tens.sem, g.rt_tr)
                g.rt_crow = g.vec.inc(v.tensor_copy(crow_sb, psE[0:8, :]))

            @block.sync
            def _(sy):
                sy.wait_ge(g.vec.sem, g.rt_crow)
                bcr.inc(sy.dma_start(out=crow_d, in_=crow_sb), 16)
                sy.wait_ge(bcr.sem, 16)
                for e in range(8):
                    if e >= 2:
                        sy.wait_ge(bcm.sem, 2 * (e - 1))
                    bcrS[e % 2].inc(sy.dma_start(
                        out=crow_fl[0:1, (e % 2) * TC:(e % 2 + 1) * TC],
                        in_=crow_d[e:e + 1, :]), 16)

            @block.tensor
            def _(t):
                hcc = 0
                w2c = 0
                EO = [8] + list(range(8))
                for pos, e in enumerate(EO):
                    if e < 8:
                        t.wait_ge(bcrS[e % 2].sem, 16 * (e // 2 + 1))
                        for ch in range(2):
                            idx = 2 * e + ch
                            if idx >= 1:
                                t.wait_ge(bce.sem, idx)
                            last = t.matmul(
                                psD[:, 0:512], g.onesb_s[0:1, :],
                                crow_fl[0:1, (e % 2) * TC + ch * 512:(e % 2) * TC + (ch + 1) * 512],
                                start=True, stop=True)
                            bcm.inc(last)
                    if pos >= 1:
                        t.wait_ge(w2ev.sem, 32 * pos)
                    for i in range(8):
                        p = pos * 8 + i
                        q0, q1 = 2 * p, 2 * p + 1
                        t.wait_ge(wldS[q0 % 4].sem, 16 * (q0 // 4 + 1))
                        t.wait_ge(wldS[q1 % 4].sem, 16 * (q1 // 4 + 1))
                        for ch in range(2):
                            slot = [g.psA, g.psB, g.psC][hcc % 3]
                            if hcc >= 3:
                                t.wait_ge(hev.sem, hcc - 2)
                            for kt in range(KT):
                                if hcc == 0:
                                    t.wait_ge(g.vec.sem, g.ln_aply0 + kt + 1)
                                t.matmul(slot[:, 0:512],
                                         wbuf[:, (2 * p % 4) * D + kt * 128:(2 * p % 4) * D + (kt + 1) * 128],
                                         yb[:, kt * TC + ch * 512:kt * TC + (ch + 1) * 512],
                                         start=(kt == 0), stop=(kt == KT - 1))
                            last = None
                            for kt in range(KT):
                                last = t.matmul(
                                    slot[:, 512:1024],
                                    wbuf[:, ((2 * p + 1) % 4) * D + kt * 128:((2 * p + 1) % 4) * D + (kt + 1) * 128],
                                    yb[:, kt * TC + ch * 512:kt * TC + (ch + 1) * 512],
                                    start=(kt == 0), stop=(kt == KT - 1))
                            hmm.inc(last)
                            hcc += 1
                    t.wait_ge(w2ld.sem, 16 * 8 * (pos + 1))
                    t.wait_ge(hev.sem, 16 * (pos + 1))
                    for d in range(KT):
                        for ch in range(2):
                            pt = W2BANK[w2c % 6]()
                            if w2c >= 6:
                                t.wait_ge(w2ev.sem, w2c - 5)
                            last = None
                            for kt in range(8):
                                last = t.matmul(
                                    pt,
                                    scr4[:, kt * D + d * 128:kt * D + (d + 1) * 128],
                                    gsl[:, kt * TC + ch * 512:kt * TC + (ch + 1) * 512],
                                    start=(kt == 0), stop=(kt == 7))
                            w2mm.inc(last)
                            w2c += 1

            @block.scalar
            def _(s):
                hcc = 0
                EO = [8] + list(range(8))
                for pos, e in enumerate(EO):
                    if e < 8:
                        for ch in range(2):
                            s.wait_ge(bcm.sem, 2 * e + ch + 1)
                            if e >= 2:
                                s.wait_ge(hev.sem, 16 * e)
                            bce.inc(s.activation(
                                bcast[:, (e % 2) * TC + ch * 512:(e % 2) * TC + (ch + 1) * 512],
                                psD[:, 0:512], AF.Copy))
                    for i in range(8):
                        for ch in range(2):
                            s.wait_ge(hmm.sem, hcc + 1)
                            if hcc >= 2:
                                s.wait_ge(hev.sem, hcc - 1)
                            slot = [g.psA, g.psB, g.psC][hcc % 3]
                            sil.inc(s.activation(
                                tmpbf[:, (hcc % 2) * 512:(hcc % 2 + 1) * 512],
                                slot[:, 0:512], AF.Silu))
                            hcc += 1

            @block.vector
            def _(v):
                hcc = 0
                w2c = 0
                EO = [8] + list(range(8))
                for pos, e in enumerate(EO):
                    for i in range(8):
                        for ch in range(2):
                            v.wait_ge(sil.sem, hcc + 1)
                            slot = [g.psA, g.psB, g.psC][hcc % 3]
                            dst = gsl[:, i * TC + ch * 512:i * TC + (ch + 1) * 512]
                            if e < 8:
                                if i == 0 and ch == 0:
                                    v.wait_ge(bce.sem, 2 * (e + 1))
                                v.tensor_tensor(
                                    g0buf[:, (hcc % 2) * 512:(hcc % 2 + 1) * 512],
                                    tmpbf[:, (hcc % 2) * 512:(hcc % 2 + 1) * 512],
                                    slot[:, 512:1024], ALU.mult)
                                v.drain()
                                hev.inc(v.tensor_tensor(
                                    dst, g0buf[:, (hcc % 2) * 512:(hcc % 2 + 1) * 512],
                                    bcast[:, (e % 2) * TC + ch * 512:(e % 2) * TC + (ch + 1) * 512],
                                    ALU.mult))
                            else:
                                hev.inc(v.tensor_tensor(
                                    dst, tmpbf[:, (hcc % 2) * 512:(hcc % 2 + 1) * 512],
                                    slot[:, 512:1024], ALU.mult))
                            hcc += 1
                    for d in range(KT):
                        for ch in range(2):
                            v.wait_ge(w2mm.sem, w2c + 1)
                            pt = W2BANK[w2c % 6]()
                            dst = xs[:, d * TC + ch * 512:d * TC + (ch + 1) * 512]
                            if e < 8:
                                w2ev.inc(v.tensor_tensor(dst, dst, pt, ALU.add))
                            else:
                                w2ev.inc(v.scalar_tensor_tensor(dst, pt, 0.5, dst,
                                                                ALU.mult, ALU.add))
                            w2c += 1

            if P.store_x:
                @block.sync
                def _(sy):
                    sy.wait_ge(w2ev.sem, NEXP * 32)
                    for d in range(KT):
                        stf.inc(sy.dma_start(out=xoT[d * 128:(d + 1) * 128, :],
                                             in_=xs[:, d * TC:(d + 1) * TC]), 16)
                    sy.wait_ge(stf.sem, 16 * KT)

        _phase_close(nc, sems)


def _moe_drams(nc, P):
    P.gTt = nc.dram_tensor("gTt", [128, 128], F32, kind="ExternalInput").ap()
    P.w1S = nc.dram_tensor("w1S", [8 * 8 * 128, D], F32, kind="ExternalInput").ap()
    P.w3S = nc.dram_tensor("w3S", [8 * 8 * 128, D], F32, kind="ExternalInput").ap()
    P.w2S = nc.dram_tensor("w2S", [8 * 8 * 128, D], F32, kind="ExternalInput").ap()
    P.ws1S = nc.dram_tensor("ws1S", [8 * 128, D], F32, kind="ExternalInput").ap()
    P.ws3S = nc.dram_tensor("ws3S", [8 * 128, D], F32, kind="ExternalInput").ap()
    P.ws2S = nc.dram_tensor("ws2S", [8 * 128, D], F32, kind="ExternalInput").ap()
    P.crow_d = nc.dram_tensor("crow_d" + P.tag, [8, TC], BF16).ap()


def build_moe():
    nc = bass.Bass("TRN2", target_bir_lowering=False)
    P = NS()
    P.tag = ""
    P.load_x = True
    P.store_x = True
    P.xT = nc.dram_tensor("xT", [D, TC], F32, kind="ExternalInput").ap()
    _moe_drams(nc, P)
    P.cst = _decl_consts(nc)
    P.xoT = nc.dram_tensor("xoT", [D, TC], F32, kind="ExternalOutput").ap()
    with ExitStack() as gx:
        P.xs = gx.enter_context(nc.sbuf_tensor("xs_g", [128, KT * TC], F32)).ap()
        emit_moe(nc, P)
    return nc


def moe_prep(inp):
    w1 = _f32(inp["L0_ffn_w1"])
    w3 = _f32(inp["L0_ffn_w3"])
    w2 = _f32(inp["L0_ffn_w2"])
    gate = _f32(inp["L0_ffn_gate_w"])
    ws1 = _f32(inp["L0_ffn_ws1"])
    ws2 = _f32(inp["L0_ffn_ws2"])
    ws3 = _f32(inp["L0_ffn_ws3"])

    def slabify_in_out(wT):
        a = wT.reshape(KT, 128, 1024)
        out = np.empty((8, 128, KT * 128), dtype=np.float32)
        for i in range(8):
            out[i] = a[:, :, i * 128:(i + 1) * 128].transpose(1, 0, 2).reshape(128, KT * 128)
        return out

    w1S = np.stack([slabify_in_out(w1[e].T) for e in range(8)])
    w3S = np.stack([slabify_in_out(w3[e].T) for e in range(8)])
    w2S = np.stack([w2[e].T.reshape(8, 128, D) for e in range(8)])
    ws1Sl = slabify_in_out(ws1.T)
    ws3Sl = slabify_in_out(ws3.T)
    ws2Sl = ws2.T.reshape(8, 128, D)
    gTt = gate.T.reshape(KT, 128, 8).transpose(1, 0, 2).reshape(128, 128)
    return {
        "gTt": _f32(gTt),
        "w1S": _f32(w1S.reshape(8 * 8 * 128, D)),
        "w3S": _f32(w3S.reshape(8 * 8 * 128, D)),
        "w2S": _f32(w2S.reshape(8 * 8 * 128, D)),
        "ws1S": _f32(ws1Sl.reshape(8 * 128, D)),
        "ws3S": _f32(ws3Sl.reshape(8 * 128, D)),
        "ws2S": _f32(ws2Sl.reshape(8 * 128, D)),
    }


def run_moe(xT_full, pre):
    nc = build_moe()
    in_maps = []
    for c in range(NC):
        m = dict(pre)
        m["xT"] = _f32(xT_full[:, c * TC:(c + 1) * TC])
        m.update(CONSTS)
        in_maps.append(m)
    res = run_bass_kernel_spmd(nc, in_maps, core_ids=list(range(NC)))
    return np.concatenate([res.results[c]["xoT"] for c in range(NC)], axis=1)



def emit_attn0(nc, P):
    wqS, wkvS, bq_c, bkv_c, mask01 = P.wqS, P.wkvS, P.bq_c, P.bkv_c, P.mask01
    cst = P.cst
    y_src = P.y_src

    NCH = 16
    ORD = getattr(P, "ch_order", None) or list(range(NCH))
    y_gate = getattr(P, "y_gate", None)
    with ExitStack() as ex:
        sbt = lambda nm, sh, dt: ex.enter_context(nc.sbuf_tensor(nm + P.tag, sh, dt)).ap()
        pst = lambda nm, sh, dt: ex.enter_context(nc.psum_tensor(nm + P.tag, sh, dt)).ap()
        qT = [sbt("qT%d" % h, [128, T], BF16) for h in range(2)]
        kT = sbt("kT", [128, T], BF16)
        vb = sbt("vb", [128, 64 * 130], BF16)
        aoT = [sbt("aoTs%d" % h, [128, T], BF16) for h in range(2)]
        prb = sbt("prb", [128, 2 * 1024], BF16)
        ybuf = sbt("ybuf", [128, 4 * 512], BF16)
        wq_s = sbt("wq_s", [128, KT * 256], BF16)
        wkv_s = sbt("wkv_s", [128, KT * 256], BF16)
        vt_tmp = sbt("vt_tmp", [128, 2 * 512], BF16)
        sqb = sbt("sqb", [128, 2 * 512], BF16)
        bqs = sbt("bqs", [128, 4], F32)
        mrows = sbt("mrows", [1, 64], F32)
        mbias = sbt("mbias", [128, 32], F32)
        rr = sbt("rr", [128, 512], F32)
        rcb = sbt("rcb", [128, 512], BF16)
        mask_s = sbt("mask_s", [128, 896], BF16)
        ones_b = sbt("ones_b", [128, 128], BF16)
        ones_f = sbt("ones_f", [128, 128], F32)
        ident_s = sbt("ident_s", [128, 128], BF16)

        psS1 = pst("psS1", [128, 1024], F32)
        psS2 = pst("psS2", [128, 1024], F32)
        psO = pst("psO", [128, 1024], F32)
        psM = pst("psM", [128, 512], F32)
        psV = pst("psV", [128, 128], BF16)

        NG = [2 * (j + 1) for j in range(NCH)]

        sems = []
        with nc.Block() as block:
            mk = _mk_factory(nc, P.tag, sems)
            cld = mk("cld")
            wl = mk("wl", dma=True)
            yld = [mk("yld%d" % i) for i in range(4)]
            bp = mk("bp")
            qk = mk("qk")
            qev = mk("qev")
            vtr = mk("vtr")
            vbe = mk("vbe")
            sqs = mk("sqs")
            qnm = mk("qnm")
            qnr = mk("qnr")
            ngm = mk("ngm")
            nsc = mk("nsc")
            mbm = mk("mbm")
            mbe = mk("mbe")
            smm = mk("smm")
            xp = mk("xp")
            mkv = mk("mkv")
            omm = mk("omm")
            rcp = mk("rcp")
            rbm = mk("rbm")
            aoev = mk("aoev")
            stf = mk("stf")

            @block.sync
            def _(sy):
                cld.inc(sy.dma_start(out=ones_b, in_=cst.onesb), 16)
                cld.inc(sy.dma_start(out=ones_f, in_=cst.onesf), 16)
                cld.inc(sy.dma_start(out=ident_s, in_=cst.ident), 16)
                cld.inc(sy.dma_start(out=mask_s, in_=mask01), 16)
                cld.inc(sy.dma_start(out=bqs[:, 0:2], in_=bq_c), 16)
                cld.inc(sy.dma_start(out=bqs[:, 2:4], in_=bkv_c), 16)

            @block.gpsimd
            def _(gp):
                wl.inc(gp.dma_start(out=wq_s, in_=wqS), 16)
                wl.inc(gp.dma_start(out=wkv_s, in_=wkvS), 16)

            @block.sync
            def _(sy):
                for idx in range(NCH * KT):
                    si = idx // 16
                    if idx % 16 == 0 and y_gate is not None and si in y_gate:
                        sy.wait_ge(y_gate[si][0], y_gate[si][1])
                    if idx >= 4:
                        sy.wait_ge(qk.sem, idx - 3)
                    yld[idx % 4].inc(sy.dma_start(
                        out=ybuf[:, (idx % 4) * 512:(idx % 4 + 1) * 512],
                        in_=y_src(idx % 16, ORD[si])), 16)

            @block.vector
            def _(v):
                v.wait_ge(cld.sem, 96)
                v.tensor_scalar(bqs[:, 0:2], bqs[:, 0:2],
                                float(1.0 / np.sqrt(128.0)), None, ALU.mult)
                v.memset(bass.AP(vb.tensor, 64, [[64 * 130, 128], [130, 64], [1, 1]]), 1.0)
                bp.inc(v.memset(bass.AP(vb.tensor, 129,
                                        [[64 * 130, 128], [130, 64], [1, 1]]), 1.0))

            def emit_transposes(t, si, ch):
                t.wait_ge(qev.sem, 4 * si + 4)
                for k4 in range(4):
                    ti = si * 4 + k4
                    if ti >= 1:
                        t.wait_ge(vbe.sem, ti)
                    vtr.inc(t.transpose(
                        psV[:, :],
                        vt_tmp[:, (si % 2) * 512 + k4 * 128:(si % 2) * 512 + (k4 + 1) * 128],
                        ident_s))

            @block.tensor
            def _(t):
                t.wait_ge(wl.sem, 32)
                t.wait_ge(bp.sem, 1)
                for si in range(NCH):
                    if si >= 1:
                        t.wait_ge(qev.sem, 4 * si)
                    for kt in range(KT):
                        idx = si * 16 + kt
                        t.wait_ge(yld[idx % 4].sem, 16 * (idx // 4 + 1))
                        rhs = ybuf[:, (idx % 4) * 512:(idx % 4 + 1) * 512]
                        st = (kt == 0)
                        sp = (kt == KT - 1)
                        t.matmul(psS1[:, 0:512], wq_s[:, kt * 256:kt * 256 + 128],
                                 rhs, start=st, stop=sp)
                        t.matmul(psS1[:, 512:1024],
                                 wq_s[:, kt * 256 + 128:kt * 256 + 256],
                                 rhs, start=st, stop=sp)
                        t.matmul(psS2[:, 0:512], wkv_s[:, kt * 256:kt * 256 + 128],
                                 rhs, start=st, stop=sp)
                        qk.inc(t.matmul(psS2[:, 512:1024],
                                        wkv_s[:, kt * 256 + 128:kt * 256 + 256],
                                        rhs, start=st, stop=sp))
                    if si >= 1:
                        emit_transposes(t, si - 1, ORD[si - 1])
                emit_transposes(t, NCH - 1, ORD[NCH - 1])

            @block.scalar
            def _(s):
                s.wait_ge(bp.sem, 1)
                for si in range(NCH):
                    ch = ORD[si]
                    s.wait_ge(qk.sem, 16 * (si + 1))
                    qev.inc(s.activation(qT[0][:, ch * 512:(ch + 1) * 512],
                                         psS1[:, 0:512], AF.Identity,
                                         bias=bqs[:, 0:1],
                                         scale=float(1.0 / np.sqrt(128.0))))
                    qev.inc(s.activation(qT[1][:, ch * 512:(ch + 1) * 512],
                                         psS1[:, 512:1024], AF.Identity,
                                         bias=bqs[:, 1:2],
                                         scale=float(1.0 / np.sqrt(128.0))))
                    qev.inc(s.activation(kT[:, ch * 512:(ch + 1) * 512],
                                         psS2[:, 0:512], AF.Identity, bias=bqs[:, 2:3]))
                    if si >= 2:
                        s.wait_ge(vtr.sem, 4 * (si - 1))
                    qev.inc(s.activation(vt_tmp[:, (si % 2) * 512:(si % 2 + 1) * 512],
                                         psS2[:, 512:1024], AF.Identity, bias=bqs[:, 3:4]))

            @block.vector
            def _(v):
                for si in range(NCH):
                    for k4 in range(4):
                        v.wait_ge(vtr.sem, 4 * si + k4 + 1)
                        kt = ORD[si] * 4 + k4
                        v.tensor_copy(vb[:, kt * 130:kt * 130 + 64], psV[:, 0:64])
                        vbe.inc(v.tensor_copy(vb[:, kt * 130 + 65:kt * 130 + 129],
                                              psV[:, 64:128]))

            @block.scalar
            def _(s):
                for ci in range(48):
                    if ci >= 2:
                        s.wait_ge(qnm.sem, ci - 1)
                    if ci < 32:
                        src = qT[ci // 16][:, (ci % 16) * 512:((ci % 16) + 1) * 512]
                    else:
                        src = kT[:, (ci - 32) * 512:(ci - 31) * 512]
                    sqs.inc(s.activation(sqb[:, (ci % 2) * 512:(ci % 2 + 1) * 512],
                                         src, AF.Square))

            @block.tensor
            def _(t):
                for ci in range(48):
                    t.wait_ge(sqs.sem, ci + 1)
                    if ci >= 1:
                        t.wait_ge(qnr.sem, ci)
                    qnm.inc(t.matmul(psM[0:1, 0:512], ones_b[:, 0:1],
                                     sqb[:, (ci % 2) * 512:(ci % 2 + 1) * 512],
                                     start=True, stop=True))

            @block.vector
            def _(v):
                for ci in range(48):
                    v.wait_ge(qnm.sem, ci + 1)
                    qnr.inc(v.tensor_reduce(mrows[0:1, ci:ci + 1], psM[0:1, 0:512],
                                            axis=mybir.AxisListType.X, op=ALU.max))
                v.drain()
                v.tensor_reduce(mrows[0:1, 48:49], mrows[0:1, 32:48],
                                axis=mybir.AxisListType.X, op=ALU.max)
                v.drain()
                ngm.inc(v.tensor_scalar(mrows[0:1, 0:32], mrows[0:1, 0:32],
                                        mrows[0:1, 48:49], None, ALU.mult))

            @block.scalar
            def _(s):
                s.wait_ge(ngm.sem, 1)
                s.activation(mrows[0:1, 0:32], mrows[0:1, 0:32], AF.Ln)
                s.drain()
                nsc.inc(s.activation(mrows[0:1, 0:32], mrows[0:1, 0:32],
                                     AF.Exp, scale=0.5))

            @block.vector
            def _(v):
                v.wait_ge(nsc.sem, 1)
                ngm.inc(v.tensor_scalar(mrows[0:1, 0:32], mrows[0:1, 0:32],
                                        -1.0, None, ALU.mult))

            @block.tensor
            def _(t):
                t.wait_ge(ngm.sem, 2)
                mbm.inc(t.matmul(psM[:, 0:32], ones_f[0:1, :], mrows[0:1, 0:32],
                                 start=True, stop=True))

            @block.vector
            def _(v):
                v.wait_ge(mbm.sem, 1)
                mbe.inc(v.tensor_copy(mbias, psM[:, 0:32]))

            @block.tensor
            def _(t):
                t.wait_ge(mbe.sem, 1)
                t.wait_ge(vbe.sem, 4 * NCH)
                kgi = 0
                pend = [None]
                for ci in range(32):
                    h, ch = divmod(ci, NCH)

                    def emit_out0(kg, kgl, ng=NG[ch], ci=ci, h=h, ch=ch):
                        if kg == 0 and ci >= 1:
                            t.wait_ge(aoev.sem, 2 * ci)
                        t.wait_ge(xp.sem, kgl + 1)
                        if kg >= ng - 2:
                            t.wait_ge(mkv.sem, 4 * ci + 2 * (kg - (ng - 2) + 1))
                        st = (kg == 0)
                        sp = (kg == ng - 1)
                        for j in range(2):
                            kt = kg * 2 + j
                            t.matmul(psO[0:65, 0:512],
                                     vb[:, kt * 130:kt * 130 + 65],
                                     prb[:, (kgl % 2) * 1024 + j * 512:(kgl % 2) * 1024 + (j + 1) * 512],
                                     start=(st and j == 0), stop=(sp and j == 1))
                            mm = t.matmul(
                                psO[0:65, 512:1024],
                                vb[:, kt * 130 + 65:kt * 130 + 130],
                                prb[:, (kgl % 2) * 1024 + j * 512:(kgl % 2) * 1024 + (j + 1) * 512],
                                start=(st and j == 0), stop=(sp and j == 1))
                            if j == 1:
                                omm.inc(mm)

                    for kg in range(NG[ch]):
                        pgrp = [psS1, psS2][kgi % 2]
                        if kgi >= 2:
                            t.wait_ge(xp.sem, kgi - 1)
                        for j in range(2):
                            kt = kg * 2 + j
                            mm = t.matmul(pgrp[:, j * 512:(j + 1) * 512],
                                          kT[:, kt * 128:(kt + 1) * 128],
                                          qT[h][:, ch * 512:(ch + 1) * 512],
                                          start=True, stop=True)
                            if j == 1:
                                smm.inc(mm)
                        if kg == 0 and pend[0] is not None:
                            t.wait_ge(rcp.sem, pend[0] + 1)
                            rbm.inc(t.matmul(psM[0:64, 0:512], ones_f[64:65, 0:64],
                                             rr[64:65, 0:512], start=True, stop=True))
                            pend[0] = None
                        if kg >= 1:
                            emit_out0(kg - 1, kgi - 1)
                        kgi += 1
                    emit_out0(NG[ch] - 1, kgi - 1)
                    pend[0] = ci
                t.wait_ge(rcp.sem, pend[0] + 1)
                rbm.inc(t.matmul(psM[0:64, 0:512], ones_f[64:65, 0:64],
                                 rr[64:65, 0:512], start=True, stop=True))

            @block.scalar
            def _(s):
                s.wait_ge(mbe.sem, 1)
                kgi = 0
                for ci in range(32):
                    h, ch = divmod(ci, NCH)
                    for kg in range(NG[ch]):
                        s.wait_ge(smm.sem, kgi + 1)
                        if kgi >= 2:
                            s.wait_ge(omm.sem, kgi - 1)
                        pgrp = [psS1, psS2][kgi % 2]
                        xp.inc(s.activation(prb[:, (kgi % 2) * 1024:(kgi % 2 + 1) * 1024],
                                            pgrp[:, 0:1024], AF.Exp,
                                            bias=mbias[:, ci:ci + 1]))
                        kgi += 1

            @block.vector
            def _(v):
                kgi = 0
                for ci in range(32):
                    h, ch = divmod(ci, NCH)
                    for kg in range(NG[ch]):
                        if kg >= NG[ch] - 2:
                            v.wait_ge(xp.sem, kgi + 1)
                            for j in range(2):
                                kt = kg * 2 + j
                                dlt = kt * 128 - ch * 512
                                sl = 384 - dlt
                                mkv.inc(v.tensor_tensor(
                                    prb[:, (kgi % 2) * 1024 + j * 512:(kgi % 2) * 1024 + (j + 1) * 512],
                                    prb[:, (kgi % 2) * 1024 + j * 512:(kgi % 2) * 1024 + (j + 1) * 512],
                                    mask_s[:, sl:sl + 512], ALU.mult))
                        kgi += 1
                    v.wait_ge(omm.sem, kgi)
                    rcp.inc(v.reciprocal(rr[64:65, 0:512], psO[64:65, 0:512]))
                    v.wait_ge(rbm.sem, ci + 1)
                    v.tensor_copy(rcb[0:64, :], psM[0:64, 0:512])
                    v.drain()
                    v.tensor_tensor(aoT[h][0:64, ch * 512:(ch + 1) * 512],
                                    psO[0:64, 0:512], rcb[0:64, :], ALU.mult)
                    aoev.inc(v.tensor_tensor(aoT[h][64:128, ch * 512:(ch + 1) * 512],
                                             psO[0:64, 512:1024], rcb[0:64, :],
                                             ALU.mult), 2)

            @block.sync
            def _(sy):
                sy.wait_ge(aoev.sem, 64)
                if P.ao_ext is not None:
                    for h2 in range(2):
                        stf.inc(sy.dma_start(out=P.ao_ext[h2 * 128:(h2 + 1) * 128, :],
                                             in_=aoT[h2]), 16)
                    sy.wait_ge(stf.sem, 32)
                else:
                    n = 0
                    for r in range(NC):
                        for h2 in range(2):
                            stf.inc(sy.dma_start(
                                out=P.ao_in[256 * r + 128 * h2:256 * r + 128 * (h2 + 1), :],
                                in_=aoT[h2][:, r * TC:(r + 1) * TC]), 16)
                            n += 16
                    sy.wait_ge(stf.sem, n)

        _phase_close(nc, sems)


def build_attn0():
    nc = bass.Bass("TRN2", target_bir_lowering=False)
    P = NS()
    P.tag = ""
    yT = nc.dram_tensor("yT", [D, T], BF16, kind="ExternalInput").ap()
    P.y_src = lambda kt, ch: yT[kt * 128:(kt + 1) * 128, ch * 512:(ch + 1) * 512]
    P.wqS = nc.dram_tensor("wqS", [128, KT * 256], F32, kind="ExternalInput").ap()
    P.wkvS = nc.dram_tensor("wkvS", [128, KT * 256], F32, kind="ExternalInput").ap()
    P.bq_c = nc.dram_tensor("bq_c", [128, 2], F32, kind="ExternalInput").ap()
    P.bkv_c = nc.dram_tensor("bkv_c", [128, 2], F32, kind="ExternalInput").ap()
    P.mask01 = nc.dram_tensor("mask01", [128, 896], BF16, kind="ExternalInput").ap()
    P.cst = _decl_consts(nc)
    P.ao_ext = nc.dram_tensor("aoT", [256, T], BF16, kind="ExternalOutput").ap()
    P.ao_in = None
    emit_attn0(nc, P)
    return nc


def attn0_prep(inp):
    wqkv = _f32(inp["L0_attn_w_qkv"])
    bqkv = _f32(inp["L0_attn_b_qkv"])

    def tile_in_out(w):
        Fd = w.shape[1]
        return np.ascontiguousarray(
            w.reshape(KT, 128, Fd).transpose(1, 0, 2).reshape(128, KT * Fd))

    pre = {"wkvS": _f32(tile_in_out(wqkv[:, 2048:2304]))}
    bkv = np.stack([bqkv[2048:2176], bqkv[2176:2304]], axis=1)
    pre["bkv_c"] = _f32(bkv)
    pre["wqS"] = [None] * NC
    pre["bq_c"] = [None] * NC
    for c in range(NC):
        pre["wqS"][c] = _f32(tile_in_out(wqkv[:, 256 * c:256 * (c + 1)]))
        bq = bqkv[256 * c:256 * (c + 1)].reshape(2, 128).T
        pre["bq_c"][c] = _f32(bq)
    k = np.arange(128)[:, None]
    u = np.arange(896)[None, :]
    pre["mask01"] = _bf((k <= u - 384).astype(np.float32))
    return pre


def attn0_in_map(c, yT_bf, pre):
    return dict(yT=yT_bf, wqS=pre["wqS"][c], wkvS=pre["wkvS"],
                bq_c=pre["bq_c"][c], bkv_c=pre["bkv_c"], mask01=pre["mask01"],
                **CONSTS)



def emit_lnx(nc, P, rms=False):
    cst, yoT = P.cst, getattr(P, "yoT", None)
    if rms:
        wa_c, hoT = P.wa_c, getattr(P, "hoT", None)
    xs = P.xs
    xT = P.xT if P.load_x else None

    with ExitStack() as ex:
        sbt = lambda nm, sh, dt: ex.enter_context(nc.sbuf_tensor(nm + P.tag, sh, dt)).ap()
        pst = lambda nm, sh, dt: ex.enter_context(nc.psum_tensor(nm + P.tag, sh, dt)).ap()
        g = NS()
        yb = sbt("yb", [128, KT * TC], BF16)
        sqb = sbt("sqb", [128, KT * TC], BF16)
        hb = sbt("hb", [128, KT * TC], BF16) if rms else None
        g.tmp = sbt("tmp", [128, TC], F32)
        g.lnm = sbt("lnm", [128, TC], F32)
        g.lnr = sbt("lnr", [128, TC], F32)
        g.rows = sbt("rows", [1, 4 * TC], F32)
        g.onesb_s = sbt("onesb_s", [128, 128], BF16)
        g.onesf_s = sbt("onesf_s", [128, 128], F32)
        wac = sbt("wac", [128, KT], F32) if rms else None
        g.psA = pst("psA", [128, 1024], F32)
        g.psB = pst("psB", [128, 1024], F32)
        g.psC = pst("psC", [128, 1024], F32)

        sems = []
        with nc.Block() as block:
            mk = _mk_factory(nc, P.tag, sems)
            g.xld = mk("xld") if P.load_x else None
            cld = mk("cld")
            g.cld = cld
            g.scal = mk("scal")
            g.vec = mk("vec")
            g.tens = mk("tens")
            stf = mk("stf")

            @block.sync
            def _(sy):
                if P.load_x:
                    for i in range(KT):
                        g.xld.inc(sy.dma_start(out=xs[:, i * TC:(i + 1) * TC],
                                               in_=xT[i * 128:(i + 1) * 128, :]), 16)
                cld.inc(sy.dma_start(out=g.onesb_s, in_=cst.onesb), 16)
                cld.inc(sy.dma_start(out=g.onesf_s, in_=cst.onesf), 16)
                if rms:
                    cld.inc(sy.dma_start(out=wac, in_=wa_c), 16)
            g.cld_total = 48 if rms else 32

            emit_ln(block, g, xs, yb, sqb)

            if rms:
                r_sq0 = g.vec.n

                @block.vector
                def _(v):
                    for i in range(KT):
                        g.vec.inc(v.tensor_tensor(sqb[:, i * TC:(i + 1) * TC],
                                                  yb[:, i * TC:(i + 1) * TC],
                                                  yb[:, i * TC:(i + 1) * TC], ALU.mult))
                g.r_sq = g.vec.n

                @block.tensor
                def _(t):
                    last = None
                    for i in range(KT):
                        t.wait_ge(g.vec.sem, r_sq0 + i + 1)
                        for ch in range(2):
                            last = t.matmul(g.psB[0:1, ch * 512:(ch + 1) * 512],
                                            g.onesb_s[:, 0:1],
                                            sqb[:, i * TC + ch * 512:i * TC + (ch + 1) * 512],
                                            start=(i == 0), stop=(i == KT - 1))
                    g.r_mm = g.tens.inc(last)

                @block.vector
                def _(v):
                    v.wait_ge(g.tens.sem, g.r_mm)
                    var = g.rows[0:1, 0:TC]
                    v.tensor_scalar(var, g.psB[0:1, 0:TC], 1.0 / D, None, ALU.mult)
                    v.drain()
                    g.r_var = g.vec.inc(
                        v.tensor_scalar(var, var, float(LNEPS), None, ALU.add))

                @block.scalar
                def _(s):
                    s.wait_ge(g.vec.sem, g.r_var)
                    var = g.rows[0:1, 0:TC]
                    lnv = g.rows[0:1, TC:2 * TC]
                    s.activation(lnv, var, AF.Ln)
                    s.drain()
                    g.r_rstd = g.scal.inc(s.activation(var, lnv, AF.Exp, scale=-0.5))

                @block.tensor
                def _(t):
                    t.wait_ge(g.scal.sem, g.r_rstd)
                    last = None
                    for ch in range(2):
                        last = t.matmul(g.psA[:, ch * 512:(ch + 1) * 512],
                                        g.onesf_s[0:1, :],
                                        g.rows[0:1, ch * 512:(ch + 1) * 512],
                                        start=True, stop=True)
                    g.r_bc = g.tens.inc(last)

                r_h0 = g.vec.n

                @block.vector
                def _(v):
                    v.wait_ge(g.tens.sem, g.r_bc)
                    for i in range(KT):
                        g.vec.inc(v.scalar_tensor_tensor(
                            hb[:, i * TC:(i + 1) * TC], yb[:, i * TC:(i + 1) * TC],
                            wac[:, i:i + 1], g.psA[:, 0:TC], ALU.mult, ALU.mult))
                g.r_h = g.vec.n

            yoT_h = getattr(P, "yoT_halves", None)
            hoT_h = getattr(P, "hoT_halves", None)
            ag_emit = getattr(P, "ag_emit", None)
            agbuf, agdst = (hb, hoT_h) if rms and hoT_h is not None else (yb, yoT_h)

            stfH = [mk("stfA"), mk("stfB")] if agdst is not None else None

            ap0 = (r_h0 if rms else g.ln_aply0) if agdst is not None else 0

            @block.sync
            def _(sy):
                n = 0
                if agdst is not None:
                    for hf in range(2):
                        for i in range(KT):
                            if hf == 0:
                                sy.wait_ge(g.vec.sem, ap0 + i + 1)
                            stfH[hf].inc(sy.dma_start(
                                out=agdst[hf][i * 128:(i + 1) * 128, :],
                                in_=agbuf[:, i * TC + hf * 512:i * TC + (hf + 1) * 512]), 16)
                else:
                    sy.wait_ge(g.vec.sem, g.r_h if rms else g.ln_done)
                for i in range(KT):
                    if yoT is not None:
                        stf.inc(sy.dma_start(out=yoT[i * 128:(i + 1) * 128, :],
                                             in_=yb[:, i * TC:(i + 1) * TC]), 16)
                        n += 16
                    if rms and hoT is not None:
                        stf.inc(sy.dma_start(out=hoT[i * 128:(i + 1) * 128, :],
                                             in_=hb[:, i * TC:(i + 1) * TC]), 16)
                        n += 16
                if stfH is not None:
                    sy.wait_ge(stfH[0].sem, 16 * KT)
                    sy.wait_ge(stfH[1].sem, 16 * KT)
                if n:
                    sy.wait_ge(stf.sem, n)

            if ag_emit is not None:
                @block.gpsimd
                def _(gp):
                    gp.wait_ge(stfH[0].sem, 16 * KT)
                    ag_emit(gp, 0)
                    gp.wait_ge(stfH[1].sem, 16 * KT)
                    ag_emit(gp, 1)

        _phase_close(nc, sems)


def build_lnx(rms=False):
    nc = bass.Bass("TRN2", target_bir_lowering=False)
    P = NS()
    P.tag = ""
    P.load_x = True
    P.xT = nc.dram_tensor("xT", [D, TC], F32, kind="ExternalInput").ap()
    P.cst = _decl_consts(nc)
    P.yoT = nc.dram_tensor("yoT", [D, TC], BF16, kind="ExternalOutput").ap()
    if rms:
        P.wa_c = nc.dram_tensor("wa_c", [128, KT], F32, kind="ExternalInput").ap()
        P.hoT = nc.dram_tensor("hoT", [D, TC], BF16, kind="ExternalOutput").ap()
    with ExitStack() as gx:
        P.xs = gx.enter_context(nc.sbuf_tensor("xs_g", [128, KT * TC], F32)).ap()
        emit_lnx(nc, P, rms=rms)
    return nc


def emit_proj(nc, P):
    aT, wf, bf_c = P.aT, P.wf, P.bf_c
    xs = P.xs
    xT = P.xT if P.load_x else None
    xoT = P.xoT if P.store_x else None

    with ExitStack() as ex:
        sbt = lambda nm, sh, dt: ex.enter_context(nc.sbuf_tensor(nm + P.tag, sh, dt)).ap()
        pst = lambda nm, sh, dt: ex.enter_context(nc.psum_tensor(nm + P.tag, sh, dt)).ap()
        asb = sbt("asb", [128, KT * TC], BF16)
        wfb = sbt("wfb", [128, KT * D], BF16)
        bfc = sbt("bfc", [128, KT], F32)
        psA = pst("psA", [128, 1024], F32)
        psB = pst("psB", [128, 1024], F32)
        psC = pst("psC", [128, 1024], F32)
        psD = pst("psD", [128, 1024], F32)
        PB = [lambda a=a, c=c: [psA, psB, psC, psD][a][:, c * 512:(c + 1) * 512]
              for a in range(4) for c in range(2)]

        sems = []
        with nc.Block() as block:
            mk = _mk_factory(nc, P.tag, sems)
            xld = mk("xld") if P.load_x else None
            ald = mk("ald")
            wld = mk("wld", dma=True)
            cld = mk("cld")
            pmm = mk("pmm")
            pev = mk("pev")
            stf = mk("stf")

            @block.sync
            def _(sy):
                if P.load_x:
                    for i in range(KT):
                        xld.inc(sy.dma_start(out=xs[:, i * TC:(i + 1) * TC],
                                             in_=xT[i * 128:(i + 1) * 128, :]), 16)
                cld.inc(sy.dma_start(out=bfc, in_=bf_c), 16)
                if getattr(P, "a_gate", None) is not None:
                    sy.wait_ge(P.a_gate[0], P.a_gate[1])
                for i in range(KT):
                    ald.inc(sy.dma_start(out=asb[:, i * TC:(i + 1) * TC],
                                         in_=aT[i * 128:(i + 1) * 128, :]), 16)

            @block.gpsimd
            def _(gp):
                for kt in range(KT):
                    wld.inc(gp.dma_start(out=wfb[:, kt * D:(kt + 1) * D],
                                         in_=wf[kt * 128:(kt + 1) * 128, :]), 16)

            @block.tensor
            def _(t):
                t.wait_ge(ald.sem, 16 * KT)
                t.wait_ge(wld.sem, 16 * KT)
                c = 0
                for d in range(KT):
                    for ch in range(2):
                        if c >= 8:
                            t.wait_ge(pev.sem, c - 7)
                        last = None
                        for kt in range(KT):
                            last = t.matmul(PB[c % 8](),
                                            wfb[:, kt * D + d * 128:kt * D + (d + 1) * 128],
                                            asb[:, kt * TC + ch * 512:kt * TC + (ch + 1) * 512],
                                            start=(kt == 0), stop=(kt == KT - 1))
                        pmm.inc(last)
                        c += 1

            @block.vector
            def _(v):
                if P.load_x:
                    v.wait_ge(xld.sem, 16 * KT)
                v.wait_ge(cld.sem, 16)
                c = 0
                for d in range(KT):
                    for ch in range(2):
                        v.wait_ge(pmm.sem, c + 1)
                        dst = xs[:, d * TC + ch * 512:d * TC + (ch + 1) * 512]
                        pev.inc(v.scalar_tensor_tensor(dst, PB[c % 8](),
                                                       bfc[:, d:d + 1],
                                                       dst, ALU.add, ALU.add))
                        c += 1

            if P.store_x:
                @block.sync
                def _(sy):
                    sy.wait_ge(pev.sem, 32)
                    for d in range(KT):
                        stf.inc(sy.dma_start(out=xoT[d * 128:(d + 1) * 128, :],
                                             in_=xs[:, d * TC:(d + 1) * TC]), 16)
                    sy.wait_ge(stf.sem, 16 * KT)

        _phase_close(nc, sems)


def build_proj():
    nc = bass.Bass("TRN2", target_bir_lowering=False)
    P = NS()
    P.tag = ""
    P.load_x = True
    P.store_x = True
    P.xT = nc.dram_tensor("xT", [D, TC], F32, kind="ExternalInput").ap()
    P.aT = nc.dram_tensor("aT", [D, TC], BF16, kind="ExternalInput").ap()
    P.wf = nc.dram_tensor("wf", [D, D], F32, kind="ExternalInput").ap()
    P.bf_c = nc.dram_tensor("bf_c", [128, KT], F32, kind="ExternalInput").ap()
    P.xoT = nc.dram_tensor("xoT", [D, TC], F32, kind="ExternalOutput").ap()
    with ExitStack() as gx:
        P.xs = gx.enter_context(nc.sbuf_tensor("xs_g", [128, KT * TC], F32)).ap()
        emit_proj(nc, P)
    return nc


def emit_fold(nc, P):
    xT_pre = getattr(P, "xT_pre", None)
    wup, wgt, wdn = P.wup, P.wgt, P.wdn
    su_ug, su_dp = P.su_ug, P.su_dp
    wsc_up, wsc_gt, wsc_dn = P.wsc_up, P.wsc_gt, P.wsc_dn
    h128, cst = P.h128, P.cst
    a_up, a_gt, a_dn = P.a_up, P.a_gt, P.a_dn

    with ExitStack() as ex:
        sbt = lambda nm, sh, dt: ex.enter_context(nc.sbuf_tensor(nm + P.tag, sh, dt)).ap()
        pst = lambda nm, sh, dt: ex.enter_context(nc.psum_tensor(nm + P.tag, sh, dt)).ap()
        W = sbt("W", [128, 64 * 256], BF16)
        B1 = sbt("B1", [128, 64 * 256], BF16)
        B2 = sbt("B2", [128, 64 * 256], BF16)
        ob = sbt("ob", [128, 64 * 256], BF16)
        suc = sbt("suc", [128, 64 + 16], F32)
        wsc = sbt("wsc", [1, 1024], F32)
        h128s = sbt("h128s", [128, 128], BF16)
        onesf_s = sbt("onesf_s", [128, 128], F32)
        psA = pst("psA", [128, 1024], F32)
        psB = pst("psB", [128, 1024], F32)

        specs = [
            ("up", wup, 16, 1024, a_up, (0, 16), wsc_up, 1024, 4, float(2.0 ** -5.5)),
            ("gt", wgt, 16, 1024, a_gt, (0, 16), wsc_gt, 1024, 4, float(2.0 ** -5.5)),
            ("dn", wdn, 64, 256, a_dn, (16, 80), wsc_dn, 256, 6, float(2.0 ** -6.5)),
        ]

        sems = []
        with nc.Block() as block:
            mk = _mk_factory(nc, P.tag, sems)
            cld = mk("cld")
            wscld = mk("wscld")
            wl = [mk("wl%d" % i, dma=True) for i in range(3)]
            s1 = mk("s1")
            e1 = mk("e1")
            e1s = mk("e1s")
            bfl = mk("bfl")
            scd = mk("scd")
            bcm = mk("bcm")
            stf = mk("stf")

            @block.sync
            def _(sy):
                cld.inc(sy.dma_start(out=h128s, in_=h128), 16)
                cld.inc(sy.dma_start(out=onesf_s, in_=cst.onesf), 16)
                cld.inc(sy.dma_start(out=suc[:, 0:16], in_=su_ug), 16)
                cld.inc(sy.dma_start(out=suc[:, 16:80], in_=su_dp), 16)

            if xT_pre is not None:
                xld = mk("xld", dma=True)
                P.xld_out = xld

                @block.sync
                def _(sy):
                    for i in range(KT):
                        xld.inc(sy.dma_start(out=P.xs[:, i * TC:(i + 1) * TC],
                                             in_=xT_pre[i * 128:(i + 1) * 128, :]), 16)

            STF_TOT = [0, 16 * 8, 16 * 16, 16 * 20]

            @block.gpsimd
            def _(gp):
                srcs = [wup, wgt, wdn]
                for mi, (nm, w_ap, ntile, width, *_rest) in enumerate(specs):
                    if mi > 0:
                        gp.wait_ge(stf.sem, STF_TOT[mi])
                    for i in range(ntile):
                        wl[mi].inc(gp.dma_start(
                            out=W[:, i * width:(i + 1) * width],
                            in_=bass.AP(srcs[mi].tensor, i * 128 * width,
                                        [[width, 128], [1, width]])), 16)

            @block.vector
            def _(v):
                v.wait_ge(cld.sem, 64)
                v.tensor_scalar(suc[:, 0:16], suc[:, 0:16], float(2.0 ** -5.5),
                                None, ALU.mult)
                v.tensor_scalar(suc[:, 16:80], suc[:, 16:80], float(2.0 ** -6.5),
                                None, ALU.mult)
                v.drain()

            mtile_base = 0
            for mi, (nm, w_ap, ntile, width, out_ap, su_sl, wsc_ap, wsc_w,
                     nstage, smul) in enumerate(specs):

                @block.sync
                def _(sy, mi=mi, wsc_ap=wsc_ap, wsc_w=wsc_w):
                    if mi >= 1:
                        sy.wait_ge(bcm.sem, mi)
                    wscld.inc(sy.dma_start(out=wsc[0:1, 0:wsc_w], in_=wsc_ap), 16)

                @block.tensor
                def _(t, mi=mi, ntile=ntile, width=width):
                    t.wait_ge(cld.sem, 64)
                    t.wait_ge(wl[mi].sem, 16 * ntile)
                    if mi > 0:
                        t.wait_ge(scd.sem, sum(sp[2] for sp in specs[:mi]))
                    nch = width // 512 if width >= 512 else 1
                    eb_v = sum((sp[2] + 1) // 2 for sp in specs[:mi])
                    eb_s = sum(sp[2] // 2 for sp in specs[:mi])
                    for i in range(ntile):
                        if i >= 2:
                            j = i - 2
                            if j % 2 == 0:
                                t.wait_ge(e1.sem, eb_v + j // 2 + 1)
                            else:
                                t.wait_ge(e1s.sem, eb_s + j // 2 + 1)
                        last = None
                        if width >= 512:
                            for ch in range(width // 512):
                                last = t.matmul(
                                    [psA, psB][i % 2][:, ch * 512:(ch + 1) * 512],
                                    h128s,
                                    W[:, i * width + ch * 512:i * width + (ch + 1) * 512],
                                    start=True, stop=True)
                        else:
                            last = t.matmul([psA, psB][i % 2][:, 0:width], h128s,
                                            W[:, i * width:(i + 1) * width],
                                            start=True, stop=True)
                        s1.inc(last)

                @block.vector
                def _(v, mi=mi, ntile=ntile, width=width):
                    for i in range(0, ntile, 2):
                        v.wait_ge(s1.sem, mtile_base + i + 1)
                        e1.inc(v.tensor_copy(B1[:, i * width:(i + 1) * width],
                                             [psA, psB][i % 2][:, 0:width]))

                @block.scalar
                def _(sc, mi=mi, ntile=ntile, width=width):
                    for i in range(1, ntile, 2):
                        sc.wait_ge(s1.sem, mtile_base + i + 1)
                        e1s.inc(sc.activation(B1[:, i * width:(i + 1) * width],
                                              [psA, psB][i % 2][:, 0:width],
                                              AF.Copy))

                @block.vector
                def _(v, mi=mi, ntile=ntile, width=width, nstage=nstage):
                    v.wait_ge(e1.sem, sum((sp[2] + 1) // 2 for sp in specs[:mi + 1]))
                    v.wait_ge(e1s.sem, sum(sp[2] // 2 for sp in specs[:mi + 1]))
                    v.drain()
                    bufs = [B1, B2]
                    FW = 64 * 256
                    last = None
                    for st in range(nstage):
                        src, dst = bufs[st % 2], bufs[(st + 1) % 2]
                        hh = 1 << st
                        ng2 = ntile // (2 * hh)
                        blkw = hh * width
                        lo_o = bass.AP(dst.tensor, 0, [[FW, 128], [2 * blkw, ng2], [1, blkw]])
                        lo_a = bass.AP(src.tensor, 0, [[FW, 128], [2 * blkw, ng2], [1, blkw]])
                        lo_b = bass.AP(src.tensor, blkw, [[FW, 128], [2 * blkw, ng2], [1, blkw]])
                        hi_o = bass.AP(dst.tensor, blkw, [[FW, 128], [2 * blkw, ng2], [1, blkw]])
                        v.tensor_tensor(lo_o, lo_a, lo_b, ALU.add)
                        last = v.tensor_tensor(hi_o, lo_a, lo_b, ALU.subtract)
                        if st < nstage - 1:
                            v.drain()
                    bfl.inc(last)

                @block.tensor
                def _(t, mi=mi, wsc_w=wsc_w):
                    t.wait_ge(wscld.sem, 16 * (mi + 1))
                    t.wait_ge(bfl.sem, mi + 1)
                    last = None
                    for ch in range((wsc_w + 511) // 512):
                        wdt = min(512, wsc_w - ch * 512)
                        last = t.matmul(psA[:, ch * 512:ch * 512 + wdt],
                                        onesf_s[0:1, :],
                                        wsc[0:1, ch * 512:ch * 512 + wdt],
                                        start=True, stop=True)
                    bcm.inc(last)

                @block.vector
                def _(v, mi=mi, ntile=ntile, width=width, su_sl=su_sl, smul=smul,
                      nstage=nstage):
                    v.wait_ge(bcm.sem, mi + 1)
                    src = [B1, B2][nstage % 2]
                    last = None
                    for i in range(ntile):
                        if mi < 2:
                            dst = bass.AP(ob.tensor, i * 128,
                                          [[64 * 256, 128], [16 * 128, 8], [1, 128]])
                        else:
                            dst = ob[:, i * width:(i + 1) * width]
                        last = v.scalar_tensor_tensor(
                            dst,
                            src[:, i * width:(i + 1) * width],
                            suc[:, su_sl[0] + i:su_sl[0] + i + 1],
                            psA[:, 0:width], ALU.mult, ALU.mult)
                        scd.inc(last)

                @block.sync
                def _(sy, mi=mi, ntile=ntile, width=width, out_ap=out_ap):
                    sy.wait_ge(scd.sem, sum(sp[2] for sp in specs[:mi + 1]))
                    nslab = 8 if mi < 2 else 4
                    w2 = 2048 if mi < 2 else 4096
                    for o in range(nslab):
                        stf.inc(sy.dma_start(
                            out=bass.AP(out_ap.tensor, o * 128 * w2,
                                        [[w2, 128], [1, w2]]),
                            in_=ob[:, o * w2:(o + 1) * w2]), 16)

                mtile_base += ntile

            @block.sync
            def _(sy):
                sy.wait_ge(stf.sem, STF_TOT[3])

        _phase_close(nc, sems)


def _fold_drams(nc, P):
    P.wup = nc.dram_tensor("wup", [D, 1024], F32, kind="ExternalInput").ap()
    P.wgt = nc.dram_tensor("wgt", [D, 1024], F32, kind="ExternalInput").ap()
    P.wdn = nc.dram_tensor("wdn", [8192, 256], F32, kind="ExternalInput").ap()
    P.su_ug = nc.dram_tensor("su_ug", [128, 16], F32, kind="ExternalInput").ap()
    P.su_dp = nc.dram_tensor("su_dp", [128, 64], F32, kind="ExternalInput").ap()
    P.wsc_up = nc.dram_tensor("wsc_up", [1, 1024], F32, kind="ExternalInput").ap()
    P.wsc_gt = nc.dram_tensor("wsc_gt", [1, 1024], F32, kind="ExternalInput").ap()
    P.wsc_dn = nc.dram_tensor("wsc_dn", [1, 256], F32, kind="ExternalInput").ap()
    P.h128 = nc.dram_tensor("h128", [128, 128], BF16, kind="ExternalInput").ap()


def build_fold():
    nc = bass.Bass("TRN2", target_bir_lowering=False)
    P = NS()
    P.tag = ""
    _fold_drams(nc, P)
    P.cst = _decl_consts(nc)
    P.a_up = nc.dram_tensor("a_up", [8 * 128, D], BF16, kind="ExternalOutput").ap()
    P.a_gt = nc.dram_tensor("a_gt", [8 * 128, D], BF16, kind="ExternalOutput").ap()
    P.a_dn = nc.dram_tensor("a_dn", [4 * 128, 4096], BF16, kind="ExternalOutput").ap()
    emit_fold(nc, P)
    return nc


def emit_swiglu(nc, P):
    auS, agS, adS, cst = P.auS, P.agS, P.adS, P.cst
    xs = P.xs
    xT = P.xT if P.load_x else None
    xoT = P.xoT if P.store_x else None

    with ExitStack() as ex:
        sbt = lambda nm, sh, dt: ex.enter_context(nc.sbuf_tensor(nm + P.tag, sh, dt)).ap()
        pst = lambda nm, sh, dt: ex.enter_context(nc.psum_tensor(nm + P.tag, sh, dt)).ap()
        g = NS()
        yb = sbt("yb", [128, KT * TC], BF16)
        hb = sbt("hb", [128, 16 * TC], BF16)
        wub = sbt("wub", [128, 4 * D], BF16)
        adb = sbt("adb", [128, 2 * 16 * 256], BF16)
        sqb = sbt("sqb", [128, 2 * TC], BF16)
        tmpbf = sbt("tmpbf", [128, 2 * 512], BF16)
        g.tmp = sbt("tmp", [128, TC], F32)
        g.lnm = sbt("lnm", [128, TC], F32)
        g.lnr = sbt("lnr", [128, TC], F32)
        g.rows = sbt("rows", [1, 4 * TC], F32)
        g.onesb_s = sbt("onesb_s", [128, 128], BF16)
        g.onesf_s = sbt("onesf_s", [128, 128], F32)
        g.psA = pst("psA", [128, 1024], F32)
        g.psB = pst("psB", [128, 1024], F32)
        g.psC = pst("psC", [128, 1024], F32)
        psD = pst("psD", [128, 1024], F32)

        sems = []
        with nc.Block() as block:
            mk = _mk_factory(nc, P.tag, sems)
            g.xld = mk("xld") if P.load_x else None
            cld = mk("cld")
            g.cld = cld
            g.scal = mk("scal")
            g.vec = mk("vec")
            g.tens = mk("tens")
            wls = [mk("wls%d" % i) for i in range(2)]
            adl = [mk("adl%d" % i, dma=True) for i in range(2)]
            umm = mk("umm")
            sil = mk("sil")
            hev = mk("hev")
            dmm = mk("dmm")
            dev = mk("dev")
            stf = mk("stf")

            @block.sync
            def _(sy):
                if P.load_x:
                    for i in range(KT):
                        g.xld.inc(sy.dma_start(out=xs[:, i * TC:(i + 1) * TC],
                                               in_=xT[i * 128:(i + 1) * 128, :]), 16)
                cld.inc(sy.dma_start(out=g.onesb_s, in_=cst.onesb), 16)
                cld.inc(sy.dma_start(out=g.onesf_s, in_=cst.onesf), 16)
            g.cld_total = 32

            emit_ln(block, g, xs, yb, hb)

            @block.sync
            def _(sy):
                for o in range(64):
                    if o >= 2:
                        sy.wait_ge(umm.sem, 2 * (o - 1))
                    wls[o % 2].inc(sy.dma_start(
                        out=wub[:, (o % 2) * 2 * D:(o % 2) * 2 * D + D],
                        in_=bass.AP(auS.tensor, o * 128 * D, [[D, 128], [1, D]])), 16)
                    wls[o % 2].inc(sy.dma_start(
                        out=wub[:, (o % 2) * 2 * D + D:(o % 2) * 2 * D + 2 * D],
                        in_=bass.AP(agS.tensor, o * 128 * D, [[D, 128], [1, D]])), 16)

            @block.gpsimd
            def _(gp):
                for q in range(4):
                    for dg in range(8):
                        ci = q * 8 + dg
                        if ci >= 2:
                            gp.wait_ge(dmm.sem, 4 * (ci - 1))
                        adl[ci % 2].inc(gp.dma_start(
                            out=adb[:, (ci % 2) * 16 * 256:(ci % 2 + 1) * 16 * 256],
                            in_=bass.AP(adS.tensor, (dg * 512 + q * 128) * 4096,
                                        [[4096, 128], [1, 4096]])), 16)

            @block.tensor
            def _(t):
                ucc = 0
                dcc = 0
                for q in range(4):
                    if q >= 1:
                        t.wait_ge(dev.sem, 32 * q)
                    for oi in range(16):
                        o = q * 16 + oi
                        t.wait_ge(wls[o % 2].sem, 32 * (o // 2 + 1))
                        for ch in range(2):
                            slot = [g.psA, g.psB, g.psC][ucc % 3]
                            if ucc >= 3:
                                t.wait_ge(hev.sem, ucc - 2)
                            for kt in range(KT):
                                if ucc == 0:
                                    t.wait_ge(g.vec.sem, g.ln_aply0 + kt + 1)
                                t.matmul(slot[:, 0:512],
                                         wub[:, (o % 2) * 2 * D + kt * 128:(o % 2) * 2 * D + (kt + 1) * 128],
                                         yb[:, kt * TC + ch * 512:kt * TC + (ch + 1) * 512],
                                         start=(kt == 0), stop=(kt == KT - 1))
                            last = None
                            for kt in range(KT):
                                last = t.matmul(
                                    slot[:, 512:1024],
                                    wub[:, (o % 2) * 2 * D + D + kt * 128:(o % 2) * 2 * D + D + (kt + 1) * 128],
                                    yb[:, kt * TC + ch * 512:kt * TC + (ch + 1) * 512],
                                    start=(kt == 0), stop=(kt == KT - 1))
                            umm.inc(last)
                            ucc += 1
                    t.wait_ge(hev.sem, 32 * (q + 1))
                    for dg in range(8):
                        ci = q * 8 + dg
                        t.wait_ge(adl[ci % 2].sem, 16 * (ci // 2 + 1))
                        for dd in range(2):
                            d = dg * 2 + dd
                            for ch in range(2):
                                pt = [g.psA[:, 0:512], g.psA[:, 512:1024],
                                      g.psB[:, 0:512], g.psB[:, 512:1024],
                                      g.psC[:, 0:512], g.psC[:, 512:1024],
                                      psD[:, 0:512], psD[:, 512:1024]][dcc % 8]
                                if dcc >= 8:
                                    t.wait_ge(dev.sem, dcc - 7)
                                last = None
                                for kt in range(16):
                                    last = t.matmul(
                                        pt,
                                        adb[:, (ci % 2) * 16 * 256 + kt * 256 + dd * 128:(ci % 2) * 16 * 256 + kt * 256 + dd * 128 + 128],
                                        hb[:, kt * TC + ch * 512:kt * TC + (ch + 1) * 512],
                                        start=(kt == 0), stop=(kt == 15))
                                dmm.inc(last)
                                dcc += 1

            @block.scalar
            def _(s):
                ucc = 0
                for q in range(4):
                    for oi in range(16):
                        for ch in range(2):
                            s.wait_ge(umm.sem, ucc + 1)
                            if ucc >= 2:
                                s.wait_ge(hev.sem, ucc - 1)
                            slot = [g.psA, g.psB, g.psC][ucc % 3]
                            sil.inc(s.activation(
                                tmpbf[:, (ucc % 2) * 512:(ucc % 2 + 1) * 512],
                                slot[:, 512:1024], AF.Silu))
                            ucc += 1

            @block.vector
            def _(v):
                ucc = 0
                dcc = 0
                for q in range(4):
                    for oi in range(16):
                        for ch in range(2):
                            v.wait_ge(sil.sem, ucc + 1)
                            slot = [g.psA, g.psB, g.psC][ucc % 3]
                            hev.inc(v.tensor_tensor(
                                hb[:, oi * TC + ch * 512:oi * TC + (ch + 1) * 512],
                                tmpbf[:, (ucc % 2) * 512:(ucc % 2 + 1) * 512],
                                slot[:, 0:512], ALU.mult))
                            ucc += 1
                    for dg in range(8):
                        for dd in range(2):
                            d = dg * 2 + dd
                            for ch in range(2):
                                v.wait_ge(dmm.sem, dcc + 1)
                                pt = [g.psA[:, 0:512], g.psA[:, 512:1024],
                                      g.psB[:, 0:512], g.psB[:, 512:1024],
                                      g.psC[:, 0:512], g.psC[:, 512:1024],
                                      psD[:, 0:512], psD[:, 512:1024]][dcc % 8]
                                dst = xs[:, d * TC + ch * 512:d * TC + (ch + 1) * 512]
                                dev.inc(v.scalar_tensor_tensor(
                                    dst, pt, 0.5, dst, ALU.mult, ALU.add))
                                dcc += 1

            if P.store_x:
                @block.sync
                def _(sy):
                    sy.wait_ge(dev.sem, 128)
                    for d in range(KT):
                        stf.inc(sy.dma_start(out=xoT[d * 128:(d + 1) * 128, :],
                                             in_=xs[:, d * TC:(d + 1) * TC]), 16)
                    sy.wait_ge(stf.sem, 16 * KT)

        _phase_close(nc, sems)


def build_swiglu():
    nc = bass.Bass("TRN2", target_bir_lowering=False)
    P = NS()
    P.tag = ""
    P.load_x = True
    P.store_x = True
    P.xT = nc.dram_tensor("xT", [D, TC], F32, kind="ExternalInput").ap()
    P.auS = nc.dram_tensor("auS", [64 * 128, D], BF16, kind="ExternalInput").ap()
    P.agS = nc.dram_tensor("agS", [64 * 128, D], BF16, kind="ExternalInput").ap()
    P.adS = nc.dram_tensor("adS", [65536, 256], BF16, kind="ExternalInput").ap()
    P.cst = _decl_consts(nc)
    P.xoT = nc.dram_tensor("xoT", [D, TC], F32, kind="ExternalOutput").ap()
    with ExitStack() as gx:
        P.xs = gx.enter_context(nc.sbuf_tensor("xs_g", [128, KT * TC], F32)).ap()
        emit_swiglu(nc, P)
    return nc



def emit_attn1(nc, P):
    wS, cosT_d, sinT_d, pt_d, mask01 = P.wS, P.cosT_d, P.sinT_d, P.ptm_d, P.mask01
    cst = P.cst
    y_src = P.y_src

    NCH = 16
    ORD = getattr(P, "ch_order", None) or list(range(NCH))
    y_gate = getattr(P, "y_gate", None)
    with ExitStack() as ex:
        sbt = lambda nm, sh, dt: ex.enter_context(nc.sbuf_tensor(nm + P.tag, sh, dt)).ap()
        pst = lambda nm, sh, dt: ex.enter_context(nc.psum_tensor(nm + P.tag, sh, dt)).ap()
        qT = [sbt("qT%d" % i, [128, T], BF16) for i in range(2)]
        kT = [sbt("kTt%d" % i, [128, T], BF16) for i in range(2)]
        vb = sbt("vb", [128, 2 * 64 * 130], BF16)
        cbuf = sbt("cbuf", [128, 2 * 1024], BF16)
        prb = sbt("prb", [128, 2 * 1024], BF16)
        ybuf = sbt("ybuf", [128, 4 * 512], BF16)
        w_s = sbt("w_s", [128, KT * 768], BF16)
        vt_tmp = sbt("vt_tmp", [128, 2 * 1024], BF16)
        rp2 = sbt("rp2", [128, 512], F32)
        mrows = sbt("mrows", [1, 160], F32)
        mbias = sbt("mbias", [128, 64], F32)
        rr = sbt("rr", [128, 512], F32)
        rp1 = rr
        sqb = prb
        rcb = cbuf[:, 0:512]
        mask_s = cbuf[:, 1024:1920]
        ptm_s = sbt("ptm_s", [128, 128], BF16)
        ones_b = sbt("ones_b", [128, 128], BF16)
        ones_f = sbt("ones_f", [128, 128], F32)
        ident_s = sbt("ident_s", [128, 128], BF16)

        psA = pst("psA", [128, 1024], F32)
        psB = pst("psB", [128, 1024], F32)
        psC = pst("psC", [128, 1024], F32)
        psD = pst("psD", [128, 512], F32)
        psV = pst("psV", [128, 512], BF16)

        sems = []
        with nc.Block() as block:
            mk = _mk_factory(nc, P.tag, sems)
            cld = mk("cld")
            wl = mk("wl", dma=True)
            yld = [mk("yld%d" % i) for i in range(4)]
            qk = mk("qk")
            qev = mk("qev")
            rpm = mk("rpm")
            rpe = mk("rpe")
            vtr = mk("vtr")
            vbe = mk("vbe")
            sqs = mk("sqs")
            qnm = mk("qnm")
            qnr = mk("qnr")
            ngm = mk("ngm")
            nsc = mk("nsc")
            mbm = mk("mbm")
            mbe = mk("mbe")
            smm = mk("smm")
            xp = mk("xp")
            mkv = mk("mkv")
            omm = mk("omm")
            rcp = mk("rcp")
            rbm = mk("rbm")
            aoev = mk("aoev")
            stf = mk("stf")

            cls = [mk("cls%d" % i, dma=True) for i in range(2)]
            mld = mk("mld", dma=True)

            @block.sync
            def _(sy):
                cld.inc(sy.dma_start(out=ones_b, in_=cst.onesb), 16)
                cld.inc(sy.dma_start(out=ones_f, in_=cst.onesf), 16)
                cld.inc(sy.dma_start(out=ident_s, in_=cst.ident), 16)
                cld.inc(sy.dma_start(out=ptm_s, in_=pt_d), 16)

            @block.gpsimd
            def _(gp):
                wl.inc(gp.dma_start(out=w_s, in_=wS), 16)
                for pc in range(NCH):
                    ch = ORD[pc]
                    if pc >= 2:
                        gp.wait_ge(rpe.sem, 4 * (pc - 1))
                    cls[pc % 2].inc(gp.dma_start(
                        out=cbuf[:, (pc % 2) * 1024:(pc % 2) * 1024 + 512],
                        in_=cosT_d[:, ch * 512:(ch + 1) * 512]), 16)
                    cls[pc % 2].inc(gp.dma_start(
                        out=cbuf[:, (pc % 2) * 1024 + 512:(pc % 2 + 1) * 1024],
                        in_=sinT_d[:, ch * 512:(ch + 1) * 512]), 16)
                gp.wait_ge(rpe.sem, 64)
                mld.inc(gp.dma_start(out=mask_s, in_=mask01), 16)

            @block.sync
            def _(sy):
                for idx in range(NCH * KT):
                    si = idx // 16
                    if idx % 16 == 0 and y_gate is not None and si in y_gate:
                        sy.wait_ge(y_gate[si][0], y_gate[si][1])
                    if idx >= 4:
                        sy.wait_ge(qk.sem, idx - 3)
                    yld[idx % 4].inc(sy.dma_start(
                        out=ybuf[:, (idx % 4) * 512:(idx % 4 + 1) * 512],
                        in_=y_src(idx % 16, ORD[si])), 16)

            RTILES = [qT[0], qT[1], kT[0], kT[1]]

            def emit_rope_mm(t, si, ch):
                for tile in range(4):
                    ri = si * 4 + tile
                    if ri >= 1:
                        t.wait_ge(rpe.sem, ri)
                    rpm.inc(t.matmul(psD[:, 0:512], ptm_s,
                                     RTILES[tile][:, ch * 512:(ch + 1) * 512],
                                     start=True, stop=True))

            def emit_vtr(t, si):
                for vt2 in range(2):
                    for k4 in range(4):
                        ti = si * 8 + vt2 * 4 + k4
                        if ti >= 1:
                            t.wait_ge(vbe.sem, ti)
                        vtr.inc(t.transpose(
                            psV[:, 0:128],
                            vt_tmp[:, (si % 2) * 1024 + vt2 * 512 + k4 * 128:(si % 2) * 1024 + vt2 * 512 + (k4 + 1) * 128],
                            ident_s))

            @block.tensor
            def _(t):
                t.wait_ge(wl.sem, 16)
                t.wait_ge(cld.sem, 64)
                for si in range(NCH):
                    if si >= 1:
                        t.wait_ge(qev.sem, 6 * si)
                    for kt in range(KT):
                        idx = si * 16 + kt
                        t.wait_ge(yld[idx % 4].sem, 16 * (idx // 4 + 1))
                        rhs = ybuf[:, (idx % 4) * 512:(idx % 4 + 1) * 512]
                        st = (kt == 0)
                        sp = (kt == KT - 1)
                        dsts = [psA[:, 0:512], psA[:, 512:1024],
                                psB[:, 0:512], psB[:, 512:1024],
                                psC[:, 0:512], psC[:, 512:1024]]
                        for f in range(6):
                            mm = t.matmul(dsts[f],
                                          w_s[:, kt * 768 + f * 128:kt * 768 + (f + 1) * 128],
                                          rhs, start=st, stop=sp)
                        qk.inc(mm)
                    if si >= 1:
                        emit_rope_mm(t, si - 1, ORD[si - 1])
                        emit_vtr(t, si - 1)
                t.wait_ge(qev.sem, 6 * NCH)
                emit_rope_mm(t, NCH - 1, ORD[NCH - 1])
                emit_vtr(t, NCH - 1)

            @block.scalar
            def _(s):
                for si in range(NCH):
                    ch = ORD[si]
                    s.wait_ge(qk.sem, 16 * (si + 1))
                    srcs = [psA[:, 0:512], psA[:, 512:1024],
                            psB[:, 0:512], psB[:, 512:1024],
                            psC[:, 0:512], psC[:, 512:1024]]
                    qev.inc(s.activation(qT[0][:, ch * 512:(ch + 1) * 512], srcs[0],
                                         AF.Copy, scale=0.125))
                    qev.inc(s.activation(qT[1][:, ch * 512:(ch + 1) * 512], srcs[1],
                                         AF.Copy, scale=0.125))
                    qev.inc(s.activation(kT[0][:, ch * 512:(ch + 1) * 512], srcs[2],
                                         AF.Copy))
                    qev.inc(s.activation(kT[1][:, ch * 512:(ch + 1) * 512], srcs[3],
                                         AF.Copy))
                    if si >= 2:
                        s.wait_ge(vtr.sem, 8 * (si - 1))
                    qev.inc(s.activation(vt_tmp[:, (si % 2) * 1024:(si % 2) * 1024 + 512],
                                         srcs[4], AF.Copy))
                    qev.inc(s.activation(
                        vt_tmp[:, (si % 2) * 1024 + 512:(si % 2 + 1) * 1024],
                        srcs[5], AF.Copy))

            @block.vector
            def _(v):
                v.wait_ge(cld.sem, 64)
                for pc in range(NCH):
                    ch = ORD[pc]
                    v.wait_ge(cls[pc % 2].sem, 32 * (pc // 2 + 1))
                    for tile in range(4):
                        ri = pc * 4 + tile
                        v.wait_ge(rpm.sem, ri + 1)
                        src = RTILES[tile][:, ch * 512:(ch + 1) * 512]
                        v.tensor_tensor(rp1, src, cbuf[:, (pc % 2) * 1024:(pc % 2) * 1024 + 512],
                                        ALU.mult)
                        v.tensor_tensor(rp2, psD[:, 0:512],
                                        cbuf[:, (pc % 2) * 1024 + 512:(pc % 2 + 1) * 1024], ALU.mult)
                        v.drain()
                        rpe.inc(v.tensor_tensor(src, rp1, rp2, ALU.add))
                        v.drain()
                    for vt2 in range(2):
                        for k4 in range(4):
                            ti = pc * 8 + vt2 * 4 + k4
                            v.wait_ge(vtr.sem, ti + 1)
                            ktg = ch * 4 + k4
                            base = (vt2 * 64 + ktg) * 130
                            v.tensor_copy(vb[:, base:base + 64], psV[:, 0:64])
                            vbe.inc(v.tensor_copy(vb[:, base + 65:base + 129],
                                                  psV[:, 64:128]))

            @block.vector
            def _(v):
                v.wait_ge(cld.sem, 64)
                v.memset(bass.AP(vb.tensor, 64, [[2 * 64 * 130, 128], [130, 128], [1, 1]]), 1.0)
                mbe.inc(v.memset(bass.AP(vb.tensor, 129,
                                         [[2 * 64 * 130, 128], [130, 128], [1, 1]]), 1.0))

            @block.scalar
            def _(s):
                for ci in range(128):
                    h, ch = divmod(ci % 64, NCH)
                    isq = ci < 64
                    if ci == 0:
                        s.wait_ge(rpe.sem, 64)
                    if ci >= 2:
                        s.wait_ge(qnm.sem, ci - 1)
                    src = (qT if isq else kT)[h // 2]
                    base = 64 * (h % 2)
                    sqs.inc(s.activation(
                        sqb[(0 if base == 0 else 64):(64 if base == 0 else 128),
                            (ci % 2) * 512:(ci % 2 + 1) * 512],
                        src[base:base + 64, ch * 512:(ch + 1) * 512], AF.Square))

            @block.tensor
            def _(t):
                t.wait_ge(rpe.sem, 64)
                for ci in range(128):
                    h = (ci % 64) // NCH
                    base = 64 * (h % 2)
                    t.wait_ge(sqs.sem, ci + 1)
                    if ci >= 1:
                        t.wait_ge(qnr.sem, ci)
                    qnm.inc(t.matmul(psD[0:1, 0:512], ones_b[base:base + 64, 0:1],
                                     sqb[(0 if base == 0 else 64):(64 if base == 0 else 128),
                                         (ci % 2) * 512:(ci % 2 + 1) * 512],
                                     start=True, stop=True))

            @block.vector
            def _(v):
                for ci in range(128):
                    v.wait_ge(qnm.sem, ci + 1)
                    qnr.inc(v.tensor_reduce(mrows[0:1, ci:ci + 1], psD[0:1, 0:512],
                                            axis=mybir.AxisListType.X, op=ALU.max))
                v.drain()
                for h in range(4):
                    v.tensor_reduce(mrows[0:1, 128 + h:129 + h],
                                    mrows[0:1, 64 + 16 * h:64 + 16 * (h + 1)],
                                    axis=mybir.AxisListType.X, op=ALU.max)
                v.drain()
                last = None
                for h in range(4):
                    last = v.tensor_scalar(mrows[0:1, 16 * h:16 * (h + 1)],
                                           mrows[0:1, 16 * h:16 * (h + 1)],
                                           mrows[0:1, 128 + h:129 + h], None,
                                           ALU.mult)
                ngm.inc(last)

            @block.scalar
            def _(s):
                s.wait_ge(ngm.sem, 1)
                s.activation(mrows[0:1, 0:64], mrows[0:1, 0:64], AF.Ln)
                s.drain()
                nsc.inc(s.activation(mrows[0:1, 0:64], mrows[0:1, 0:64],
                                     AF.Exp, scale=0.5))

            @block.vector
            def _(v):
                v.wait_ge(nsc.sem, 1)
                ngm.inc(v.tensor_scalar(mrows[0:1, 0:64], mrows[0:1, 0:64],
                                        -1.0, None, ALU.mult))

            @block.tensor
            def _(t):
                t.wait_ge(ngm.sem, 2)
                mbm.inc(t.matmul(psD[:, 0:64], ones_f[0:1, :], mrows[0:1, 0:64],
                                 start=True, stop=True))

            @block.vector
            def _(v):
                v.wait_ge(mbm.sem, 1)
                mbe.inc(v.tensor_copy(mbias, psD[:, 0:64]))

            @block.tensor
            def _(t):
                t.wait_ge(mbe.sem, 2)
                t.wait_ge(vbe.sem, 128)
                kgi = 0
                cc = 0
                pend = [None]
                for h in range(4):
                    vt2 = h // 2
                    base = 64 * (h % 2)
                    for blk in range(4):
                        for cl in range(4):
                            ch = blk * 4 + cl
                            ng = 2 * (cl + 1)

                            def emit_out1(kg, kgl, ng=ng, cc2=cc, blk=blk,
                                          vt2=vt2, h=h):
                                if kg == 0 and cc2 >= 2:
                                    t.wait_ge(aoev.sem, cc2 - 1)
                                t.wait_ge(xp.sem, kgl + 1)
                                if kg >= ng - 2:
                                    t.wait_ge(mkv.sem, 4 * cc2 + 2 * (kg - (ng - 2) + 1))
                                st = (kg == 0)
                                sp = (kg == ng - 1)
                                for j in range(2):
                                    ktl = kg * 2 + j
                                    ktg = blk * 16 + ktl
                                    vbase = (vt2 * 64 + ktg) * 130 + (h % 2) * 65
                                    mm = t.matmul(
                                        psC[0:65, (cc2 % 2) * 512:(cc2 % 2) * 512 + 512],
                                        vb[:, vbase:vbase + 65],
                                        prb[:, (kgl % 2) * 1024 + j * 512:(kgl % 2) * 1024 + (j + 1) * 512],
                                        start=(st and j == 0), stop=(sp and j == 1))
                                omm.inc(mm)

                            for kg in range(ng):
                                pgrp = [psA, psB][kgi % 2]
                                if kgi >= 2:
                                    t.wait_ge(xp.sem, kgi - 1)
                                for j in range(2):
                                    ktl = kg * 2 + j
                                    ktg = blk * 16 + ktl
                                    mm = t.matmul(
                                        pgrp[:, j * 512:(j + 1) * 512],
                                        kT[vt2][base:base + 64, ktg * 128:(ktg + 1) * 128],
                                        qT[vt2][base:base + 64, ch * 512:(ch + 1) * 512],
                                        start=True, stop=True)
                                smm.inc(mm)
                                if kg == 0 and pend[0] is not None:
                                    t.wait_ge(rcp.sem, pend[0] + 1)
                                    rbm.inc(t.matmul(psD[0:64, 0:512],
                                                     ones_f[64:65, 0:64],
                                                     rr[64:65, 0:512],
                                                     start=True, stop=True))
                                    pend[0] = None
                                if kg >= 1:
                                    emit_out1(kg - 1, kgi - 1)
                                kgi += 1
                            emit_out1(ng - 1, kgi - 1)
                            pend[0] = cc
                            cc += 1
                t.wait_ge(rcp.sem, pend[0] + 1)
                rbm.inc(t.matmul(psD[0:64, 0:512], ones_f[64:65, 0:64],
                                 rr[64:65, 0:512], start=True, stop=True))

            @block.scalar
            def _(s):
                s.wait_ge(mbe.sem, 2)
                kgi = 0
                for h in range(4):
                    for blk in range(4):
                        for cl in range(4):
                            ci = h * 16 + blk * 4 + cl
                            ng = 2 * (cl + 1)
                            for kg in range(ng):
                                s.wait_ge(smm.sem, kgi + 1)
                                if kgi >= 2:
                                    s.wait_ge(omm.sem, kgi - 1)
                                pgrp = [psA, psB][kgi % 2]
                                xp.inc(s.activation(
                                    prb[:, (kgi % 2) * 1024:(kgi % 2 + 1) * 1024],
                                    pgrp[:, 0:1024], AF.Exp,
                                    bias=mbias[:, ci:ci + 1]))
                                kgi += 1

            @block.vector
            def _(v):
                kgi = 0
                cc = 0
                for h in range(4):
                    vt2 = h // 2
                    base = 64 * (h % 2)
                    for blk in range(4):
                        for cl in range(4):
                            ch = blk * 4 + cl
                            ng = 2 * (cl + 1)
                            for kg in range(ng):
                                if kg >= ng - 2:
                                    v.wait_ge(mld.sem, 16)
                                    v.wait_ge(xp.sem, kgi + 1)
                                    for j in range(2):
                                        ktl = kg * 2 + j
                                        dlt = ktl * 128 - cl * 512
                                        sl = 384 - dlt
                                        mkv.inc(v.tensor_tensor(
                                            prb[:, (kgi % 2) * 1024 + j * 512:(kgi % 2) * 1024 + (j + 1) * 512],
                                            prb[:, (kgi % 2) * 1024 + j * 512:(kgi % 2) * 1024 + (j + 1) * 512],
                                            mask_s[:, sl:sl + 512], ALU.mult))
                                kgi += 1
                            v.wait_ge(omm.sem, kgi)
                            rcp.inc(v.reciprocal(rr[64:65, 0:512],
                                                 psC[64:65, (cc % 2) * 512:(cc % 2) * 512 + 512]))
                            v.wait_ge(rbm.sem, cc + 1)
                            v.tensor_copy(rcb[0:64, :], psD[0:64, 0:512])
                            v.drain()
                            aoev.inc(v.tensor_tensor(
                                qT[vt2][base:base + 64, ch * 512:(ch + 1) * 512],
                                psC[0:64, (cc % 2) * 512:(cc % 2) * 512 + 512],
                                rcb[0:64, :], ALU.mult))
                            cc += 1

            @block.sync
            def _(sy):
                sy.wait_ge(aoev.sem, 64)
                if P.ao_ext is not None:
                    for vt2 in range(2):
                        stf.inc(sy.dma_start(out=P.ao_ext[vt2 * 128:(vt2 + 1) * 128, :],
                                             in_=qT[vt2]), 16)
                    sy.wait_ge(stf.sem, 32)
                else:
                    n = 0
                    for r in range(NC):
                        for vt2 in range(2):
                            stf.inc(sy.dma_start(
                                out=P.ao_in[256 * r + 128 * vt2:256 * r + 128 * (vt2 + 1), :],
                                in_=qT[vt2][:, r * TC:(r + 1) * TC]), 16)
                            n += 16
                    sy.wait_ge(stf.sem, n)

        _phase_close(nc, sems)


def build_attn1():
    nc = bass.Bass("TRN2", target_bir_lowering=False)
    P = NS()
    P.tag = ""
    hT = nc.dram_tensor("hT", [D, T], BF16, kind="ExternalInput").ap()
    P.y_src = lambda kt, ch: hT[kt * 128:(kt + 1) * 128, ch * 512:(ch + 1) * 512]
    P.wS = nc.dram_tensor("wS", [128, KT * 768], F32, kind="ExternalInput").ap()
    P.cosT_d = nc.dram_tensor("cosT", [128, T], BF16, kind="ExternalInput").ap()
    P.sinT_d = nc.dram_tensor("sinT", [128, T], BF16, kind="ExternalInput").ap()
    P.ptm_d = nc.dram_tensor("ptm", [128, 128], BF16, kind="ExternalInput").ap()
    P.mask01 = nc.dram_tensor("mask01", [128, 896], BF16, kind="ExternalInput").ap()
    P.cst = _decl_consts(nc)
    P.ao_ext = nc.dram_tensor("aoT", [256, T], BF16, kind="ExternalOutput").ap()
    P.ao_in = None
    emit_attn1(nc, P)
    return nc


def attn1_prep(inp):
    wqkv = _f32(inp["L1_attn_w_qkv"])
    wT = np.ascontiguousarray(wqkv.T)

    def tile_in_out(w):
        Fd = w.shape[1]
        return np.ascontiguousarray(
            w.reshape(KT, 128, Fd).transpose(1, 0, 2).reshape(128, KT * Fd))

    pre = {"wS": [None] * NC}
    for c in range(NC):
        cols = np.concatenate([wT[:, 256 * c:256 * (c + 1)],
                               wT[:, 2048 + 256 * c:2048 + 256 * (c + 1)],
                               wT[:, 4096 + 256 * c:4096 + 256 * (c + 1)]], axis=1)
        pre["wS"][c] = _f32(tile_in_out(cols))
    dd = np.arange(64)
    inv = 1.0 / (10000.0 ** (np.concatenate([dd[:32], dd[:32]]) * 2.0 / 64.0))
    tpos = np.arange(T)
    ang = tpos[None, :] * inv[:, None]
    cos2 = np.concatenate([np.cos(ang), np.cos(ang)], axis=0)
    sin_half = np.sin(ang)
    sin2 = np.concatenate([sin_half, sin_half], axis=0)
    pre["cosT"] = _bf(cos2)
    pre["sinT"] = _bf(sin2)
    P = np.zeros((128, 128), dtype=np.float32)
    for hh in range(2):
        b = hh * 64
        for i in range(32):
            P[b + i, b + 32 + i] = -1.0
            P[b + 32 + i, b + i] = 1.0
    pre["ptm"] = _bf(P.T)
    k = np.arange(128)[:, None]
    u = np.arange(896)[None, :]
    pre["mask01"] = _bf((k <= u - 384).astype(np.float32))
    return pre



def emit_post1(nc, P):
    yT_c, aT, woT, wn_c, wupS, wdnT = P.yT_c, P.aT, P.woT, P.wn_c, P.wupS, P.wdnT
    cst = P.cst
    xs = P.xs
    xT = P.xT if P.load_x else None
    xoT = P.xoT if P.store_x else None

    with ExitStack() as ex:
        sbt = lambda nm, sh, dt: ex.enter_context(nc.sbuf_tensor(nm + P.tag, sh, dt)).ap()
        pst = lambda nm, sh, dt: ex.enter_context(nc.psum_tensor(nm + P.tag, sh, dt)).ap()
        tb = sbt("tb", [128, KT * TC], BF16)
        ab = sbt("ab", [128, KT * TC], BF16)
        hb = sbt("hb", [128, KT * TC], BF16)
        wob = sbt("wob", [128, 4 * 512], BF16)
        wub = sbt("wub", [128, 4 * D], BF16)
        adb = sbt("adb", [128, 2 * 16 * 256], BF16)
        tmpbf = sbt("tmpbf", [128, 2 * 512], BF16)
        wnc = sbt("wnc", [128, KT], F32)
        rows = sbt("rows", [1, 2 * TC], F32)
        ones_b = sbt("ones_b", [128, 128], BF16)
        ones_f = sbt("ones_f", [128, 128], F32)
        psA = pst("psA", [128, 1024], F32)
        psB = pst("psB", [128, 1024], F32)
        psC = pst("psC", [128, 1024], F32)
        psD = pst("psD", [128, 1024], F32)
        PB8 = [lambda a=a, c=c: [psA, psB, psC, psD][a][:, c * 512:(c + 1) * 512]
               for a in range(4) for c in range(2)]

        sems = []
        with nc.Block() as block:
            mk = _mk_factory(nc, P.tag, sems)
            xld = mk("xld") if P.load_x else None
            yld = mk("yld")
            ald = mk("ald")
            cld = mk("cld")
            wol = [mk("wol%d" % i, dma=True) for i in range(4)]
            qmm = mk("qmm")
            pev = mk("pev")
            rsq = mk("rsq")
            rmm = mk("rmm")
            rrw = mk("rrw")
            rsc = mk("rsc")
            rbc = mk("rbc")
            rhv = mk("rhv")
            wul = [mk("wul%d" % i, dma=True) for i in range(2)]
            umm = mk("umm")
            sil = mk("sil")
            hev = mk("hev")
            adl = [mk("adl%d" % i, dma=True) for i in range(2)]
            dmm = mk("dmm")
            dev = mk("dev")
            stf = mk("stf")

            @block.sync
            def _(sy):
                if P.load_x:
                    for i in range(KT):
                        xld.inc(sy.dma_start(out=xs[:, i * TC:(i + 1) * TC],
                                             in_=xT[i * 128:(i + 1) * 128, :]), 16)
                for i in range(KT):
                    yld.inc(sy.dma_start(out=tb[:, i * TC:(i + 1) * TC],
                                         in_=yT_c[i * 128:(i + 1) * 128, :]), 16)
                cld.inc(sy.dma_start(out=ones_b, in_=cst.onesb), 16)
                cld.inc(sy.dma_start(out=ones_f, in_=cst.onesf), 16)
                cld.inc(sy.dma_start(out=wnc, in_=wn_c), 16)
                if getattr(P, "a_gate", None) is not None:
                    sy.wait_ge(P.a_gate[0], P.a_gate[1])
                for i in range(KT):
                    ald.inc(sy.dma_start(out=ab[:, i * TC:(i + 1) * TC],
                                         in_=aT[i * 128:(i + 1) * 128, :]), 16)

            @block.gpsimd
            def _(gp):
                for idx in range(64):
                    dg, kt = divmod(idx, 16)
                    if idx >= 4:
                        gp.wait_ge(qmm.sem, idx - 3)
                    wol[idx % 4].inc(gp.dma_start(
                        out=wob[:, (idx % 4) * 512:(idx % 4 + 1) * 512],
                        in_=bass.AP(woT.tensor, kt * 128 * D + dg * 512,
                                    [[D, 128], [1, 512]])), 16)

            @block.tensor
            def _(t):
                t.wait_ge(ald.sem, 16 * KT)
                for dg in range(4):
                    if dg >= 1:
                        t.wait_ge(pev.sem, 8 * dg)
                    for kt in range(16):
                        idx = dg * 16 + kt
                        t.wait_ge(wol[idx % 4].sem, 16 * (idx // 4 + 1))
                        mm = None
                        for dd in range(4):
                            for ch in range(2):
                                mm = t.matmul(
                                    PB8[dd * 2 + ch](),
                                    wob[:, (idx % 4) * 512 + dd * 128:(idx % 4) * 512 + (dd + 1) * 128],
                                    ab[:, kt * TC + ch * 512:kt * TC + (ch + 1) * 512],
                                    start=(kt == 0), stop=(kt == 15))
                        qmm.inc(mm)

            @block.vector
            def _(v):
                if P.load_x:
                    v.wait_ge(xld.sem, 16 * KT)
                v.wait_ge(yld.sem, 16 * KT)
                for dg in range(4):
                    v.wait_ge(qmm.sem, 16 * (dg + 1))
                    for dd in range(4):
                        for ch in range(2):
                            d = dg * 4 + dd
                            tt = tb[:, d * TC + ch * 512:d * TC + (ch + 1) * 512]
                            v.tensor_tensor(tt, PB8[dd * 2 + ch](), tt, ALU.add)
                            v.drain()
                            dst = xs[:, d * TC + ch * 512:d * TC + (ch + 1) * 512]
                            pev.inc(v.tensor_tensor(dst, dst, tt, ALU.add))

            @block.vector
            def _(v):
                v.wait_ge(pev.sem, 32)
                last = None
                for i in range(KT):
                    last = v.tensor_tensor(hb[:, i * TC:(i + 1) * TC],
                                           tb[:, i * TC:(i + 1) * TC],
                                           tb[:, i * TC:(i + 1) * TC], ALU.mult)
                rsq.inc(last)

            @block.tensor
            def _(t):
                t.wait_ge(cld.sem, 48)
                t.wait_ge(rsq.sem, 1)
                last = None
                for i in range(KT):
                    for ch in range(2):
                        last = t.matmul(psA[0:1, ch * 512:(ch + 1) * 512],
                                        ones_b[:, 0:1],
                                        hb[:, i * TC + ch * 512:i * TC + (ch + 1) * 512],
                                        start=(i == 0), stop=(i == KT - 1))
                rmm.inc(last)

            @block.vector
            def _(v):
                v.wait_ge(rmm.sem, 1)
                var = rows[0:1, 0:TC]
                v.tensor_scalar(var, psA[0:1, 0:TC], 1.0 / D, None, ALU.mult)
                v.drain()
                rrw.inc(v.tensor_scalar(var, var, float(LNEPS), None, ALU.add))

            @block.scalar
            def _(s):
                s.wait_ge(rrw.sem, 1)
                var = rows[0:1, 0:TC]
                lnv = rows[0:1, TC:2 * TC]
                s.activation(lnv, var, AF.Ln)
                s.drain()
                rsc.inc(s.activation(var, lnv, AF.Exp, scale=-0.5))

            @block.tensor
            def _(t):
                t.wait_ge(rsc.sem, 1)
                last = None
                for ch in range(2):
                    last = t.matmul(psB[:, ch * 512:(ch + 1) * 512], ones_f[0:1, :],
                                    rows[0:1, ch * 512:(ch + 1) * 512],
                                    start=True, stop=True)
                rbc.inc(last)

            @block.vector
            def _(v):
                v.wait_ge(rbc.sem, 1)
                last = None
                for i in range(KT):
                    last = v.scalar_tensor_tensor(ab[:, i * TC:(i + 1) * TC],
                                                  tb[:, i * TC:(i + 1) * TC],
                                                  wnc[:, i:i + 1], psB[:, 0:TC],
                                                  ALU.mult, ALU.mult)
                rhv.inc(last)

            @block.gpsimd
            def _(gp):
                for hi in range(2):
                    for oi in range(16):
                        gi = hi * 16 + oi
                        if gi >= 2:
                            gp.wait_ge(umm.sem, 2 * (gi - 1))
                        og = hi * 16 + oi
                        ov = 32 + hi * 16 + oi
                        wul[gi % 2].inc(gp.dma_start(
                            out=wub[:, (gi % 2) * 2 * D:(gi % 2) * 2 * D + D],
                            in_=bass.AP(wupS.tensor, og * 128 * D, [[D, 128], [1, D]])), 16)
                        wul[gi % 2].inc(gp.dma_start(
                            out=wub[:, (gi % 2) * 2 * D + D:(gi % 2) * 2 * D + 2 * D],
                            in_=bass.AP(wupS.tensor, ov * 128 * D, [[D, 128], [1, D]])), 16)
                    for ci in range(8):
                        gi2 = hi * 8 + ci
                        if gi2 >= 2:
                            gp.wait_ge(dmm.sem, 4 * (gi2 - 1))
                        adl[gi2 % 2].inc(gp.dma_start(
                            out=adb[:, (gi2 % 2) * 16 * 256:(gi2 % 2 + 1) * 16 * 256],
                            in_=bass.AP(wdnT.tensor, gi2 * 128 * 4096,
                                        [[4096, 128], [1, 4096]])), 16)

            @block.tensor
            def _(t):
                t.wait_ge(rhv.sem, 1)
                ucc = 0
                dcc = 0
                for hi in range(2):
                    if hi >= 1:
                        t.wait_ge(dev.sem, 32 * hi)
                    for oi in range(16):
                        gi = hi * 16 + oi
                        t.wait_ge(wul[gi % 2].sem, 32 * (gi // 2 + 1))
                        for ch in range(2):
                            slot = [psA, psB, psC][ucc % 3]
                            if ucc >= 3:
                                t.wait_ge(hev.sem, ucc - 2)
                            for kt in range(KT):
                                t.matmul(slot[:, 0:512],
                                         wub[:, (gi % 2) * 2 * D + kt * 128:(gi % 2) * 2 * D + (kt + 1) * 128],
                                         ab[:, kt * TC + ch * 512:kt * TC + (ch + 1) * 512],
                                         start=(kt == 0), stop=(kt == KT - 1))
                            last = None
                            for kt in range(KT):
                                last = t.matmul(
                                    slot[:, 512:1024],
                                    wub[:, (gi % 2) * 2 * D + D + kt * 128:(gi % 2) * 2 * D + D + (kt + 1) * 128],
                                    ab[:, kt * TC + ch * 512:kt * TC + (ch + 1) * 512],
                                    start=(kt == 0), stop=(kt == KT - 1))
                            umm.inc(last)
                            ucc += 1
                    t.wait_ge(hev.sem, 32 * (hi + 1))
                    for ci in range(8):
                        gi2 = hi * 8 + ci
                        t.wait_ge(adl[gi2 % 2].sem, 16 * (gi2 // 2 + 1))
                        for dd in range(2):
                            d = ci * 2 + dd
                            for ch in range(2):
                                pt = PB8[dcc % 8]()
                                if dcc >= 8:
                                    t.wait_ge(dev.sem, dcc - 7)
                                last = None
                                for kt in range(16):
                                    last = t.matmul(
                                        pt,
                                        adb[:, (gi2 % 2) * 16 * 256 + kt * 256 + dd * 128:(gi2 % 2) * 16 * 256 + kt * 256 + (dd + 1) * 128],
                                        hb[:, kt * TC + ch * 512:kt * TC + (ch + 1) * 512],
                                        start=(kt == 0), stop=(kt == 15))
                                dmm.inc(last)
                                dcc += 1

            @block.scalar
            def _(s):
                ucc = 0
                for hi in range(2):
                    for oi in range(16):
                        for ch in range(2):
                            s.wait_ge(umm.sem, ucc + 1)
                            if ucc >= 2:
                                s.wait_ge(hev.sem, ucc - 1)
                            slot = [psA, psB, psC][ucc % 3]
                            sil.inc(s.activation(
                                tmpbf[:, (ucc % 2) * 512:(ucc % 2 + 1) * 512],
                                slot[:, 0:512], AF.Silu))
                            ucc += 1

            @block.vector
            def _(v):
                ucc = 0
                dcc = 0
                for hi in range(2):
                    for oi in range(16):
                        for ch in range(2):
                            v.wait_ge(sil.sem, ucc + 1)
                            slot = [psA, psB, psC][ucc % 3]
                            hev.inc(v.tensor_tensor(
                                hb[:, oi * TC + ch * 512:oi * TC + (ch + 1) * 512],
                                tmpbf[:, (ucc % 2) * 512:(ucc % 2 + 1) * 512],
                                slot[:, 512:1024], ALU.mult))
                            ucc += 1
                    for ci in range(8):
                        for dd in range(2):
                            d = ci * 2 + dd
                            for ch in range(2):
                                v.wait_ge(dmm.sem, dcc + 1)
                                pt = PB8[dcc % 8]()
                                dst = xs[:, d * TC + ch * 512:d * TC + (ch + 1) * 512]
                                dev.inc(v.tensor_tensor(dst, dst, pt, ALU.add))
                                dcc += 1

            if P.store_x:
                @block.sync
                def _(sy):
                    sy.wait_ge(dev.sem, 64)
                    for d in range(KT):
                        stf.inc(sy.dma_start(out=xoT[d * 128:(d + 1) * 128, :],
                                             in_=xs[:, d * TC:(d + 1) * TC]), 16)
                    sy.wait_ge(stf.sem, 16 * KT)

        _phase_close(nc, sems)


def build_post1():
    nc = bass.Bass("TRN2", target_bir_lowering=False)
    P = NS()
    P.tag = ""
    P.load_x = True
    P.store_x = True
    P.xT = nc.dram_tensor("xT", [D, TC], F32, kind="ExternalInput").ap()
    P.yT_c = nc.dram_tensor("yT_c", [D, TC], BF16, kind="ExternalInput").ap()
    P.aT = nc.dram_tensor("aT", [D, TC], BF16, kind="ExternalInput").ap()
    P.woT = nc.dram_tensor("woT", [D, D], F32, kind="ExternalInput").ap()
    P.wn_c = nc.dram_tensor("wn_c", [128, KT], F32, kind="ExternalInput").ap()
    P.wupS = nc.dram_tensor("wupS", [64 * 128, D], F32, kind="ExternalInput").ap()
    P.wdnT = nc.dram_tensor("wdnT", [2048, 4096], F32, kind="ExternalInput").ap()
    P.cst = _decl_consts(nc)
    P.xoT = nc.dram_tensor("xoT", [D, TC], F32, kind="ExternalOutput").ap()
    with ExitStack() as gx:
        P.xs = gx.enter_context(nc.sbuf_tensor("xs_g", [128, KT * TC], F32)).ap()
        emit_post1(nc, P)
    return nc


_BUILD_CACHE = {}


def _get(name, fn):
    if name not in _BUILD_CACHE:
        _BUILD_CACHE[name] = fn()
    return _BUILD_CACHE[name]


def _run(name, fn, in_maps):
    nc = _get(name, fn)
    return run_bass_kernel_spmd(nc, in_maps, core_ids=list(range(NC)))


def _hadamard(n):
    H = np.array([[1.0]], dtype=np.float32)
    while H.shape[0] < n:
        H = np.block([[H, H], [H, -H]])
    return H.astype(np.float32)


def _cols(v):
    return _f32(np.asarray(v, dtype=np.float32).reshape(KT, 128).T)


def _slabify_up(A):
    out = np.empty((64, 128, D), dtype=A.dtype)
    a = A.reshape(KT, 128, 8192)
    for o in range(64):
        out[o] = a[:, :, o * 128:(o + 1) * 128].transpose(1, 0, 2).reshape(128, D)
    return np.ascontiguousarray(out.reshape(64 * 128, D))



def build_mega(upto=99):
    nc = bass.Bass("TRN2", target_bir_lowering=False)
    cst = _decl_consts(nc)
    xT0 = nc.dram_tensor("xT", [D, TC], F32, kind="ExternalInput").ap()
    xoT = nc.dram_tensor("xoT", [D, TC], F32, kind="ExternalOutput").ap()

    W = NS()
    W.tag = "_w"
    _moe_drams(nc, W)
    _fold_drams(nc, W)
    W.wqS = nc.dram_tensor("wqS", [128, KT * 256], F32, kind="ExternalInput").ap()
    W.wkvS = nc.dram_tensor("wkvS", [128, KT * 256], F32, kind="ExternalInput").ap()
    W.bq_c = nc.dram_tensor("bq_c", [128, 2], F32, kind="ExternalInput").ap()
    W.bkv_c = nc.dram_tensor("bkv_c", [128, 2], F32, kind="ExternalInput").ap()
    W.mask01 = nc.dram_tensor("mask01", [128, 896], BF16, kind="ExternalInput").ap()
    W.wf = nc.dram_tensor("wf", [D, D], F32, kind="ExternalInput").ap()
    W.bf_c = nc.dram_tensor("bf_c", [128, KT], F32, kind="ExternalInput").ap()
    W.wa_c = nc.dram_tensor("wa_c", [128, KT], F32, kind="ExternalInput").ap()
    W.wS = nc.dram_tensor("wS", [128, KT * 768], F32, kind="ExternalInput").ap()
    W.cosT_d = nc.dram_tensor("cosT", [128, T], BF16, kind="ExternalInput").ap()
    W.sinT_d = nc.dram_tensor("sinT", [128, T], BF16, kind="ExternalInput").ap()
    W.ptm_d = nc.dram_tensor("ptm", [128, 128], BF16, kind="ExternalInput").ap()
    W.woT = nc.dram_tensor("woT", [D, D], F32, kind="ExternalInput").ap()
    W.wn_c = nc.dram_tensor("wn_c", [128, KT], F32, kind="ExternalInput").ap()
    W.wupS = nc.dram_tensor("wupS", [64 * 128, D], F32, kind="ExternalInput").ap()
    W.wdnT = nc.dram_tensor("wdnT", [2048, 4096], F32, kind="ExternalInput").ap()

    a_up_sl = nc.dram_tensor("a_up_sl", [8 * 128, D], BF16).ap()
    a_gt_sl = nc.dram_tensor("a_gt_sl", [8 * 128, D], BF16).ap()
    a_dn_sl = nc.dram_tensor("a_dn_sl", [4 * 128, 4096], BF16).ap()
    a_up_f = nc.dram_tensor("a_up_f", [64 * 128, D], BF16, addr_space="Shared").ap()
    a_gt_f = nc.dram_tensor("a_gt_f", [64 * 128, D], BF16, addr_space="Shared").ap()
    a_dn_f = nc.dram_tensor("a_dn_f", [32 * 128, 4096], BF16, addr_space="Shared").ap()
    y1sl = nc.dram_tensor("y1sl", [D, TC], BF16).ap()
    y1f = nc.dram_tensor("y1f", [NC * D, TC], BF16, addr_space="Shared").ap()
    h4slA = nc.dram_tensor("h4slA", [D, 512], BF16).ap()
    h4slB = nc.dram_tensor("h4slB", [D, 512], BF16).ap()
    h4fA = nc.dram_tensor("h4fA", [NC * D, 512], BF16, addr_space="Shared").ap()
    h4fB = nc.dram_tensor("h4fB", [NC * D, 512], BF16, addr_space="Shared").ap()
    y4sl = nc.dram_tensor("y4sl", [D, TC], BF16).ap()
    ao0in = nc.dram_tensor("ao0in", [D, TC], BF16).ap()
    ao0out = nc.dram_tensor("ao0out", [D, TC], BF16).ap()
    ao1in = nc.dram_tensor("ao1in", [D, TC], BF16).ap()
    ao1out = nc.dram_tensor("ao1out", [D, TC], BF16).ap()

    with ExitStack() as gx:
        _xs = [None]
        cc = Sem(nc.alloc_semaphore(name="cc"))

        def coll(kind, in_ap, out_ap, a2a=False, wait=True):
            with nc.Block() as blk:
                @blk.gpsimd
                def _(gp):
                    ctx = allow_ici_all_to_all() if a2a else nullcontext()
                    with ctx:
                        op = gp.collective_compute(
                            kind, ALU.bypass,
                            replica_groups=[list(range(NC))],
                            ins=[in_ap], outs=[out_ap])
                    cc.inc(op, 1)
                    if wait:
                        gp.wait_ge(cc.sem, cc.n)

        def coll_wait():
            with nc.Block() as blk:
                @blk.gpsimd
                def _(gp):
                    gp.wait_ge(cc.sem, cc.n)

        def mkP(tag, **kw):
            P = NS()
            P.__dict__.update(W.__dict__)
            P.tag = tag
            P.cst = cst
            P.xs = _xs[0]
            P.xT = None
            P.xoT = None
            P.load_x = False
            P.store_x = False
            P.__dict__.update(kw)
            return P

        _xs[0] = gx.enter_context(nc.sbuf_tensor("xs_g", [128, KT * TC], F32)).ap()
        xldf = [None]
        if upto >= 1:
            with nc.named_scope("ph_fold"):
                Pf = mkP("_f", a_up=a_up_sl, a_gt=a_gt_sl, a_dn=a_dn_sl,
                         xT_pre=xT0)
                emit_fold(nc, Pf)
                xldf[0] = Pf.xld_out
        import os as _os
        if upto >= 2:
            _nag = int(_os.environ.get("MEGA_NAG", "3"))
            _defer = _os.environ.get("MEGA_DEFER_AG", "1") == "1"
            with nc.named_scope("ph_agA"):
                if _nag >= 1:
                    coll("AllGather", a_up_sl, a_up_f, wait=not _defer)
                if _nag >= 2:
                    coll("AllGather", a_gt_sl, a_gt_f, wait=not _defer)
                if _nag >= 3:
                    coll("AllGather", a_dn_sl, a_dn_f, wait=not _defer)

        if upto >= 3:
            with nc.named_scope("ph_moe1"):
                if xldf[0] is not None:
                    P1 = mkP("_m1", xld_ext=xldf[0], crow_d=W.crow_d)
                else:
                    P1 = mkP("_m1", load_x=True, xT=xT0, crow_d=W.crow_d)
                emit_moe(nc, P1)
        if upto >= 4:
            with nc.named_scope("ph_ln1"):
                Pl1 = mkP("_l1", yoT=y1sl)
                emit_lnx(nc, Pl1, rms=False)
        if upto >= 5:
            with nc.named_scope("ph_agy1"):
                coll("AllGather", y1sl, y1f, wait=False)
                gy1 = cc.n
        CH_ORD = list(range(0, 16, 2)) + list(range(1, 16, 2))
        Pa0 = mkP("_a0", ao_ext=None, ao_in=ao0in,
                  y_gate={0: (cc.sem, gy1)} if upto >= 5 else None,
                  y_src=lambda kt, ch: y1f[(ch // 2) * D + kt * 128:(ch // 2) * D + (kt + 1) * 128,
                                           (ch % 2) * 512:(ch % 2 + 1) * 512])
        if upto >= 6:
            with nc.named_scope("ph_attn0"):
                emit_attn0(nc, Pa0)
        if upto >= 7:
            with nc.named_scope("ph_a2a0"):
                coll("AllToAll", ao0in, ao0out, a2a=True, wait=False)
                ga2a0 = cc.n
        if upto >= 8:
            with nc.named_scope("ph_proj"):
                Ppj = mkP("_pj", aT=ao0out,
                          a_gate=(cc.sem, ga2a0) if upto >= 7 else None)
                emit_proj(nc, Ppj)
        if upto >= 9:
            with nc.named_scope("ph_moe2"):
                P2 = mkP("_m2", crow_d=W.crow_d)
                emit_moe(nc, P2)

        if upto >= 10:
            with nc.named_scope("ph_agAw"):
                coll_wait()
            with nc.named_scope("ph_swi1"):
                Ps1 = mkP("_s1", auS=a_up_f, agS=a_gt_f, adS=a_dn_f)
                emit_swiglu(nc, Ps1)
        if upto >= 11:
            with nc.named_scope("ph_ln2"):
                Pl2 = mkP("_l2", yoT=y4sl, hoT=None, hoT_halves=(h4slA, h4slB))
                emit_lnx(nc, Pl2, rms=True)
        if upto >= 12:
            with nc.named_scope("ph_agh4"):
                coll("AllGather", h4slA, h4fA, wait=False)
                gh4A = cc.n
                coll("AllGather", h4slB, h4fB, wait=False)
                gh4B = cc.n
        h4fh = [h4fA, h4fB]
        Pa1 = mkP("_a1", ao_ext=None, ao_in=ao1in, ch_order=CH_ORD,
                  y_gate={0: (cc.sem, gh4A), 8: (cc.sem, gh4B)} if upto >= 12 else None,
                  y_src=lambda kt, ch: h4fh[ch % 2][
                      (ch // 2) * D + kt * 128:(ch // 2) * D + (kt + 1) * 128, :])
        if upto >= 13:
            with nc.named_scope("ph_attn1"):
                emit_attn1(nc, Pa1)
        if upto >= 14:
            with nc.named_scope("ph_a2a1"):
                coll("AllToAll", ao1in, ao1out, a2a=True, wait=False)
                ga2a1 = cc.n
        if upto >= 15:
            with nc.named_scope("ph_post1"):
                Pp1 = mkP("_p1", yT_c=y4sl, aT=ao1out,
                          a_gate=(cc.sem, ga2a1) if upto >= 14 else None)
                emit_post1(nc, Pp1)
        if upto >= 16:
            with nc.named_scope("ph_swi2"):
                Ps2 = mkP("_s2", auS=a_up_f, agS=a_gt_f, adS=a_dn_f,
                          store_x=True, xoT=xoT)
                emit_swiglu(nc, Ps2)
        else:
            with ExitStack() as ex2:
                tz = ex2.enter_context(nc.sbuf_tensor("tz_fin", [128, TC], F32)).ap()
                szf = Sem(nc.alloc_semaphore(name="szf"))
                with nc.Block() as blk:
                    @blk.gpsimd
                    def _(gp):
                        gp.memset(tz, 0.0).then_inc(szf.sem, 1)
                    @blk.sync
                    def _(sy):
                        sy.wait_ge(szf.sem, 1)
                        n = 0
                        for dd2 in range(KT):
                            sy.dma_start(out=xoT[dd2 * 128:(dd2 + 1) * 128, :],
                                         in_=tz).then_inc(szf.sem, 16)
                            n += 16
                        sy.wait_ge(szf.sem, 1 + n)

    return nc


def mega_in_maps(inputs):
    pre_moe = moe_prep(inputs)
    pre_a0 = attn0_prep(inputs)
    pre_a1 = attn1_prep(inputs)
    x0 = _f32(np.asarray(inputs["L0_attn_x"], dtype=np.float32)[0].T)

    W_up = _f32(inputs["L1_ffn_W_up"])
    W_gt = _f32(inputs["L1_ffn_W_gate"])
    W_dn = _f32(inputs["L1_ffn_W_down"])
    SU_ug = _f32(inputs["L1_ffn_SU_ug"])
    SU_dp = _f32(inputs["L1_ffn_SU_dp"])
    Wsc_ug = _f32(inputs["L1_ffn_Wscale_ug"])
    Wsc_dp = _f32(inputs["L1_ffn_Wscale_dp"])
    h128 = _bf(_hadamard(128))

    wfc = _f32(inputs["L0_attn_w_fc"])
    bfc = _cols(inputs["L0_attn_b_fc"])
    woT = _f32(np.asarray(inputs["L1_attn_w_o"], dtype=np.float32).T)
    wupS = _slabify_up(_f32(np.asarray(inputs["L1_attn_w_up"], dtype=np.float32).T))
    _wdn_raw = _f32(np.asarray(inputs["L1_attn_w_down"], dtype=np.float32).T)
    _wdnS = np.empty((16, 128, 16, 256), dtype=np.float32)
    for hi in range(2):
        for ci in range(8):
            blk = _wdn_raw[hi * 2048:(hi + 1) * 2048, ci * 256:(ci + 1) * 256]
            _wdnS[hi * 8 + ci] = blk.reshape(16, 128, 256).transpose(1, 0, 2)
    wdnT = _f32(_wdnS.reshape(2048, 4096))
    wn_c = _cols(inputs["L1_attn_ffn_norm_w"])
    wa_c = _cols(inputs["L1_attn_attn_norm_w"])

    maps = []
    for c in range(NC):
        m = dict(pre_moe)
        m.update(CONSTS)
        m["xT"] = _f32(x0[:, c * TC:(c + 1) * TC])
        m["wup"] = _f32(W_up[1024 * c:1024 * (c + 1), :].T)
        m["wgt"] = _f32(W_gt[1024 * c:1024 * (c + 1), :].T)
        m["wdn"] = _f32(W_dn[256 * c:256 * (c + 1), :].T)
        m["su_ug"] = _f32(SU_ug.reshape(16, 128).T)
        m["su_dp"] = _f32(SU_dp.reshape(64, 128).T)
        m["wsc_up"] = _f32(Wsc_ug[None, 1024 * c:1024 * (c + 1)])
        m["wsc_gt"] = _f32(Wsc_ug[None, 8192 + 1024 * c:8192 + 1024 * (c + 1)])
        m["wsc_dn"] = _f32(Wsc_dp[None, 256 * c:256 * (c + 1)])
        m["h128"] = h128
        m["wqS"] = pre_a0["wqS"][c]
        m["wkvS"] = pre_a0["wkvS"]
        m["bq_c"] = pre_a0["bq_c"][c]
        m["bkv_c"] = pre_a0["bkv_c"]
        m["mask01"] = pre_a0["mask01"]
        m["wf"] = wfc
        m["bf_c"] = bfc
        m["wa_c"] = wa_c
        m["wS"] = pre_a1["wS"][c]
        m["cosT"] = pre_a1["cosT"]
        m["sinT"] = pre_a1["sinT"]
        m["ptm"] = pre_a1["ptm"]
        m["woT"] = woT
        m["wn_c"] = wn_c
        m["wupS"] = wupS
        m["wdnT"] = wdnT
        maps.append(m)
    return maps


def kernel(**inputs):
    maps = mega_in_maps(inputs)
    nc = _get("mega", build_mega)
    res = run_bass_kernel_spmd(nc, maps, core_ids=list(range(NC)))
    x6 = np.concatenate([np.asarray(res.results[c]["xoT"]) for c in range(NC)], axis=1)
    return np.ascontiguousarray(x6.T, dtype=np.float32).reshape(1, T, D)


def kernel_unfused(**inputs):
    x0 = _f32(np.asarray(inputs["L0_attn_x"], dtype=np.float32)[0].T)

    pre_moe = moe_prep(inputs)
    pre_a0 = attn0_prep(inputs)
    pre_a1 = attn1_prep(inputs)

    W_up = _f32(inputs["L1_ffn_W_up"])
    W_gt = _f32(inputs["L1_ffn_W_gate"])
    W_dn = _f32(inputs["L1_ffn_W_down"])
    SU_ug = _f32(inputs["L1_ffn_SU_ug"])
    SU_dp = _f32(inputs["L1_ffn_SU_dp"])
    Wsc_ug = _f32(inputs["L1_ffn_Wscale_ug"])
    Wsc_dp = _f32(inputs["L1_ffn_Wscale_dp"])
    h128 = _bf(_hadamard(128))
    fold_maps = []
    for c in range(NC):
        m = {
            "wup": _f32(W_up[1024 * c:1024 * (c + 1), :].T),
            "wgt": _f32(W_gt[1024 * c:1024 * (c + 1), :].T),
            "wdn": _f32(W_dn[256 * c:256 * (c + 1), :].T),
            "su_ug": _f32(SU_ug.reshape(16, 128).T),
            "su_dp": _f32(SU_dp.reshape(64, 128).T),
            "wsc_up": _f32(Wsc_ug[None, 1024 * c:1024 * (c + 1)]),
            "wsc_gt": _f32(Wsc_ug[None, 8192 + 1024 * c:8192 + 1024 * (c + 1)]),
            "wsc_dn": _f32(Wsc_dp[None, 256 * c:256 * (c + 1)]),
            "h128": h128,
        }
        m.update(CONSTS)
        fold_maps.append(m)
    res = _run("fold", build_fold, fold_maps)
    pre_sw = {
        "auS": np.concatenate([np.asarray(res.results[c]["a_up"]) for c in range(NC)], axis=0),
        "agS": np.concatenate([np.asarray(res.results[c]["a_gt"]) for c in range(NC)], axis=0),
        "adS": np.concatenate([np.asarray(res.results[c]["a_dn"]) for c in range(NC)], axis=0),
    }

    def run_moe_l(xT):
        maps = []
        for c in range(NC):
            m = dict(pre_moe)
            m["xT"] = _f32(xT[:, c * TC:(c + 1) * TC])
            m.update(CONSTS)
            maps.append(m)
        r = _run("moe", build_moe, maps)
        return np.concatenate([r.results[c]["xoT"] for c in range(NC)], axis=1)

    def run_lnx_l(xT, rms=False, wa=None):
        maps = []
        for c in range(NC):
            m = {"xT": _f32(xT[:, c * TC:(c + 1) * TC])}
            if rms:
                m["wa_c"] = _cols(wa)
            m.update(CONSTS)
            maps.append(m)
        r = _run("lnx_rms" if rms else "lnx",
                 (lambda: build_lnx(True)) if rms else (lambda: build_lnx(False)),
                 maps)
        y = np.concatenate([r.results[c]["yoT"] for c in range(NC)], axis=1)
        if rms:
            h = np.concatenate([r.results[c]["hoT"] for c in range(NC)], axis=1)
            return y, h
        return y

    def run_swiglu_l(xT):
        maps = []
        for c in range(NC):
            m = {"xT": _f32(xT[:, c * TC:(c + 1) * TC])}
            m.update(pre_sw)
            m.update(CONSTS)
            maps.append(m)
        r = _run("swiglu", build_swiglu, maps)
        return np.concatenate([r.results[c]["xoT"] for c in range(NC)], axis=1)

    x1 = run_moe_l(x0)
    y1 = run_lnx_l(x1)
    r = _run("attn0", build_attn0,
             [attn0_in_map(c, y1, pre_a0) for c in range(NC)])
    ao0 = np.concatenate([r.results[c]["aoT"] for c in range(NC)], axis=0)

    wfc = _f32(inputs["L0_attn_w_fc"])
    bfc = _cols(inputs["L0_attn_b_fc"])
    maps = []
    for c in range(NC):
        maps.append({"xT": _f32(x1[:, c * TC:(c + 1) * TC]),
                     "aT": np.ascontiguousarray(ao0[:, c * TC:(c + 1) * TC]),
                     "wf": wfc, "bf_c": bfc})
    r = _run("proj", build_proj, maps)
    x2 = np.concatenate([r.results[c]["xoT"] for c in range(NC)], axis=1)

    x3 = run_moe_l(x2)

    x4 = run_swiglu_l(x3)
    y4, h4 = run_lnx_l(x4, rms=True, wa=inputs["L1_attn_attn_norm_w"])
    maps = []
    for c in range(NC):
        m = dict(wS=pre_a1["wS"][c], cosT=pre_a1["cosT"], sinT=pre_a1["sinT"],
                 ptm=pre_a1["ptm"], mask01=pre_a1["mask01"], hT=h4)
        m.update(CONSTS)
        maps.append(m)
    r = _run("attn1", build_attn1, maps)
    ao1 = np.concatenate([r.results[c]["aoT"] for c in range(NC)], axis=0)

    woT = _f32(np.asarray(inputs["L1_attn_w_o"], dtype=np.float32).T)
    wupS = _slabify_up(_f32(np.asarray(inputs["L1_attn_w_up"], dtype=np.float32).T))
    _wdn_raw = _f32(np.asarray(inputs["L1_attn_w_down"], dtype=np.float32).T)
    _wdnS = np.empty((16, 128, 16, 256), dtype=np.float32)
    for hi in range(2):
        for ci in range(8):
            blk = _wdn_raw[hi * 2048:(hi + 1) * 2048, ci * 256:(ci + 1) * 256]
            _wdnS[hi * 8 + ci] = blk.reshape(16, 128, 256).transpose(1, 0, 2)
    wdnT = _f32(_wdnS.reshape(2048, 4096))
    wn_c = _cols(inputs["L1_attn_ffn_norm_w"])
    maps = []
    for c in range(NC):
        m = {"xT": _f32(x4[:, c * TC:(c + 1) * TC]),
             "yT_c": np.ascontiguousarray(y4[:, c * TC:(c + 1) * TC]),
             "aT": np.ascontiguousarray(ao1[:, c * TC:(c + 1) * TC]),
             "woT": woT, "wn_c": wn_c, "wupS": wupS, "wdnT": wdnT}
        m.update(CONSTS)
        maps.append(m)
    r = _run("post1", build_post1, maps)
    x5 = np.concatenate([r.results[c]["xoT"] for c in range(NC)], axis=1)

    x6 = run_swiglu_l(x5)
    return np.ascontiguousarray(x6.T, dtype=np.float32).reshape(1, T, D)
